# version 04
# baseline (speedup 1.0000x reference)
import math
import numpy as np
import ml_dtypes

import concourse.bass as bass
import concourse.mybir as mybir
from concourse.bass_utils import run_bass_kernel_spmd
from concourse.dve_ops import DveOp, OPS
from concourse import dve_spec as dvs
from concourse.dve_spec import Spec, Src0, Src1, C0, C1, C2, relu, lower, One, Zero, Latch, Bin, AluOp

F32 = mybir.dt.float32
BF16 = mybir.dt.bfloat16
ALU = mybir.AluOpType
ACT = mybir.ActivationFunctionType
AX = mybir.AxisListType

BF = ml_dtypes.bfloat16

N_CORES = 8
T_FULL = 16384
D = 1024
DCH = D // 128
MAGIC = 12582912.0

SLICES = [2944, 2560, 2304, 2048, 1920, 1792, 1536, 1280]
assert sum(SLICES) == T_FULL and all(s % 128 == 0 for s in SLICES)
STARTS = [sum(SLICES[:c]) for c in range(N_CORES)]
OV = 256

H = 16
HS = 64
CHUNK = 64
SUP = 512
SUPD = 256

EPS_LN = 1e-5
EPS_GN = 0.00064
L0FF_SCALE = 1.0 / 127.0

_PWL_X64 = np.linspace(-4.0, 4.0, 33)
_PWL_Y64 = 0.5 * _PWL_X64 * (1.0 + np.array([math.erf(v / math.sqrt(2.0)) for v in _PWL_X64]))


def _pwl_knots():
    xs = _PWL_X64[:17]
    ys = _PWL_Y64[:17]
    slopes = (_PWL_Y64[1:17] - _PWL_Y64[0:16]) / 0.25
    knots = []
    s_jump = ys[0] * 256.0
    knots.append((-16.0 - 1.0 / 16.0, 64.0 * s_jump))
    prev = s_jump
    for k in range(16):
        pos = -16.0 + k
        knots.append((pos, 64.0 * (slopes[k] - prev)))
        prev = slopes[k]
    return knots


_KNOTS = _pwl_knots()


def _const_node(v):
    raise RuntimeError("unused")


def _register_pwl_ops():
    ops = []
    toff2 = Latch(Bin(AluOp.ADD, C0, One))
    body_seed = relu(Src0 + C0) * C1 + relu(Src0 + toff2) * C2
    body_chain = (Src1 + relu(Src0 + C0) * C2) + relu(Src0 + toff2) * C1
    body_one = Src1 + relu(Src0 + C0) * C1
    specs = [("ATHPWL_SEED", Spec(body=body_seed),
              lambda in0, in1, s0, s1, imm2: np.maximum(in0 + s0, 0) * s1 + np.maximum(in0 + s0 + 1.0, 0) * imm2),
             ("ATHPWL_CHAIN", Spec(body=body_chain),
              lambda in0, in1, s0, s1, imm2: in1 + np.maximum(in0 + s0, 0) * imm2 + np.maximum(in0 + s0 + 1.0, 0) * s1)]
    from concourse.dve_ops import (_SUB_OPCODE_FOR_NAME, _CUSTOM_DVE_ROW_BASE,
                                   CUSTOM_DVE_REFS, CUSTOM_DVE_SPECS)
    have = {op.name for op in OPS}
    out = {}
    repl = len(OPS)
    for _ in specs:
        OPS.append(None)
    for i, (name, spec, ref) in enumerate(specs):
        if name in have:
            out[name] = next(op for op in OPS if op.name == name)
            continue
        shas = {}
        for ver in ("v3", "v4"):
            try:
                shas[ver] = lower(spec, ver=ver).sha(ver)
            except Exception:
                pass
        op = DveOp(name, spec, subdim=False, uops_sha=shas, reference=ref)
        idx = repl + i
        OPS[idx] = op
        _SUB_OPCODE_FOR_NAME[name] = _CUSTOM_DVE_ROW_BASE + idx
        CUSTOM_DVE_SPECS[name] = spec
        CUSTOM_DVE_REFS[name] = ref
        out[name] = op
    return out


_PWL_OPS = _register_pwl_ops()


def _bf(x):
    return np.ascontiguousarray(np.asarray(x)).astype(BF)


def _f32(x):
    return np.ascontiguousarray(np.asarray(x), dtype=np.float32)


def _perpart(vec):
    v = _f32(vec).reshape(DCH, 128)
    return np.ascontiguousarray(v.T)


class EngCtr:
    def __init__(self, sem):
        self.sem = sem
        self.n = 0

    def inc(self, instr, k=1):
        instr.then_inc(self.sem, k)
        self.n += k
        return self.n

    def sw(self, eng):
        eng.wait_ge(self.sem, self.n)


def ceil_div(a, b):
    return (a + b - 1) // b


import contextlib
import types


class Prog:

    def __init__(self, tap=None):
        self.tap = tap
        self.nc = bass.Bass("TRN2", target_bir_lowering=False, debug=False)
        self.stack = contextlib.ExitStack()
        self.b = {}
        self.MAXSLICE = max(SLICES)
        self.MAXTOT = self.MAXSLICE + OV

    def ein(self, name, shape, dt):
        self.b[name] = self.nc.dram_tensor(name, shape, dt, kind="ExternalInput").ap()

    def ph_open(self, tag):
        self._ph = contextlib.ExitStack()
        self._tag = tag

    def ph_sbuf(self, name, shape, dt):
        self.b[name] = self._ph.enter_context(
            self.nc.sbuf_tensor(f"{name}_{self._tag}", shape, dt))[:]
        return self.b[name]

    def ph_psum(self, name, shape, dt):
        self.b[name] = self._ph.enter_context(
            self.nc.psum_tensor(f"{name}_{self._tag}", shape, dt))[:]
        return self.b[name]

    def ph_close(self):
        self._ph.close()

    def sbuf(self, name, shape, dt):
        self.b[name] = self.stack.enter_context(self.nc.sbuf_tensor(name, shape, dt))[:]

    def psum(self, name, shape, dt):
        self.b[name] = self.stack.enter_context(self.nc.psum_tensor(name, shape, dt))[:]

    def sem(self, name):
        self.b[name] = EngCtr(self.stack.enter_context(self.nc.semaphore(name)))

    def dram(self, name, shape, dt, kind="Internal"):
        self.b[name] = self.nc.dram_tensor(name, shape, dt, kind=kind).ap()


def build_program(tap=None):
    p = Prog(tap)
    nc = p.nc
    b = p.b
    MAXTOT = p.MAXTOT

    p.ein("x", [T_FULL, D], F32)
    p.ein("invd_in", [128, MAXTOT], F32)
    p.ein("ident", [128, 128], BF16)
    p.ein("identf", [128, 128], F32)
    p.ein("trimask", [128, 128], BF16)
    p.ein("ln0g_in", [128, DCH], F32)
    p.ein("ln0b_in", [128, DCH], F32)
    p.ein("Whid", [D, 4096], BF16)
    p.ein("bhid_in", [1, 4096], BF16)
    p.ein("Wqk", [D, 128], BF16)
    p.ein("bqk_in", [128, 1], F32)
    p.ein("os_in", [128, 4], F32)
    p.ein("Wout", [2048, D], BF16)
    p.ein("bout_in", [1, D], BF16)
    p.ein("ones_in", [1, 128], BF16)
    p.ein("wgT", [D, 3072], BF16)
    p.ein("wuT", [D, 3072], BF16)
    p.ein("wdT", [3072, D], BF16)
    p.ein("maa_w1T", [D, 128], BF16)
    p.ein("gate_w1T", [D, 128], BF16)
    p.ein("decay_w1T", [D, 64], BF16)
    p.ein("WrT", [D, D], BF16)
    p.ein("WkT", [D, D], BF16)
    p.ein("WvT", [D, D], BF16)
    p.ein("lw1T_in", [4 * D, 16], BF16)
    p.ein("maa_w2_in", [32, 4 * D], BF16)
    p.ein("gate_w2_in", [128, D], BF16)
    p.ein("decay_w2_in", [64, D], BF16)
    p.ein("lw2_in", [16, 4 * D], BF16)
    p.ein("pvecs_in", [128, 40], F32)
    p.ein("npvecs_in", [128, 40], F32)
    p.ein("tmaa_in", [128, 4 * DCH], F32)
    p.ein("bo16_in", [128, 8 * 16], BF16)
    p.ein("sel16_in", [16, 8 * 128], BF16)
    p.ein("masks_in", [128, 5 * SUPD], F32)
    p.ein("eye64_in", [128, 64], F32)
    p.ein("WoT", [D, D], BF16)
    p.ein("w1_in", [D, 4096], BF16)
    p.ein("w2_in", [4096, D], BF16)
    p.ein("b2_in", [1, D], BF16)
    p.ein("lnwb_in", [128, 3 * DCH], F32)
    p.ein("b1_in", [128, 32], F32)

    y_out = nc.dram_tensor("y", [p.MAXSLICE, D], F32, kind="ExternalOutput").ap()
    b["y_out_ap"] = y_out
    dbg = None
    if tap is not None:
        dbg = nc.dram_tensor("dbg", [MAXTOT, 4096], F32, kind="ExternalOutput").ap()
        b["dbg2"] = nc.dram_tensor("dbg2", [128, 2 * MAXTOT], F32, kind="ExternalOutput").ap()

    p.dram("scr_vt", [T_FULL, 2048], BF16)
    p.dram("scr_gt", [MAXTOT, 2048], BF16)
    p.dram("scr_xh", [MAXTOT, D], F32)
    p.dram("scr_x1", [MAXTOT, D], F32)
    p.dram("scr_x2", [MAXTOT, D], F32)
    MS = p.MAXSLICE
    p.dram("scr_r", [128, DCH * MS], BF16)
    p.dram("scr_k", [128, DCH * MS], BF16)
    p.dram("scr_v1", [128, DCH * MS], BF16)
    p.dram("scr_g", [128, DCH * MS], BF16)
    p.dram("scr_zh", [128, DCH * MS], BF16)
    p.dram("scr_y", [128, DCH * MS], F32)
    b["cc_in"] = nc.dram_tensor("cc_in", [128, 1024], F32).ap()
    b["cc_out"] = nc.dram_tensor("cc_out", [8 * 128, 1024], F32).ap()

    for s in ("s_ld", "s_gp", "s_pe", "s_dve", "s_sce", "s_x0", "s_x1",
              "s_vt", "s_gt", "s_xh", "s_r0", "s_r1", "s_w0", "s_w1",
              "s_w2a", "s_w2b", "s_cc"):
        p.sem(s)

    with p.stack:
        engs = [nc.tensor, nc.vector, nc.scalar, nc.sync, nc.gpsimd]
        pids = [e.partition_id() for e in engs]
        for core in nc.Switch(engines=engs, index=pids, n=N_CORES):
            _emit_core(p, core, y_out, dbg)
    return nc


def _emit_core(p, c, y_out, dbg):
    import os as _os
    nc = p.nc
    b = p.b
    p.stop = int(_os.environ.get("A_STOP", "99"))
    for s in ("s_ld", "s_gp", "s_pe", "s_dve", "s_sce", "s_x0", "s_x1",
              "s_vt", "s_gt", "s_xh", "s_r0", "s_r1", "s_w0", "s_w1",
              "s_w2a", "s_w2b", "s_cc"):
        b[s].n = 0
    start = STARTS[c]
    sl = SLICES[c]
    pend = start + sl
    astart = max(0, start - OV)
    tot = pend - astart
    n_pref = pend // 128
    own0 = astart // 128

    p.ph_open(f"bd{c}")
    body_stack = p._ph
    p.ph_sbuf("onesr", [1, 128], BF16)
    p.ph_sbuf("idt", [128, 128], BF16)
    p.ph_sbuf("trim", [128, 128], BF16)
    p.ph_sbuf("idtf", [128, 128], F32)
    p.ph_sbuf("stats", [128, 24], F32)
    p.ph_sbuf("bo16", [128, 8 * 16], BF16)
    p.ph_sbuf("sel16", [16, 8 * 128], BF16)

    p.ph_open(f"ab{c}")
    ab_stack = p._ph
    p.ph_sbuf("kT", [128, T_FULL], BF16)
    p.ph_sbuf("qf", [128, p.MAXTOT], BF16)
    p.ph_sbuf("invd", [128, p.MAXTOT], F32)

    p.ph_open(f"A{c}")
    _phase_setup(p, c)
    _phase_A(p, c, n_pref, own0, astart)
    if p.tap == "A":
        _tap_A(p, c, dbg, astart, tot)
        p.ph_close()
        ab_stack.close()
        body_stack.close()
        return
    p.ph_close()
    p.ph_open(f"B{c}")
    _phase_B(p, c, astart, pend, dbg)
    p.ph_close()
    ab_stack.close()
    if p.tap == "B":
        body_stack.close()
        return
    p.ph_open(f"C{c}")
    _phase_C(p, c, astart, pend, dbg)
    p.ph_close()
    if p.tap == "C":
        body_stack.close()
        return
    p.ph_open(f"D{c}")
    _phase_D(p, c, astart, pend, dbg)
    if p.tap == "D":
        p.ph_close()
        body_stack.close()
        return
    p.ph_close()
    p.ph_open(f"E{c}")
    _phase_EF(p, c, astart, pend, dbg, y_out)
    nc = p.nc
    for en in ("sync", "gpsimd", "vector", "scalar", "tensor"):
        pass
    _barrier(p)
    p.ph_close()
    body_stack.close()


def _phase_setup(p, c):
    nc, b = p.nc, p.b
    p.ph_sbuf("xt", [128, 2 * D], F32)
    p.ph_sbuf("tbf", [128, D], BF16)
    p.ph_sbuf("nT", [128, DCH * 128], BF16)
    p.ph_sbuf("qk_s", [128, 128], F32)
    p.ph_sbuf("w_hid", [128, DCH * 4096], BF16)
    p.ph_sbuf("w_qk", [128, DCH * 128], BF16)
    p.ph_sbuf("b_hid", [1, 4096], BF16)
    p.ph_sbuf("ln0g", [128, DCH], F32)
    p.ph_sbuf("ln0b", [128, DCH], F32)
    p.ph_sbuf("bqk", [128, 1], F32)
    p.ph_sbuf("osv", [128, 4], F32)
    p.ph_sbuf("qtmp", [128, 128], F32)
    p.ph_sbuf("vt_sb", [128, 2048], BF16)
    p.ph_sbuf("gt_sb", [128, 2048], BF16)
    p.ph_sbuf("xh_sb", [128, D], F32)
    p.ph_psum("ps_big", [128, 2048], F32)
    p.ph_psum("ps_tp", [128, 2048], BF16)
    p.ph_psum("ps_s", [128, 512], F32)
    ld = b["s_ld"]
    sy = nc.sync
    ld.inc(sy.dma_start(out=b["invd"], in_=b["invd_in"]), 16)
    ld.inc(sy.dma_start(out=b["idt"], in_=b["ident"]), 16)
    ld.inc(sy.dma_start(out=b["idtf"], in_=b["identf"]), 16)
    ld.inc(sy.dma_start(out=b["trim"], in_=b["trimask"]), 16)
    ld.inc(sy.dma_start(out=b["ln0g"], in_=b["ln0g_in"]), 16)
    ld.inc(sy.dma_start(out=b["ln0b"], in_=b["ln0b_in"]), 16)
    ld.inc(sy.dma_start(out=b["bqk"], in_=b["bqk_in"]), 16)
    ld.inc(sy.dma_start(out=b["osv"], in_=b["os_in"]), 16)
    ld.inc(sy.dma_start(out=b["b_hid"], in_=b["bhid_in"]), 16)
    ld.inc(sy.dma_start(out=b["onesr"], in_=b["ones_in"]), 16)
    for dch in range(DCH):
        ld.inc(sy.dma_start(out=b["w_hid"][:, dch * 4096:(dch + 1) * 4096],
                            in_=b["Whid"][dch * 128:(dch + 1) * 128, :]), 16)
        ld.inc(sy.dma_start(out=b["w_qk"][:, dch * 128:(dch + 1) * 128],
                            in_=b["Wqk"][dch * 128:(dch + 1) * 128, :]), 16)


def _phase_A(p, c, n_pref, own0, astart):
    nc, b = p.nc, p.b
    ld, gp, pe, dv, sc = (b[s] for s in ("s_ld", "s_gp", "s_pe", "s_dve", "s_sce"))
    sx = [b["s_x0"], b["s_x1"]]
    svt, sgt, sxh = b["s_vt"], b["s_gt"], b["s_xh"]
    st = b["stats"]
    xt, tbf, nT = b["xt"], b["tbf"], b["nT"]
    ps_big, ps_tp, ps_s = b["ps_big"], b["ps_tp"], b["ps_s"]
    setup_ld = ld.n

    sx[0].inc(nc.sync.dma_start(out=xt[:, 0:D], in_=b["x"][0:128, :]), 16)

    prev = dict(nT_ev_dv=0, nT_ev_sc=0, silu_sc=0, vt_gp=0, gt_gp=0, xh_gp=0,
                qk_sc=0, xh_dv=0, tp_pe=0, apply_sc=0)

    import os as _os
    _lim = int(_os.environ.get("A_TILES", "0"))
    if _lim:
        n_pref = min(n_pref, _lim)
    for g in range(n_pref):
        slot = (g % 2) * D
        own = g >= own0
        loc = g * 128 - astart
        if g + 1 < n_pref:
            nslot = ((g + 1) % 2) * D
            if g >= 1:
                nc.sync.wait_ge(sc.sem, prev["apply_sc"])
                nc.sync.wait_ge(dv.sem, prev["xh_dv"])
            sx[(g + 1) % 2].inc(nc.sync.dma_start(out=xt[:, nslot:nslot + D],
                                     in_=b["x"][(g + 1) * 128:(g + 2) * 128, :]), 16)
        xsem = sx[g % 2]
        x_ready = 16 * (g // 2 + 1)

        if p.stop < 1:
            continue
        nc.vector.wait_ge(xsem.sem, x_ready)
        dv.inc(nc.vector.reduce_sum(st[:, 0:1], xt[:, slot:slot + D], AX.X))
        nc.scalar.wait_ge(xsem.sem, x_ready)
        if g > 0:
            nc.scalar.wait_ge(dv.sem, prev["xh_dv"])
        sc.inc(nc.scalar.activation(tbf[:], xt[:, slot:slot + D], ACT.Square,
                                    accum_out=st[:, 1:2]))
        sq_done = sc.n
        if p.stop < 2:
            continue
        nc.vector.wait_ge(sc.sem, sq_done)
        dv.sw(nc.vector)
        dv.inc(nc.vector.tensor_scalar_mul(st[:, 2:3], st[:, 0:1], 1.0 / D))
        dv.inc(nc.vector.tensor_scalar_mul(st[:, 3:4], st[:, 1:2], 1.0 / D))
        dv.sw(nc.vector)
        dv.inc(nc.vector.tensor_mul(st[:, 4:5], st[:, 2:3], st[:, 2:3]))
        dv.sw(nc.vector)
        dv.inc(nc.vector.scalar_tensor_tensor(st[:, 5:6], st[:, 3:4], EPS_LN,
                                              st[:, 4:5], ALU.add, ALU.subtract))
        dv.sw(nc.vector)
        dv.inc(nc.vector.tensor_scalar_add(st[:, 15:16], st[:, 5:6], -EPS_LN))
        v1_done = dv.n
        nc.scalar.wait_ge(dv.sem, v1_done)
        sc.inc(nc.scalar.activation(st[:, 6:7], st[:, 5:6], ACT.Sqrt))
        nc.vector.wait_ge(sc.sem, sc.n)
        dv.inc(nc.vector.reciprocal(st[:, 7:8], st[:, 6:7]))
        dv.sw(nc.vector)
        dv.inc(nc.vector.tensor_mul(st[:, 8:9], st[:, 15:16], st[:, 7:8]))
        dv.sw(nc.vector)
        dv.inc(nc.vector.tensor_mul(st[:, 9:10], st[:, 8:9], st[:, 7:8]))
        dv.sw(nc.vector)
        dv.inc(nc.vector.tensor_scalar_add(st[:, 9:10], st[:, 9:10], EPS_LN))
        v2_done = dv.n
        nc.scalar.wait_ge(dv.sem, v2_done)
        sc.inc(nc.scalar.activation(st[:, 10:11], st[:, 9:10], ACT.Sqrt))
        invr2_done = sc.n
        nc.vector.wait_ge(sc.sem, invr2_done)
        dv.inc(nc.vector.reciprocal(st[:, 11:12], st[:, 10:11]))
        dv.sw(nc.vector)
        dv.inc(nc.vector.tensor_mul(st[:, 12:13], st[:, 7:8], st[:, 11:12]))
        dv.sw(nc.vector)
        dv.inc(nc.vector.tensor_mul(st[:, 13:14], st[:, 2:3], st[:, 12:13]))
        dv.sw(nc.vector)
        dv.inc(nc.vector.tensor_scalar_mul(st[:, 14:15], st[:, 13:14], -1.0))
        micro_done = dv.n
        if p.stop < 3:
            continue
        nc.scalar.wait_ge(dv.sem, micro_done)
        sc.inc(nc.scalar.activation(tbf[:], xt[:, slot:slot + D], ACT.Identity,
                                    bias=st[:, 14:15], scale=st[:, 12:13]))
        apply_done = sc.n
        prev["apply_sc"] = apply_done
        if p.stop < 4:
            continue
        nc.tensor.wait_ge(sc.sem, apply_done)
        nc.tensor.wait_ge(ld.sem, setup_ld)
        if g > 0:
            nc.tensor.wait_ge(dv.sem, prev["nT_ev_dv"])
            nc.tensor.wait_ge(sc.sem, prev["nT_ev_sc"])
        for dch in range(DCH):
            off = (dch % 4) * 128 + (dch // 4) * 1024
            mm = nc.tensor.transpose(ps_tp[:, off:off + 128],
                                     tbf[:, dch * 128:(dch + 1) * 128], b["idt"])
            if dch == DCH - 1:
                pe.inc(mm)
        tp_done = pe.n
        if p.stop < 5:
            continue
        nc.vector.wait_ge(pe.sem, tp_done)
        nc.scalar.wait_ge(pe.sem, tp_done)
        for dch in range(DCH):
            off = (dch % 4) * 128 + (dch // 4) * 1024
            src = ps_tp[:, off:off + 128]
            dst = nT[:, dch * 128:(dch + 1) * 128]
            gsl = b["ln0g"][:, dch:dch + 1]
            bsl = b["ln0b"][:, dch:dch + 1]
            if dch < 4:
                dv.inc(nc.vector.tensor_scalar(dst, src, gsl, bsl, ALU.mult, ALU.add))
            else:
                sc.inc(nc.scalar.activation(dst, src, ACT.Identity, bias=bsl, scale=gsl))
        prev["nT_ev_dv"] = dv.n
        prev["nT_ev_sc"] = sc.n
        if p.stop < 6:
            continue
        nc.tensor.wait_ge(dv.sem, prev["nT_ev_dv"])
        nc.tensor.wait_ge(sc.sem, prev["nT_ev_sc"])
        nc.tensor.wait_ge(ld.sem, setup_ld)
        if g > 0:
            nc.tensor.wait_ge(sc.sem, prev["silu_sc"])
        for q4 in range(4):
            sl4 = slice(q4 * 512, (q4 + 1) * 512)
            for dch in range(DCH):
                nc.tensor.matmul(ps_big[:, sl4],
                                 nT[:, dch * 128:(dch + 1) * 128],
                                 b["w_hid"][:, dch * 4096 + q4 * 512:
                                            dch * 4096 + (q4 + 1) * 512],
                                 start=(dch == 0), stop=False,
                                 skip_group_check=True)
            mm = nc.tensor.matmul(ps_big[:, sl4],
                                  b["onesr"], b["b_hid"][:, q4 * 512:(q4 + 1) * 512],
                                  start=False, stop=True, skip_group_check=True)
            if q4 == 3:
                pe.inc(mm)
        vmm_done = pe.n
        if p.stop < 7:
            continue
        nc.scalar.wait_ge(pe.sem, vmm_done)
        if g > 0:
            nc.scalar.wait_ge(svt.sem, svt.n)
        sc.inc(nc.scalar.activation(b["vt_sb"][:, 0:1024], ps_big[:, 0:1024], ACT.Silu))
        sc.inc(nc.scalar.activation(b["vt_sb"][:, 1024:2048], ps_big[:, 1024:2048], ACT.Silu))
        prev["silu_sc"] = sc.n
        nc.gpsimd.wait_ge(sc.sem, prev["silu_sc"])
        svt.inc(nc.gpsimd.dma_start(out=b["scr_vt"][g * 128:(g + 1) * 128, :],
                                    in_=b["vt_sb"]), 16)
        if p.stop < 8:
            continue
        nc.tensor.wait_ge(sc.sem, prev["qk_sc"]) if g > 0 else None
        for dch in range(DCH):
            mm = nc.tensor.matmul(ps_s[:, 0:128],
                                  b["w_qk"][:, dch * 128:(dch + 1) * 128],
                                  nT[:, dch * 128:(dch + 1) * 128],
                                  start=(dch == 0), stop=(dch == DCH - 1))
            if dch == DCH - 1:
                pe.inc(mm)
        qkmm_done = pe.n
        nc.scalar.wait_ge(pe.sem, qkmm_done)
        sc.inc(nc.scalar.activation(b["qk_s"], ps_s[:, 0:128], ACT.Silu, bias=b["bqk"]))
        prev["qk_sc"] = sc.n
        nc.vector.wait_ge(sc.sem, prev["qk_sc"])
        dv.inc(nc.vector.tensor_scalar(b["kT"][:, g * 128:(g + 1) * 128], b["qk_s"],
                                       b["osv"][:, 2:3], b["osv"][:, 3:4],
                                       ALU.mult, ALU.add))
        if own:
            dv.inc(nc.vector.tensor_scalar(b["qtmp"], b["qk_s"],
                                           b["osv"][:, 0:1], b["osv"][:, 1:2],
                                           ALU.mult, ALU.add))
            dv.sw(nc.vector)
            dv.inc(nc.vector.tensor_mul(b["qf"][:, loc:loc + 128], b["qtmp"],
                                        b["invd"][:, loc:loc + 128]))
            nc.tensor.wait_ge(sc.sem, prev["silu_sc"])
            for q4 in range(4):
                sl4 = slice(q4 * 512, (q4 + 1) * 512)
                for dch in range(DCH):
                    nc.tensor.matmul(ps_big[:, sl4],
                                     nT[:, dch * 128:(dch + 1) * 128],
                                     b["w_hid"][:, dch * 4096 + 2048 + q4 * 512:
                                                dch * 4096 + 2048 + (q4 + 1) * 512],
                                     start=(dch == 0), stop=False,
                                     skip_group_check=True)
                mm = nc.tensor.matmul(ps_big[:, sl4],
                                      b["onesr"],
                                      b["b_hid"][:, 2048 + q4 * 512:2048 + (q4 + 1) * 512],
                                      start=False, stop=True, skip_group_check=True)
                if q4 == 3:
                    pe.inc(mm)
            gmm_done = pe.n
            nc.scalar.wait_ge(pe.sem, gmm_done)
            if sgt.n:
                nc.scalar.wait_ge(sgt.sem, sgt.n)
            sc.inc(nc.scalar.activation(b["gt_sb"][:, 0:1024], ps_big[:, 0:1024], ACT.Silu))
            sc.inc(nc.scalar.activation(b["gt_sb"][:, 1024:2048], ps_big[:, 1024:2048], ACT.Silu))
            gsilu_done = sc.n
            prev["silu_sc"] = sc.n
            nc.gpsimd.wait_ge(sc.sem, gsilu_done)
            sgt.inc(nc.gpsimd.dma_start(out=b["scr_gt"][loc:loc + 128, :],
                                        in_=b["gt_sb"]), 16)
            if sxh.n:
                nc.vector.wait_ge(sxh.sem, sxh.n)
            dv.inc(nc.vector.scalar_tensor_tensor(b["xh_sb"], tbf[:], st[:, 10:11],
                                                  xt[:, slot:slot + D],
                                                  ALU.mult, ALU.add))
            prev["xh_dv"] = dv.n
            nc.gpsimd.wait_ge(dv.sem, prev["xh_dv"])
            sxh.inc(nc.gpsimd.dma_start(out=b["scr_xh"][loc:loc + 128, :],
                                        in_=b["xh_sb"]), 16)
        else:
            prev["xh_dv"] = dv.n


def _tap_A(p, c, dbg, astart, tot):
    nc, b = p.nc, p.b
    ld, gp, pe, dv, sc = (b[s] for s in ("s_ld", "s_gp", "s_pe", "s_dve", "s_sce"))
    for nm in ("s_vt", "s_gt", "s_xh"):
        if b[nm].n:
            nc.sync.wait_ge(b[nm].sem, b[nm].n)
            nc.gpsimd.wait_ge(b[nm].sem, b[nm].n)
    nc.sync.wait_ge(dv.sem, dv.n)
    nc.sync.wait_ge(sc.sem, sc.n)
    nc.gpsimd.wait_ge(dv.sem, dv.n)
    nc.gpsimd.wait_ge(sc.sem, sc.n)
    pend = astart + tot
    import os
    if os.environ.get("TAP_MIN"):
        ld.inc(nc.sync.dma_start(out=dbg[0:128, 0:1024], in_=b["xh_sb"]), 16)
        nc.sync.wait_ge(ld.sem, ld.n)
        return
    ld.inc(nc.sync.dma_start(out=dbg[0:tot, 0:1024], in_=b["scr_xh"][0:tot, :]), 16)
    gp.inc(nc.gpsimd.dma_start(out=dbg[0:tot, 1024:3072], in_=b["scr_vt"][astart:pend, :]), 16)
    gp.inc(nc.gpsimd.dma_start(out=dbg[0:tot, 3072:4096], in_=b["scr_gt"][0:tot, 0:1024]), 16)
    dbg2 = b.get("dbg2")
    if dbg2 is not None:
        gp.inc(nc.gpsimd.dma_start(out=dbg2[0:128, 0:tot], in_=b["qf"][:, 0:tot]), 16)
        gp.inc(nc.gpsimd.dma_start(out=dbg2[0:128, p.MAXTOT:p.MAXTOT + tot],
                                   in_=b["kT"][:, astart:pend]), 16)
    nc.sync.wait_ge(ld.sem, ld.n)
    nc.sync.wait_ge(gp.sem, gp.n)


def _mk_bo16():
    m = np.zeros((128, 128), np.float32)
    for pp in range(8):
        m[0:64, pp * 16 + 2 * pp] = 1.0
        m[64:128, pp * 16 + 2 * pp + 1] = 1.0
    return m.astype(BF)


def _mk_sel16():
    m = np.zeros((16, 1024), np.float32)
    for pp in range(8):
        m[2 * pp, pp * 128:pp * 128 + 64] = 1.0
        m[2 * pp + 1, pp * 128 + 64:pp * 128 + 128] = 1.0
    return m.astype(BF)


def _mk_masks():
    r = np.arange(128)[:, None] % 64
    t = np.arange(64)[None, :]
    su = -np.where(r < t, 1.0, 0.0)
    sl = -np.where(t < r, 1.0, 0.0)
    hk = np.where(r < t, 1.0, 0.0)
    tb = -np.where(r <= t, 1.0, 0.0)
    tk = np.where(r <= t, 1.0, 0.0)
    out = np.concatenate([np.tile(m, (1, SUPD // 64)) for m in (su, sl, hk, tb, tk)], axis=1)
    return out.astype(np.float32)


def _host_inputs(inputs):
    MAXTOT = max(SLICES) + OV
    x = _f32(np.asarray(inputs["L0_attn_x"]).reshape(T_FULL, D))
    shared = {
        "x": x,
        "ident": np.eye(128, dtype=BF),
        "identf": np.eye(128, dtype=np.float32),
        "trimask": np.triu(np.ones((128, 128), np.float32)).astype(BF),
        "ln0g_in": _perpart(inputs["L0_attn_ln_g"]),
        "ln0b_in": _perpart(inputs["L0_attn_ln_b"]),
        "Whid": _bf(inputs["L0_attn_W_hidden"]),
        "bhid_in": _bf(np.asarray(inputs["L0_attn_b_hidden"]).reshape(1, 4096)),
        "Wqk": _bf(inputs["L0_attn_W_qk"]),
        "bqk_in": _f32(np.asarray(inputs["L0_attn_b_qk"]).reshape(128, 1)),
        "os_in": _f32(np.stack([np.asarray(inputs["L0_attn_os_gamma"])[0],
                                np.asarray(inputs["L0_attn_os_beta"])[0],
                                np.asarray(inputs["L0_attn_os_gamma"])[1],
                                np.asarray(inputs["L0_attn_os_beta"])[1]], axis=1)),
        "Wout": _bf(inputs["L0_attn_W_out"]),
        "bout_in": _bf(np.asarray(inputs["L0_attn_b_out"]).reshape(1, D)),
        "ones_in": np.ones((1, 128), BF),
        "wgT": _bf(np.asarray(inputs["L0_ffn_w_gate"]).T),
        "wuT": _bf(np.asarray(inputs["L0_ffn_w_up"]).T),
        "wdT": _bf(np.asarray(inputs["L0_ffn_w_down"]).T),
        "maa_w1T": _bf(np.asarray(inputs["L1_attn_maa_w1"]).T),
        "gate_w1T": _bf(np.asarray(inputs["L1_attn_gate_w1"]).T),
        "decay_w1T": _bf(np.asarray(inputs["L1_attn_decay_w1"]).T),
        "WrT": _bf(np.asarray(inputs["L1_attn_Wr"]).T),
        "WkT": _bf(np.asarray(inputs["L1_attn_Wk"]).T),
        "WvT": _bf(np.asarray(inputs["L1_attn_Wv"]).T),
        "lw1T_in": _bf(np.concatenate([np.asarray(inputs[f"L1_attn_{nm}_w1"]).T
                                       for nm in ("aaa", "kkk", "ma", "mk")], axis=0)),
        "maa_w2_in": _bf(np.concatenate(list(np.asarray(inputs["L1_attn_maa_w2"])), axis=1)),
        "gate_w2_in": _bf(inputs["L1_attn_gate_w2"]),
        "decay_w2_in": _bf(inputs["L1_attn_decay_w2"]),
        "lw2_in": _bf(np.concatenate([np.asarray(inputs[f"L1_attn_{nm}_w2"])
                                      for nm in ("aaa", "kkk", "ma", "mk")], axis=1)),
        "pvecs_in": np.concatenate([
            _perpart(inputs["L1_attn_time_maa_x"]),
            _perpart(inputs["L1_attn_time_decay"]),
            _perpart(inputs["L1_attn_time_aaaaa"]),
            _perpart(inputs["L1_attn_time_misc_a"]),
            _perpart(inputs["L1_attn_time_misc_k"])], axis=1),
        "tmaa_in": np.concatenate([_perpart(np.asarray(inputs["L1_attn_time_maa"])[g, 0, 0])
                                   for g in range(4)], axis=1),
        "npvecs_in": -np.concatenate([
            _perpart(inputs["L1_attn_time_maa_x"]),
            _perpart(inputs["L1_attn_time_decay"]),
            _perpart(inputs["L1_attn_time_aaaaa"]),
            _perpart(inputs["L1_attn_time_misc_a"]),
            _perpart(inputs["L1_attn_time_misc_k"])], axis=1),
        "bo16_in": _mk_bo16(),
        "sel16_in": _mk_sel16(),
        "masks_in": _mk_masks(),
        "eye64_in": np.concatenate([np.eye(64, dtype=np.float32)] * 2, axis=0),
        "WoT": _bf(np.asarray(inputs["L1_attn_Wo"]).T),
        "w1_in": _bf(inputs["L1_ffn_w1"]),
        "w2_in": _bf(inputs["L1_ffn_w2"]),
        "b2_in": _bf(np.asarray(inputs["L1_ffn_b2"]).reshape(1, D)),
        "lnwb_in": np.concatenate([_perpart(inputs["L1_attn_ln_w"]),
                                   _perpart(inputs["L1_attn_ln_b"]),
                                   _perpart(inputs["L1_attn_time_faaaa"])], axis=1),
        "b1_in": np.ascontiguousarray(
            _f32(inputs["L1_ffn_b1"]).reshape(32, 128).T),
    }
    inv = 1.0 / (np.arange(T_FULL, dtype=np.float32) + 1.0)
    in_maps = []
    for cc in range(N_CORES):
        start = STARTS[cc]
        pend = start + SLICES[cc]
        astart = max(0, start - OV)
        tot = pend - astart
        invd = np.zeros((128, MAXTOT), np.float32)
        invd[:, 0:tot] = np.broadcast_to(inv[astart:pend], (128, tot))
        m = dict(shared)
        m["invd_in"] = invd
        in_maps.append(m)
    return in_maps


def kernel(**inputs):
    in_maps = _host_inputs(inputs)
    nc = build_program()
    res = run_bass_kernel_spmd(nc, in_maps, core_ids=list(range(N_CORES)))
    outs = [res.results[cc]["y"][0:SLICES[cc], :] for cc in range(N_CORES)]
    return np.concatenate(outs, axis=0).reshape(1, T_FULL, D)


def _barrier(p, eng_names=("tensor", "vector", "scalar", "sync", "gpsimd")):
    nc, b = p.nc, p.b
    cts = [b[k] for k in ("s_pe", "s_dve", "s_sce", "s_ld", "s_gp", "s_x0", "s_x1",
                          "s_vt", "s_gt", "s_xh", "s_r0", "s_r1", "s_w0", "s_w1",
                          "s_w2a", "s_w2b", "s_cc")]
    for en in eng_names:
        eng = getattr(nc, en)
        for ct in cts:
            if ct.n:
                eng.wait_ge(ct.sem, ct.n)


def _phase_B(p, c, astart, pend, dbg):
    nc, b = p.nc, p.b
    pe, dv, sc = b["s_pe"], b["s_dve"], b["s_sce"]
    sv = [b["s_x0"], b["s_x1"]]
    sgt, sxh = b["s_r0"], b["s_r1"]
    sw1 = b["s_w1"]
    p.ph_sbuf("att", [128, 128 * 256], BF16)
    p.ph_sbuf("rl_sb", [128, 256], F32)
    p.ph_sbuf("vt2", [128, 2 * 2048], BF16)
    p.ph_sbuf("gt2", [128, 2 * 2048], BF16)
    p.ph_sbuf("og", [128, 2 * 2048], F32)
    p.ph_sbuf("ogT", [128, 2048], BF16)
    p.ph_sbuf("xh2", [128, 2 * D], F32)
    p.ph_sbuf("x1_sb", [128, D], F32)
    p.ph_sbuf("w_out", [128, 16 * 1024], BF16)
    p.ph_sbuf("b_out", [1, D], BF16)
    ps_o = [p.ph_psum("ps_o0", [128, 2048], F32), p.ph_psum("ps_o1", [128, 2048], F32)]
    _barrier(p)
    ld = b["s_ld"]
    for fc in range(16):
        ld.inc(nc.sync.dma_start(out=b["w_out"][:, fc * 1024:(fc + 1) * 1024],
                                 in_=b["Wout"][fc * 128:(fc + 1) * 128, :]), 16)
    ld.inc(nc.sync.dma_start(out=b["b_out"], in_=b["bout_in"]), 16)
    wout_ld = ld.n

    n_it = (pend - astart) // 128
    att, vt2, gt2, og, ogT = b["att"], b["vt2"], b["gt2"], b["og"], b["ogT"]
    prev = dict(vmm_pe=0, og_dv=0, x1_dv=0, ogT_done=(0, 0), wout_pe=0)
    for ig in range(ceil_div(n_it, 2)):
        t0 = astart // 128 + 2 * ig
        nt = min(2, n_it - 2 * ig)
        gis = [t0 + t for t in range(nt)]
        gimax = gis[-1]
        iloc0 = (t0 * 128) - astart
        nc.tensor.wait_ge(dv.sem, dv.n)
        nc.tensor.wait_ge(sc.sem, sc.n)
        nc.vector.wait_ge(pe.sem, pe.n)
        for j in range(gimax + 1):
            slot = (j % 2) * 256
            mm = nc.tensor.matmul(ps_o[1][:, slot:slot + nt * 128],
                                  b["kT"][:, j * 128:(j + 1) * 128],
                                  b["qf"][:, iloc0:iloc0 + nt * 128],
                                  start=True, stop=True, skip_group_check=True)
            pe.inc(mm)
            nc.vector.wait_ge(pe.sem, pe.n)
            nc.scalar.wait_ge(pe.sem, pe.n)
            nc.scalar.wait_ge(dv.sem, dv.n)
            sc.inc(nc.scalar.activation(b["rl_sb"][:, 0:nt * 128],
                                        ps_o[1][:, slot:slot + nt * 128], ACT.Relu))
            nc.vector.wait_ge(sc.sem, sc.n)
            dv.inc(nc.vector.tensor_mul(att[:, j * 256:j * 256 + nt * 128],
                                        b["rl_sb"][:, 0:nt * 128],
                                        ps_o[1][:, slot:slot + nt * 128]))
            for t in range(nt):
                if j == gis[t]:
                    dv.sw(nc.vector)
                    dv.inc(nc.vector.tensor_mul(
                        att[:, j * 256 + t * 128:j * 256 + (t + 1) * 128],
                        att[:, j * 256 + t * 128:j * 256 + (t + 1) * 128],
                        b["trim"]))
            if j >= 1:
                nc.tensor.wait_ge(dv.sem, dv.n - 1 if False else dv.n)
        att_done = dv.n
        nc.tensor.wait_ge(dv.sem, att_done)
        nc.sync.wait_ge(pe.sem, prev["vmm_pe"])
        vneed = {}
        sv[0].inc(nc.sync.dma_start(out=vt2[:, 0:2048], in_=b["scr_vt"][0:128, :]), 16)
        vneed[0] = sv[0].n
        for j in range(gimax + 1):
            par = j % 2
            if j + 1 <= gimax:
                if j >= 1:
                    nc.sync.wait_ge(pe.sem, prev["vj_pe"])
                npar = (j + 1) % 2
                sv[npar].inc(nc.sync.dma_start(
                    out=vt2[:, npar * 2048:(npar + 1) * 2048],
                    in_=b["scr_vt"][(j + 1) * 128:(j + 2) * 128, :]), 16)
                vneed[j + 1] = sv[npar].n
            nc.tensor.wait_ge(sv[par].sem, vneed[j])
            for t in range(nt):
                if j > gis[t]:
                    continue
                for q4 in range(4):
                    mm = nc.tensor.matmul(
                        ps_o[t][:, q4 * 512:(q4 + 1) * 512],
                        att[:, j * 256 + t * 128:j * 256 + (t + 1) * 128],
                        vt2[:, par * 2048 + q4 * 512:par * 2048 + (q4 + 1) * 512],
                        start=(j == 0), stop=(j == gis[t]),
                        skip_group_check=True)
            pe.inc(mm)
            prev["vj_pe"] = pe.n
        vmm_done = pe.n
        prev["vmm_pe"] = vmm_done
        nc.sync.wait_ge(dv.sem, prev["x1_dv"])
        for t in range(nt):
            loc = iloc0 + t * 128
            sgt.inc(nc.sync.dma_start(out=gt2[:, t * 2048:(t + 1) * 2048],
                                      in_=b["scr_gt"][loc:loc + 128, :]), 16)
            sxh.inc(nc.sync.dma_start(out=b["xh2"][:, t * D:(t + 1) * D],
                                      in_=b["scr_xh"][loc:loc + 128, :]), 16)
        for t in range(nt):
            loc = iloc0 + t * 128
            nc.vector.wait_ge(pe.sem, vmm_done)
            nc.vector.wait_ge(sgt.sem, sgt.n)
            for hh in range(2):
                dv.inc(nc.vector.scalar_tensor_tensor(
                    og[:, t * 2048 + hh * 1024:t * 2048 + (hh + 1) * 1024],
                    ps_o[t][:, hh * 1024:(hh + 1) * 1024], 1.0,
                    gt2[:, t * 2048 + hh * 1024:t * 2048 + (hh + 1) * 1024],
                    ALU.mult, ALU.mult))
        og_done = dv.n
        prev["og_dv"] = dv.n
        for t in range(nt):
            nc.tensor.wait_ge(dv.sem, og_done)
            if t > 0:
                nc.tensor.wait_ge(dv.sem, dv.n)
                nc.tensor.wait_ge(sc.sem, sc.n)
            for fc in range(16):
                grp = fc // 4
                off = grp * 512 + (fc % 4) * 128
                mm = nc.tensor.transpose(ps_o[1][:, off:off + 128],
                                         og[:, t * 2048 + fc * 128:t * 2048 + (fc + 1) * 128],
                                         b["idtf"])
                if fc == 15:
                    pe.inc(mm)
            tp_done = pe.n
            nc.vector.wait_ge(pe.sem, tp_done)
            nc.scalar.wait_ge(pe.sem, tp_done)
            for fc in range(16):
                grp = fc // 4
                off = grp * 512 + (fc % 4) * 128
                dst = ogT[:, fc * 128:(fc + 1) * 128]
                if grp < 2:
                    dv.inc(nc.vector.tensor_copy(dst, ps_o[1][:, off:off + 128]))
                else:
                    sc.inc(nc.scalar.copy(dst, ps_o[1][:, off:off + 128]))
            ev_dv, ev_sc = dv.n, sc.n
            nc.tensor.wait_ge(dv.sem, ev_dv)
            nc.tensor.wait_ge(sc.sem, ev_sc)
            nc.tensor.wait_ge(ld.sem, wout_ld)
            for fc in range(16):
                for hq in range(2):
                    nc.tensor.matmul(ps_o[0][:, hq * 512:(hq + 1) * 512],
                                     ogT[:, fc * 128:(fc + 1) * 128],
                                     b["w_out"][:, fc * 1024 + hq * 512:fc * 1024 + (hq + 1) * 512],
                                     start=(fc == 0), stop=False, skip_group_check=True)
            for hq in range(2):
                mm = nc.tensor.matmul(ps_o[0][:, hq * 512:(hq + 1) * 512],
                                      b["onesr"], b["b_out"][:, hq * 512:(hq + 1) * 512],
                                      start=False, stop=True, skip_group_check=True)
            pe.inc(mm)
            wout_done = pe.n
            nc.vector.wait_ge(pe.sem, wout_done)
            nc.vector.wait_ge(sxh.sem, sxh.n)
            if sw1.n:
                nc.vector.wait_ge(sw1.sem, sw1.n)
            dv.inc(nc.vector.scalar_tensor_tensor(
                b["x1_sb"], ps_o[0][:, 0:1024], 1.0, b["xh2"][:, t * D:(t + 1) * D],
                ALU.mult, ALU.add))
            prev["x1_dv"] = dv.n
            nc.gpsimd.wait_ge(dv.sem, dv.n)
            loc = iloc0 + t * 128
            sw1.inc(nc.gpsimd.dma_start(out=b["scr_x1"][loc:loc + 128, :],
                                        in_=b["x1_sb"]), 16)
    if p.tap == "B":
        tot = pend - astart
        _barrier(p, ("sync",))
        b["s_ld"].inc(nc.sync.dma_start(out=dbg[0:tot, 0:1024], in_=b["scr_x1"][0:tot, :]), 16)
        nc.sync.wait_ge(b["s_ld"].sem, b["s_ld"].n)


def _ln_tile(p, xsrc, tdst, extra_scale_col=None):
    nc, b = p.nc, p.b
    dv, sc = b["s_dve"], b["s_sce"]
    st = b["stats"]
    dv.inc(nc.vector.reduce_sum(st[:, 0:1], xsrc, AX.X))
    sc.inc(nc.scalar.activation(b["sq_junk"], xsrc, ACT.Square, accum_out=st[:, 1:2]))
    nc.vector.wait_ge(sc.sem, sc.n)
    dv.sw(nc.vector)
    dv.inc(nc.vector.tensor_scalar_mul(st[:, 2:3], st[:, 0:1], 1.0 / D))
    dv.inc(nc.vector.tensor_scalar_mul(st[:, 3:4], st[:, 1:2], 1.0 / D))
    dv.sw(nc.vector)
    dv.inc(nc.vector.tensor_mul(st[:, 4:5], st[:, 2:3], st[:, 2:3]))
    dv.sw(nc.vector)
    dv.inc(nc.vector.scalar_tensor_tensor(st[:, 5:6], st[:, 3:4], EPS_LN,
                                          st[:, 4:5], ALU.add, ALU.subtract))
    nc.scalar.wait_ge(dv.sem, dv.n)
    sc.inc(nc.scalar.activation(st[:, 6:7], st[:, 5:6], ACT.Sqrt))
    nc.vector.wait_ge(sc.sem, sc.n)
    dv.inc(nc.vector.reciprocal(st[:, 7:8], st[:, 6:7]))
    dv.sw(nc.vector)
    dv.inc(nc.vector.tensor_mul(st[:, 8:9], st[:, 2:3], st[:, 7:8]))
    dv.sw(nc.vector)
    dv.inc(nc.vector.tensor_scalar_mul(st[:, 9:10], st[:, 8:9], -1.0))
    nc.scalar.wait_ge(dv.sem, dv.n)
    sc.inc(nc.scalar.activation(tdst, xsrc, ACT.Identity,
                                bias=st[:, 9:10], scale=st[:, 7:8]))
    return sc.n


def _transpose_tile(p, tsrc, fm_dst_fn, pe_wait=None):
    nc, b = p.nc, p.b
    pe, dv, sc = b["s_pe"], b["s_dve"], b["s_sce"]
    ps_tp = b["ps_tp"]
    for dch in range(DCH):
        off = (dch % 4) * 128 + (dch // 4) * 1024
        mm = nc.tensor.transpose(ps_tp[:, off:off + 128],
                                 tsrc[:, dch * 128:(dch + 1) * 128], b["idt"])
        if dch == DCH - 1:
            pe.inc(mm)
    nc.vector.wait_ge(pe.sem, pe.n)
    nc.scalar.wait_ge(pe.sem, pe.n)
    for dch in range(DCH):
        off = (dch % 4) * 128 + (dch // 4) * 1024
        if dch < 4:
            dv.inc(nc.vector.tensor_copy(fm_dst_fn(dch), ps_tp[:, off:off + 128]))
        else:
            sc.inc(nc.scalar.copy(fm_dst_fn(dch), ps_tp[:, off:off + 128]))
    return dv.n, sc.n


def _phase_C(p, c, astart, pend, dbg):
    nc, b = p.nc, p.b
    pe, dv, sc = b["s_pe"], b["s_dve"], b["s_sce"]
    ld = b["s_ld"]
    sx = [b["s_x0"], b["s_x1"]]
    sw0 = b["s_w0"]
    tot = pend - astart
    p.ph_sbuf("xc", [128, 4 * D], F32)
    p.ph_sbuf("tbfC", [128, D], BF16)
    p.ph_sbuf("sq_junk", [128, D], BF16)
    p.ph_sbuf("h1T", [128, DCH * 512], BF16)
    p.ph_sbuf("hfm", [128, 24 * 512], BF16)
    p.ph_sbuf("sil", [128, 512], F32)
    p.ph_sbuf("x2_sb", [128, D], F32)
    p.ph_sbuf("w_g", [128, DCH * 3072], BF16)
    p.ph_sbuf("w_u", [128, DCH * 3072], BF16)
    p.ph_sbuf("w_d", [128, 24 * 1024], BF16)
    p.ph_psum("ps_c1", [128, 2048], F32)
    p.ph_psum("ps_tp", [128, 2048], BF16)
    p.ph_psum("ps_c2", [128, 1024], F32)
    _barrier(p)
    for dch in range(DCH):
        ld.inc(nc.sync.dma_start(out=b["w_g"][:, dch * 3072:(dch + 1) * 3072],
                                 in_=b["wgT"][dch * 128:(dch + 1) * 128, :]), 16)
        ld.inc(nc.sync.dma_start(out=b["w_u"][:, dch * 3072:(dch + 1) * 3072],
                                 in_=b["wuT"][dch * 128:(dch + 1) * 128, :]), 16)
    for hch in range(24):
        ld.inc(nc.sync.dma_start(out=b["w_d"][:, hch * 1024:(hch + 1) * 1024],
                                 in_=b["wdT"][hch * 128:(hch + 1) * 128, :]), 16)
    w_ld = ld.n
    SC = 1.0 / 127.0

    n_tiles = tot // 128
    prev = dict(x2_dv=0, hfm_dv=0, hfm_sc=0)
    for sup0 in range(0, n_tiles, 4):
        ntile = min(4, n_tiles - sup0)
        stoks = ntile * 128
        nc.sync.wait_ge(dv.sem, prev["x2_dv"])
        for ti in range(ntile):
            loc = (sup0 + ti) * 128
            sx[ti % 2].inc(nc.sync.dma_start(out=b["xc"][:, ti * D:(ti + 1) * D],
                                             in_=b["scr_x1"][loc:loc + 128, :]), 16)
        xld = [sx[ti % 2].n for ti in range(ntile)]
        for ti in range(ntile):
            nc.vector.wait_ge(sx[ti % 2].sem, xld[ti])
            nc.scalar.wait_ge(sx[ti % 2].sem, xld[ti])
            if ti == 0:
                nc.tensor.wait_ge(dv.sem, prev["hfm_dv"])
                nc.tensor.wait_ge(sc.sem, prev["hfm_sc"])
            _ln_tile(p, b["xc"][:, ti * D:(ti + 1) * D], b["tbfC"])
            nc.tensor.wait_ge(sc.sem, sc.n)
            nc.tensor.wait_ge(dv.sem, dv.n)
            _transpose_tile(p, b["tbfC"],
                            lambda dch, _ti=ti: b["h1T"][:, dch * 512 + _ti * 128:
                                                         dch * 512 + (_ti + 1) * 128])
        h1_dv, h1_sc = dv.n, sc.n
        nc.tensor.wait_ge(dv.sem, h1_dv)
        nc.tensor.wait_ge(sc.sem, h1_sc)
        nc.tensor.wait_ge(ld.sem, w_ld)
        ev_after = {}
        for hch in range(24):
            par = hch % 2
            gsl = slice(par * 1024, par * 1024 + 512)
            usl = slice(par * 1024 + 512, par * 1024 + 1024)
            if hch >= 2:
                nc.tensor.wait_ge(dv.sem, ev_after[hch - 2])
            for dch in range(DCH):
                nc.tensor.matmul(b["ps_c1"][:, gsl][:, 0:stoks],
                                 b["w_g"][:, dch * 3072 + hch * 128:dch * 3072 + (hch + 1) * 128],
                                 b["h1T"][:, dch * 512:dch * 512 + stoks],
                                 start=(dch == 0), stop=(dch == DCH - 1),
                                 skip_group_check=True)
            for dch in range(DCH):
                mm = nc.tensor.matmul(b["ps_c1"][:, usl][:, 0:stoks],
                                 b["w_u"][:, dch * 3072 + hch * 128:dch * 3072 + (hch + 1) * 128],
                                 b["h1T"][:, dch * 512:dch * 512 + stoks],
                                 start=(dch == 0), stop=(dch == DCH - 1),
                                 skip_group_check=True)
            pe.inc(mm)
            nc.scalar.wait_ge(pe.sem, pe.n)
            nc.scalar.wait_ge(dv.sem, dv.n)
            sc.inc(nc.scalar.activation(b["sil"][:, 0:stoks], b["ps_c1"][:, gsl][:, 0:stoks],
                                        ACT.Silu, scale=SC))
            nc.vector.wait_ge(sc.sem, sc.n)
            dv.inc(nc.vector.scalar_tensor_tensor(
                b["hfm"][:, hch * 512:hch * 512 + stoks],
                b["sil"][:, 0:stoks], SC,
                b["ps_c1"][:, usl][:, 0:stoks], ALU.mult, ALU.mult))
            ev_after[hch] = dv.n
        prev["hfm_dv"], prev["hfm_sc"] = dv.n, sc.n
        hfm_done = dv.n
        nc.tensor.wait_ge(dv.sem, hfm_done)
        for ti in range(ntile):
            nc.tensor.wait_ge(dv.sem, dv.n)
            for hch in range(24):
                for hq in range(2):
                    mm = nc.tensor.matmul(b["ps_c2"][:, hq * 512:(hq + 1) * 512],
                                          b["hfm"][:, hch * 512 + ti * 128:hch * 512 + (ti + 1) * 128],
                                          b["w_d"][:, hch * 1024 + hq * 512:hch * 1024 + (hq + 1) * 512],
                                          start=(hch == 0), stop=(hch == 23),
                                          skip_group_check=True)
            pe.inc(mm)
            nc.vector.wait_ge(pe.sem, pe.n)
            if b["s_w0"].n:
                nc.vector.wait_ge(sw0.sem, sw0.n)
            dv.inc(nc.vector.scalar_tensor_tensor(
                b["x2_sb"], b["ps_c2"][:], SC,
                b["xc"][:, ti * D:(ti + 1) * D], ALU.mult, ALU.add))
            prev["x2_dv"] = dv.n
            nc.gpsimd.wait_ge(dv.sem, dv.n)
            loc = (sup0 + ti) * 128
            sw0.inc(nc.gpsimd.dma_start(out=b["scr_x2"][loc:loc + 128, :],
                                        in_=b["x2_sb"]), 16)
    if p.tap == "C":
        _barrier(p, ("sync",))
        ld.inc(nc.sync.dma_start(out=dbg[0:tot, 0:1024], in_=b["scr_x2"][0:tot, :]), 16)
        nc.sync.wait_ge(ld.sem, ld.n)


BETA = 0.6065306597126334


def _phase_D(p, c, astart, pend, dbg):
    nc, b = p.nc, p.b
    pe, dv, sc = b["s_pe"], b["s_dve"], b["s_sce"]
    ld = b["s_ld"]
    sx = [b["s_x0"], b["s_x1"]]
    sw0 = b["s_w0"]
    start = STARTS[c]
    n = pend - start
    n_sup = ceil_div(n, SUPD)
    T2 = SUPD + 8

    p.ph_sbuf("maa1T", [128, DCH * 128], BF16)
    p.ph_sbuf("maa2", [32, 4 * D], BF16)
    p.ph_sbuf("gw1T", [128, DCH * 128], BF16)
    p.ph_sbuf("gw2", [128, D], BF16)
    p.ph_sbuf("dw1T", [128, DCH * 64], BF16)
    p.ph_sbuf("dw2", [64, D], BF16)
    p.ph_sbuf("lw1T", [128, 4 * DCH * 16], BF16)
    p.ph_sbuf("lw2", [16, 4 * D], BF16)
    p.ph_sbuf("wproj", [128, DCH * D], BF16)
    p.ph_sbuf("pvecs", [128, 40], F32)
    p.ph_sbuf("npvecs", [128, 40], F32)
    p.ph_sbuf("tmaa", [128, 4 * DCH], F32)
    p.ph_sbuf("masks", [128, 5 * SUPD], F32)
    p.ph_sbuf("eye64", [128, 64], F32)

    p.ph_sbuf("xc", [128, 2 * D], F32)
    p.ph_sbuf("tbfD", [128, D], BF16)
    p.ph_sbuf("sq_junk", [128, D], BF16)
    p.ph_sbuf("h2T", [128, DCH * T2], BF16)
    p.ph_sbuf("xxT", [128, DCH * SUPD], BF16)
    p.ph_sbuf("lorg", [32, 4 * SUPD], BF16)
    p.ph_sbuf("xmT", [128, DCH * SUPD], BF16)
    p.ph_sbuf("lsm", [16, SUPD], BF16)
    p.ph_sbuf("ldm", [64, SUPD], BF16)
    p.ph_sbuf("r_fm", [128, DCH * SUPD], BF16)
    p.ph_sbuf("k_fm", [128, DCH * SUPD], BF16)
    p.ph_sbuf("v_fm", [128, DCH * SUPD], BF16)
    p.ph_sbuf("kk_fm", [128, DCH * SUPD], BF16)
    p.ph_sbuf("sig", [128, DCH * SUPD], F32)
    p.ph_sbuf("spl", [128, DCH * SUPD], F32)
    p.ph_sbuf("ag_fm", [128, DCH * SUPD], BF16)
    p.ph_sbuf("ma_fm", [128, DCH * SUPD], BF16)
    p.ph_sbuf("mk_fm", [128, DCH * SUPD], BF16)
    p.ph_sbuf("cum", [128, DCH * T2], F32)
    p.ph_sbuf("cuml", [128, DCH], F32)
    p.ph_sbuf("dcs", [128, DCH * 8], F32)
    p.ph_sbuf("dcsr", [128, DCH * 8], F32)
    p.ph_sbuf("qt_fm", [128, DCH * SUPD], BF16)
    p.ph_sbuf("at_fm", [128, DCH * SUPD], BF16)
    p.ph_sbuf("bt_fm", [128, DCH * SUPD], BF16)
    p.ph_sbuf("bh_fm", [128, DCH * SUPD], BF16)
    p.ph_sbuf("kt_fm", [128, DCH * SUPD], BF16)
    p.ph_sbuf("kh_fm", [128, DCH * SUPD], BF16)
    p.ph_sbuf("tmp1", [128, SUPD], F32)
    p.ph_sbuf("tmp2", [128, SUPD], F32)
    p.ph_sbuf("tmp3", [128, SUPD], F32)
    p.ph_sbuf("tb1", [128, SUPD], BF16)
    p.ph_sbuf("tb2", [128, SUPD], BF16)
    p.ph_sbuf("gmid", [128, SUPD], BF16)
    p.ph_sbuf("g_fm", [128, DCH * SUPD], BF16)
    p.ph_sbuf("ksqb", [128, SUPD], BF16)
    p.ph_sbuf("kns", [16, 2 * SUPD], F32)
    p.ph_sbuf("knsb", [16, SUPD], BF16)
    p.ph_sbuf("vtm", [128, SUPD], BF16)
    p.ph_sbuf("bhtm", [128, SUPD], BF16)
    p.ph_sbuf("khtm", [128, SUPD], BF16)
    for nm in ("Gm", "GTm", "G2m", "G2Tm", "G4m", "G4Tm", "G8m", "G8Tm",
               "G16m", "G16Tm", "G32m", "Hkm"):
        p.ph_sbuf(nm, [128, SUPD], BF16)
    p.ph_sbuf("Abm", [128, SUPD], BF16)
    p.ph_sbuf("Akm", [128, SUPD], BF16)
    p.ph_sbuf("rhsb", [128, 128], BF16)
    p.ph_sbuf("saA", [128, 128], BF16)
    p.ph_sbuf("saB", [128, 128], BF16)
    p.ph_sbuf("zh_sb", [128, SUPD], BF16)
    p.ph_sbuf("yl_sb", [128, SUPD], F32)
    p.ph_sbuf("state", [128, 8 * 128], F32)
    p.ph_sbuf("stateb", [128, 8 * 128], BF16)

    p.ph_psum("ps_da", [128, 2048], F32)
    p.ph_psum("ps_db", [128, 1024], F32)
    p.ph_psum("ps_tp", [128, 2048], BF16)

    _barrier(p)
    sy = nc.sync
    for dch in range(DCH):
        ld.inc(sy.dma_start(out=b["maa1T"][:, dch * 128:(dch + 1) * 128],
                            in_=b["maa_w1T"][dch * 128:(dch + 1) * 128, :]), 16)
        ld.inc(sy.dma_start(out=b["gw1T"][:, dch * 128:(dch + 1) * 128],
                            in_=b["gate_w1T"][dch * 128:(dch + 1) * 128, :]), 16)
        ld.inc(sy.dma_start(out=b["dw1T"][:, dch * 64:(dch + 1) * 64],
                            in_=b["decay_w1T"][dch * 128:(dch + 1) * 128, :]), 16)
        for li in range(4):
            ld.inc(sy.dma_start(
                out=b["lw1T"][:, li * DCH * 16 + dch * 16:li * DCH * 16 + (dch + 1) * 16],
                in_=b["lw1T_in"][li * D + dch * 128:li * D + (dch + 1) * 128, :]), 16)
    ld.inc(sy.dma_start(out=b["maa2"], in_=b["maa_w2_in"]), 16)
    ld.inc(sy.dma_start(out=b["gw2"], in_=b["gate_w2_in"]), 16)
    ld.inc(sy.dma_start(out=b["dw2"], in_=b["decay_w2_in"]), 16)
    ld.inc(sy.dma_start(out=b["lw2"], in_=b["lw2_in"]), 16)
    ld.inc(sy.dma_start(out=b["pvecs"], in_=b["pvecs_in"]), 16)
    ld.inc(sy.dma_start(out=b["npvecs"], in_=b["npvecs_in"]), 16)
    ld.inc(sy.dma_start(out=b["tmaa"], in_=b["tmaa_in"]), 16)
    ld.inc(sy.dma_start(out=b["bo16"], in_=b["bo16_in"]), 16)
    ld.inc(sy.dma_start(out=b["sel16"], in_=b["sel16_in"]), 16)
    ld.inc(sy.dma_start(out=b["masks"], in_=b["masks_in"]), 16)
    ld.inc(sy.dma_start(out=b["eye64"], in_=b["eye64_in"]), 16)
    w_ld = ld.n

    dv = b["s_dve"]
    nc.vector.wait_ge(ld.sem, w_ld)
    for pp in range(8):
        dv.inc(nc.vector.memset(b["state"][:, pp * 128:pp * 128 + 64], 0.0))
        dv.inc(nc.vector.tensor_copy(b["state"][:, pp * 128 + 64:pp * 128 + 128],
                                     b["eye64"]))
        dv.inc(nc.vector.memset(b["stateb"][:, pp * 128:pp * 128 + 64], 0.0))
        dv.inc(nc.vector.tensor_copy(b["stateb"][:, pp * 128 + 64:pp * 128 + 128],
                                     b["eye64"]))
    dv.inc(nc.vector.memset(b["cuml"], 0.0))
    if c == 0:
        for dch in range(DCH):
            dv.inc(nc.vector.memset(b["h2T"][:, dch * T2:dch * T2 + 1], 0.0))
        bnd_dv, bnd_sc = dv.n, sc.n
    else:
        sx[0].inc(nc.sync.dma_start(out=b["xc"][:, 0:D],
                                    in_=b["scr_x2"][OV - 128:OV, :]), 16)
        nc.vector.wait_ge(sx[0].sem, sx[0].n)
        nc.scalar.wait_ge(sx[0].sem, sx[0].n)
        _ln_tile(p, b["xc"][:, 0:D], b["tbfD"])
        nc.tensor.wait_ge(sc.sem, sc.n)
        _transpose_tile(p, b["tbfD"],
                        lambda dch: b["xxT"][:, dch * SUPD:dch * SUPD + 128])
        nc.vector.wait_ge(sc.sem, sc.n)
        dv.sw(nc.vector)
        for dch in range(DCH):
            dv.inc(nc.vector.tensor_copy(b["h2T"][:, dch * T2:dch * T2 + 1],
                                         b["xxT"][:, dch * SUPD + 127:dch * SUPD + 128]))
        bnd_dv, bnd_sc = dv.n, sc.n

    nc.tensor.wait_ge(dv.sem, dv.n)
    import os as _os9
    if _os9.environ.get("D_NSUP"):
        n_sup = min(n_sup, int(_os9.environ["D_NSUP"]))
    for sup in range(n_sup):
        ts = sup * SUPD
        S = min(SUPD, n - ts)
        ntile = S // 128
        nch = S // CHUNK
        _D_dense(p, c, sup, ts, S, ntile, nch, T2)
        _D_chunks(p, c, sup, ts, S, nch, T2)
    gp = b["s_gp"]
    _W(nc.gpsimd, dv)
    gp.inc(nc.gpsimd.dma_start(out=b["cc_in"][:, :], in_=b["state"]), 16)
    if p.tap == "D":
        _tap_D(p, c, dbg, n)


def _W(eng, *cts):
    for ct in cts:
        if ct.n:
            eng.wait_ge(ct.sem, ct.n)


def _D_dense(p, c, sup, ts, S, ntile, nch, T2):
    nc, b = p.nc, p.b
    pe, dv, sc = b["s_pe"], b["s_dve"], b["s_sce"]
    ld = b["s_ld"]
    sx = [b["s_x0"], b["s_x1"]]
    sw0 = b["s_w0"]
    OWOFF = STARTS[c] - max(0, STARTS[c] - OV)
    MS = p.MAXSLICE
    ps_a, ps_b = b["ps_da"], b["ps_db"]
    V = nc.vector
    SCE = nc.scalar
    TEN = nc.tensor

    if sup > 0:
        _W(V, sc, pe)
        dv.sw(V)
        h2v = b["h2T"].rearrange("q (ch t) -> q ch t", t=T2)
        dv.inc(V.tensor_copy(h2v[:, :, 0:1], h2v[:, :, SUPD:SUPD + 1]))

    for ti in range(ntile):
        loc = OWOFF + ts + ti * 128
        par = ti % 2
        _W(nc.sync, dv, sc)
        sx[par].inc(nc.sync.dma_start(out=b["xc"][:, ti * D:(ti + 1) * D],
                                      in_=b["scr_x2"][loc:loc + 128, :]), 16)
        _W(V, sx[par]); _W(SCE, sx[par])
        _W(SCE, pe)
        _ln_tile(p, b["xc"][:, ti * D:(ti + 1) * D], b["tbfD"])
        _W(TEN, sc, dv)
        _transpose_tile(p, b["tbfD"],
                        lambda dch, _ti=ti: b["h2T"][:, dch * T2 + 1 + _ti * 128:
                                                     dch * T2 + 1 + (_ti + 1) * 128])
    h2_dv, h2_sc = dv.n, sc.n

    _W(V, sc)
    dv.sw(V)
    for dch in range(DCH):
        cur = b["h2T"][:, dch * T2 + 1:dch * T2 + 1 + S]
        prv = b["h2T"][:, dch * T2:dch * T2 + S]
        dv.inc(V.tensor_sub(b["xxT"][:, dch * SUPD:dch * SUPD + S], prv, cur))
        dv.sw(V)
        dv.inc(V.scalar_tensor_tensor(b["xmT"][:, dch * SUPD:dch * SUPD + S],
                                      b["xxT"][:, dch * SUPD:dch * SUPD + S],
                                      b["pvecs"][:, dch:dch + 1], cur,
                                      ALU.mult, ALU.add))
    mix_dv = dv.n

    _W(TEN, dv)
    for dch in range(DCH):
        mm = TEN.matmul(ps_a[:, 0:S], b["maa1T"][:, dch * 128:(dch + 1) * 128],
                        b["xmT"][:, dch * SUPD:dch * SUPD + S],
                        start=(dch == 0), stop=(dch == DCH - 1), skip_group_check=True)
    pe.inc(mm)
    _W(SCE, pe)
    for g in range(4):
        sc.inc(SCE.activation(b["lorg"][:, g * SUPD:g * SUPD + S],
                              ps_a[g * 32:(g + 1) * 32, 0:S], ACT.Tanh))
    lorg_sc = sc.n

    sr = b["s_r0"]

    def load_wproj(src_name):
        _W(nc.sync, pe)
        for dch in range(DCH):
            sr.inc(nc.sync.dma_start(out=b["wproj"][:, dch * D:(dch + 1) * D],
                                     in_=b[src_name][dch * 128:(dch + 1) * 128, :]), 16)
        _W(TEN, sr)

    def proj(dst_fm, wT, fch, extra=None):
        for dch in range(DCH):
            mm = TEN.matmul(ps_a[:, 512:512 + S],
                            wT[:, dch * D + fch * 128:dch * D + (fch + 1) * 128],
                            b["xmT"][:, dch * SUPD:dch * SUPD + S],
                            start=(dch == 0), stop=(dch == DCH - 1),
                            skip_group_check=True)
        pe.inc(mm)

    def lora_small(li, k_dim, midbuf, mid_act, w1T_all, w2, outbuf, out_act, bias_col):
        for dch in range(DCH):
            mm = TEN.matmul(ps_a[0:k_dim, 1024:1024 + S],
                            w1T_all(dch),
                            b["xmT"][:, dch * SUPD:dch * SUPD + S],
                            start=(dch == 0), stop=(dch == DCH - 1),
                            skip_group_check=True)
        pe.inc(mm)
        _W(SCE, pe, dv)
        if mid_act is None:
            sc.inc(SCE.copy(midbuf[0:k_dim, 0:S], ps_a[0:k_dim, 1024:1024 + S]))
        else:
            sc.inc(SCE.activation(midbuf[0:k_dim, 0:S], ps_a[0:k_dim, 1024:1024 + S],
                                  mid_act))
        _W(TEN, sc)
        for fch in range(DCH):
            mm = TEN.matmul(ps_a[:, 1536:1536 + S], w2(fch), midbuf[0:k_dim, 0:S],
                            start=True, stop=True, skip_group_check=True)
            pe.inc(mm)
            _W(SCE, pe)
            if out_act is not None:
                sc.inc(SCE.activation(outbuf(fch), ps_a[:, 1536:1536 + S], out_act,
                                      bias=bias_col(fch)))
            else:
                sc.inc(SCE.activation(outbuf(fch), ps_a[:, 1536:1536 + S], ACT.Identity,
                                      bias=bias_col(fch)))
            _W(TEN, sc)
        return sc.n

    for g in range(4):
        _W(TEN, dv, sc)
        for dch in range(DCH):
            mm = TEN.matmul(ps_a[:, 512 + (dch % 2) * SUPD:512 + (dch % 2) * SUPD + S]
                            if False else ps_a[:, 512:512 + S],
                            b["maa2"][:, g * D + dch * 128:g * D + (dch + 1) * 128],
                            b["lorg"][:, g * SUPD:g * SUPD + S],
                            start=True, stop=True, skip_group_check=True)
            pe.inc(mm)
            _W(V, pe)
            dv.sw(V)
            dv.inc(V.scalar_tensor_tensor(b["tmp1"][:, 0:S], ps_a[:, 512:512 + S],
                                          b["tmaa"][:, g * DCH + dch:g * DCH + dch + 1],
                                          b["xxT"][:, dch * SUPD:dch * SUPD + S],
                                          ALU.add, ALU.mult))
            dv.sw(V)
            dv.inc(V.tensor_add(b["xmT"][:, dch * SUPD:dch * SUPD + S],
                                b["tmp1"][:, 0:S],
                                b["h2T"][:, dch * T2 + 1:dch * T2 + 1 + S]))
            _W(TEN, dv)
        if g == 0:
            load_wproj("WrT")
            for fch in range(DCH):
                proj(None, b["wproj"], fch)
                _W(SCE, pe)
                sc.inc(SCE.copy(b["r_fm"][:, fch * SUPD:fch * SUPD + S], ps_a[:, 512:512 + S]))
                _W(TEN, sc)
            for dch in range(DCH):
                mm = TEN.matmul(ps_a[:, 1024:1024 + S],
                                b["gw1T"][:, dch * 128:(dch + 1) * 128],
                                b["xmT"][:, dch * SUPD:dch * SUPD + S],
                                start=(dch == 0), stop=(dch == DCH - 1),
                                skip_group_check=True)
            pe.inc(mm)
            _W(SCE, pe)
            sc.inc(SCE.activation(b["gmid"][:, 0:S], ps_a[:, 1024:1024 + S], ACT.Tanh))
            _W(TEN, sc)
            for fch in range(DCH):
                mm = TEN.matmul(ps_a[:, 1536:1536 + S],
                                b["gw2"][:, fch * 128:(fch + 1) * 128],
                                b["gmid"][:, 0:S],
                                start=True, stop=True, skip_group_check=True)
                pe.inc(mm)
                _W(SCE, pe)
                sc.inc(SCE.copy(b["g_fm"][:, fch * SUPD:fch * SUPD + S], ps_a[:, 1536:1536 + S]))
                _W(TEN, sc)
            _W(nc.gpsimd, sc)
            _W(nc.gpsimd, sw0)
            rv = b["r_fm"].rearrange("q (ch t) -> q ch t", t=SUPD)[:, :, 0:S]
            gv = b["g_fm"].rearrange("q (ch t) -> q ch t", t=SUPD)[:, :, 0:S]
            sw0.inc(nc.gpsimd.dma_start(
                out=b["scr_r"].rearrange("q (ch t) -> q ch t", t=MS)[:, :, ts:ts + S], in_=rv), 16)
            sw0.inc(nc.gpsimd.dma_start(
                out=b["scr_g"].rearrange("q (ch t) -> q ch t", t=MS)[:, :, ts:ts + S], in_=gv), 16)
        elif g == 1:
            for dch in range(DCH):
                mm = TEN.matmul(ps_a[0:64, 1024:1024 + S],
                                b["dw1T"][:, dch * 64:(dch + 1) * 64],
                                b["xmT"][:, dch * SUPD:dch * SUPD + S],
                                start=(dch == 0), stop=(dch == DCH - 1),
                                skip_group_check=True)
            pe.inc(mm)
            _W(SCE, pe)
            sc.inc(SCE.activation(b["ldm"][:, 0:S], ps_a[0:64, 1024:1024 + S], ACT.Tanh))
            _W(TEN, sc)
            for fch in range(DCH):
                mm = TEN.matmul(ps_a[:, 1536:1536 + S],
                                b["dw2"][:, fch * 128:(fch + 1) * 128],
                                b["ldm"][0:64, 0:S],
                                start=True, stop=True, skip_group_check=True)
                pe.inc(mm)
                _W(SCE, pe)
                sc.inc(SCE.activation(b["sig"][:, fch * SUPD:fch * SUPD + S],
                                      ps_a[:, 1536:1536 + S], ACT.Sigmoid,
                                      bias=b["pvecs"][:, 8 + fch:9 + fch]))
                sc.sw(SCE)
                sc.inc(SCE.activation(b["spl"][:, fch * SUPD:fch * SUPD + S],
                                      ps_a[:, 1536:1536 + S], ACT.Identity,
                                      bias=b["pvecs"][:, 8 + fch:9 + fch]))
                _W(TEN, sc)
            lora_small(0, 16, b["lsm"], None,
                       lambda dch: b["lw1T"][:, 0 * DCH * 16 + dch * 16:0 * DCH * 16 + (dch + 1) * 16],
                       lambda fch: b["lw2"][:, 0 * D + fch * 128:0 * D + (fch + 1) * 128],
                       lambda fch: b["ag_fm"][:, fch * SUPD:fch * SUPD + S], ACT.Sigmoid,
                       lambda fch: b["pvecs"][:, 16 + fch:17 + fch])
            lora_small(2, 16, b["lsm"], None,
                       lambda dch: b["lw1T"][:, 2 * DCH * 16 + dch * 16:2 * DCH * 16 + (dch + 1) * 16],
                       lambda fch: b["lw2"][:, 2 * D + fch * 128:2 * D + (fch + 1) * 128],
                       lambda fch: b["ma_fm"][:, fch * SUPD:fch * SUPD + S], ACT.Sigmoid,
                       lambda fch: b["pvecs"][:, 24 + fch:25 + fch])
        elif g == 2:
            load_wproj("WkT")
            for fch in range(DCH):
                proj(None, b["wproj"], fch)
                _W(SCE, pe)
                sc.inc(SCE.copy(b["k_fm"][:, fch * SUPD:fch * SUPD + S], ps_a[:, 512:512 + S]))
                _W(TEN, sc)
            for dch in range(DCH):
                mm = TEN.matmul(ps_a[0:16, 1024:1024 + S],
                                b["lw1T"][:, 1 * DCH * 16 + dch * 16:1 * DCH * 16 + (dch + 1) * 16],
                                b["xmT"][:, dch * SUPD:dch * SUPD + S],
                                start=(dch == 0), stop=(dch == DCH - 1),
                                skip_group_check=True)
            pe.inc(mm)
            _W(SCE, pe)
            sc.inc(SCE.activation(b["lsm"][0:16, 0:S], ps_a[0:16, 1024:1024 + S], ACT.Tanh))
            _W(TEN, sc)
            for fch in range(DCH):
                mm = TEN.matmul(ps_a[:, 1536:1536 + S],
                                b["lw2"][:, 1 * D + fch * 128:1 * D + (fch + 1) * 128],
                                b["lsm"][0:16, 0:S],
                                start=True, stop=True, skip_group_check=True)
                pe.inc(mm)
                _W(V, pe, sc)
                dv.sw(V)
                dv.inc(V.scalar_tensor_tensor(b["kk_fm"][:, fch * SUPD:fch * SUPD + S],
                                              ps_a[:, 1536:1536 + S], 1.0,
                                              b["k_fm"][:, fch * SUPD:fch * SUPD + S],
                                              ALU.mult, ALU.add))
                _W(TEN, dv)
            lora_small(3, 16, b["lsm"], None,
                       lambda dch: b["lw1T"][:, 3 * DCH * 16 + dch * 16:3 * DCH * 16 + (dch + 1) * 16],
                       lambda fch: b["lw2"][:, 3 * D + fch * 128:3 * D + (fch + 1) * 128],
                       lambda fch: b["mk_fm"][:, fch * SUPD:fch * SUPD + S], ACT.Sigmoid,
                       lambda fch: b["pvecs"][:, 32 + fch:33 + fch])
        else:
            load_wproj("WvT")
            for fch in range(DCH):
                proj(None, b["wproj"], fch)
                _W(SCE, pe)
                sc.inc(SCE.copy(b["v_fm"][:, fch * SUPD:fch * SUPD + S], ps_a[:, 512:512 + S]))
                _W(TEN, sc)
            _W(nc.gpsimd, sc, sw0)
            vv = b["v_fm"].rearrange("q (ch t) -> q ch t", t=SUPD)[:, :, 0:S]
            sw0.inc(nc.gpsimd.dma_start(
                out=b["scr_v1"].rearrange("q (ch t) -> q ch t", t=MS)[:, :, ts:ts + S], in_=vv), 16)

    _W(SCE, dv)
    for pp in range(DCH):
        _W(SCE, pe) if pp else None
        sc.inc(SCE.activation(b["ksqb"][:, 0:S], b["kk_fm"][:, pp * SUPD:pp * SUPD + S],
                              ACT.Square))
        _W(TEN, sc)
        mm = TEN.matmul(ps_b[0:16, 0:S], b["bo16"][:, pp * 16:(pp + 1) * 16],
                        b["ksqb"][:, 0:S],
                        start=(pp == 0), stop=(pp == DCH - 1), skip_group_check=True)
        pe.inc(mm)
    _W(V, pe)
    dv.inc(V.tensor_scalar_max(b["kns"][0:16, 0:S], ps_b[0:16, 0:S], 1e-24))
    _W(SCE, dv)
    sc.inc(SCE.activation(b["kns"][0:16, SUPD:SUPD + S], b["kns"][0:16, 0:S], ACT.Ln))
    sc.sw(SCE)
    sc.inc(SCE.activation(b["knsb"][0:16, 0:S], b["kns"][0:16, SUPD:SUPD + S], ACT.Exp,
                          scale=-0.5))
    rn_sc = sc.n
    for pp in range(DCH):
        _W(TEN, sc)
        if pp:
            _W(TEN, dv)
        mm = TEN.matmul(ps_b[:, 512:512 + S], b["sel16"][:, pp * 128:(pp + 1) * 128],
                        b["knsb"][0:16, 0:S],
                        start=True, stop=True, skip_group_check=True)
        pe.inc(mm)
        _W(V, pe)
        dv.sw(V)
        dv.inc(V.tensor_mul(b["kk_fm"][:, pp * SUPD:pp * SUPD + S],
                            b["kk_fm"][:, pp * SUPD:pp * SUPD + S], ps_b[:, 512:512 + S]))
    kk_dv = dv.n

    _W(V, sc)
    dv.sw(V)
    cumv = b["cum"].rearrange("q (ch t) -> q ch t", t=T2)
    dv.inc(V.tensor_copy(cumv[:, :, 0:1],
                         b["cuml"].rearrange("q ch -> q ch ()")))
    dv.sw(V)
    for dch in range(DCH):
        dv.inc(V.tensor_tensor_scan(b["cum"][:, dch * T2 + 1:dch * T2 + 1 + S],
                                    b["sig"][:, dch * SUPD:dch * SUPD + S],
                                    b["sig"][:, dch * SUPD:dch * SUPD + S],
                                    b["cum"][:, dch * T2:dch * T2 + 1],
                                    ALU.add, ALU.bypass))
        dv.sw(V)
    dv.inc(V.tensor_copy(b["cuml"].rearrange("q ch -> q ch ()"),
                         cumv[:, :, S:S + 1]))
    dv.sw(V)
    for dch in range(DCH):
        ends = b["cum"][:, dch * T2 + 64: dch * T2 + 64 + 64 * (nch - 1) + 1: 64]
        starts = b["cum"][:, dch * T2: dch * T2 + 64 * (nch - 1) + 1: 64]
        dv.inc(V.tensor_sub(b["dcsr"][:, dch * 8:dch * 8 + nch], ends, starts))
    _W(SCE, dv)
    dview = b["dcsr"].rearrange("q (ch k) -> q ch k", k=8)[:, :, 0:nch]
    dview_o = b["dcs"].rearrange("q (ch k) -> q ch k", k=8)[:, :, 0:nch]
    sc.inc(SCE.activation(dview_o, dview, ACT.Exp, scale=-BETA))
    dcs_sc = sc.n

    for dch in range(DCH):
        sl = slice(dch * SUPD, dch * SUPD + S)
        cur = b["cum"][:, dch * T2 + 1:dch * T2 + 1 + S].rearrange(
            "q (ci k) -> q ci k", k=64)
        prv = b["cum"][:, dch * T2:dch * T2 + S].rearrange("q (ci k) -> q ci k", k=64)
        starts = b["cum"][:, dch * T2: dch * T2 + 64 * (nch - 1) + 1: 64].rearrange(
            "q ci -> q ci ()").broadcast_to((128, nch, 64))
        ends = b["cum"][:, dch * T2 + 64: dch * T2 + 64 + 64 * (nch - 1) + 1: 64].rearrange(
            "q ci -> q ci ()").broadcast_to((128, nch, 64))
        t1v = b["tmp1"][:, 0:S].rearrange("q (ci k) -> q ci k", k=64)
        _W(V, sc, pe)
        dv.sw(V)
        dv.inc(V.tensor_mul(b["tb1"][:, 0:S], b["ma_fm"][:, sl], b["ag_fm"][:, sl]))
        dv.sw(V)
        dv.inc(V.scalar_tensor_tensor(b["tb2"][:, 0:S], b["tb1"][:, 0:S], -1.0,
                                      b["ag_fm"][:, sl], ALU.mult, ALU.add))
        dv.sw(V)
        dv.inc(V.tensor_add(b["tb1"][:, 0:S], b["tb2"][:, 0:S], b["ma_fm"][:, sl]))
        _W(V, sc)
        dv.sw(V)
        dv.inc(V.scalar_tensor_tensor(b["tmp2"][:, 0:S], b["spl"][:, sl], -1.0,
                                      b["spl"][:, sl], ALU.mult, ALU.min))
        _W(SCE, dv)
        sc.inc(SCE.activation(b["tmp2"][:, 0:S], b["tmp2"][:, 0:S], ACT.Exp))
        sc.sw(SCE)
        sc.inc(SCE.activation(b["tmp2"][:, 0:S], b["tmp2"][:, 0:S], ACT.Ln, bias=1.0))
        _W(V, sc)
        dv.sw(V)
        dv.inc(V.tensor_scalar_min(b["tmp3"][:, 0:S], b["spl"][:, sl], 0.0))
        dv.sw(V)
        dv.inc(V.tensor_sub(b["tmp2"][:, 0:S], b["tmp3"][:, 0:S], b["tmp2"][:, 0:S]))
        dv.sw(V)
        dv.inc(V.scalar_tensor_tensor(b["tmp2"][:, 0:S], b["tmp2"][:, 0:S], -0.5,
                                      b["mk_fm"][:, sl], ALU.add, ALU.mult))
        _W(SCE, dv)
        sc.inc(SCE.activation(b["tmp3"][:, 0:S], b["tmp2"][:, 0:S], ACT.Exp))
        _W(V, sc)
        dv.sw(V)
        dv.inc(V.tensor_mul(b["k_fm"][:, sl], b["k_fm"][:, sl], b["tb1"][:, 0:S]))
        dv.sw(V)
        dv.inc(V.tensor_mul(b["k_fm"][:, sl], b["k_fm"][:, sl], b["tmp3"][:, 0:S]))
        kflast = dv.n
        dv.sw(V)
        dv.inc(V.tensor_sub(t1v, cur, starts))
        _W(SCE, dv)
        sc.inc(SCE.activation(b["tmp2"][:, 0:S], b["tmp1"][:, 0:S], ACT.Exp, scale=-BETA))
        _W(V, sc)
        dv.sw(V)
        dv.inc(V.tensor_mul(b["qt_fm"][:, sl], b["r_fm"][:, sl], b["tmp2"][:, 0:S]))
        _W(SCE, dv)
        sc.inc(SCE.activation(b["tmp2"][:, 0:S], b["tmp1"][:, 0:S], ACT.Exp, scale=BETA))
        _W(V, sc)
        dv.sw(V)
        dv.inc(V.tensor_mul(b["kt_fm"][:, sl], b["k_fm"][:, sl], b["tmp2"][:, 0:S]))
        dv.inc(V.tensor_mul(b["bt_fm"][:, sl], b["kk_fm"][:, sl], b["tmp2"][:, 0:S]))
        dv.sw(V)
        dv.inc(V.tensor_mul(b["bt_fm"][:, sl], b["bt_fm"][:, sl], b["ag_fm"][:, sl]))
        dv.inc(V.tensor_sub(t1v, ends, cur))
        _W(SCE, dv)
        sc.inc(SCE.activation(b["tmp2"][:, 0:S], b["tmp1"][:, 0:S], ACT.Exp, scale=-BETA))
        _W(V, sc)
        dv.sw(V)
        dv.inc(V.tensor_mul(b["kh_fm"][:, sl], b["k_fm"][:, sl], b["tmp2"][:, 0:S]))
        dv.inc(V.tensor_mul(b["bh_fm"][:, sl], b["kk_fm"][:, sl], b["tmp2"][:, 0:S]))
        dv.sw(V)
        dv.inc(V.tensor_mul(b["bh_fm"][:, sl], b["bh_fm"][:, sl], b["ag_fm"][:, sl]))
        dv.inc(V.tensor_sub(t1v, prv, starts))
        _W(SCE, dv)
        sc.inc(SCE.activation(b["tmp2"][:, 0:S], b["tmp1"][:, 0:S], ACT.Exp, scale=-BETA))
        _W(V, sc)
        dv.sw(V)
        dv.inc(V.tensor_mul(b["at_fm"][:, sl], b["kk_fm"][:, sl], b["tmp2"][:, 0:S]))
    _W(nc.gpsimd, dv, sw0)
    kv = b["k_fm"].rearrange("q (ch t) -> q ch t", t=SUPD)[:, :, 0:S]
    sw0.inc(nc.gpsimd.dma_start(
        out=b["scr_k"].rearrange("q (ch t) -> q ch t", t=MS)[:, :, ts:ts + S], in_=kv), 16)


def _D_chunks(p, c, sup, ts, S, nch, T2):
    nc, b = p.nc, p.b
    pe, dv, sc = b["s_pe"], b["s_dve"], b["s_sce"]
    sw0 = b["s_w0"]
    MS = p.MAXSLICE
    ps_a, ps_b, ps_tp = b["ps_da"], b["ps_db"], b["ps_tp"]
    V, SCE, TEN = nc.vector, nc.scalar, nc.tensor
    msk = b["masks"]
    m_su, m_sl, m_hk, m_tb, m_tk = (msk[:, i * SUPD:(i + 1) * SUPD] for i in range(5))
    W = S

    def pairmm(out_ap_fn, lhs_fn, rhs_fn, start, stop, n=None):
        for hh in range(2):
            r0 = hh * 64
            mm = TEN.matmul(out_ap_fn(hh)[r0:r0 + 64, :], lhs_fn(hh), rhs_fn(hh),
                            start=start, stop=stop,
                            skip_group_check=True, tile_position=(r0, r0))
        pe.inc(mm)

    import os as _os4
    _np_ = int(_os4.environ.get("D_NPAIR", "8"))
    _sect = _os4.environ.get("D_SECT", "z")
    for pp in range(min(DCH, _np_)):
        base = pp * SUPD
        st_sl = slice(pp * 128, (pp + 1) * 128)
        _W(TEN, dv, sc)
        for ci in range(nch):
            for hh in range(2):
                r0 = hh * 64
                ident = b["idt"][r0:r0 + 64, r0:r0 + 64]
                TEN.matmul(ps_a[r0:r0 + 64, ci * 64:(ci + 1) * 64],
                           b["v_fm"][r0:r0 + 64, base + ci * 64:base + (ci + 1) * 64],
                           ident, start=True, stop=True, skip_group_check=True,
                           tile_position=(r0, r0))
                TEN.matmul(ps_a[r0:r0 + 64, 512 + ci * 64:512 + (ci + 1) * 64],
                           b["bh_fm"][r0:r0 + 64, base + ci * 64:base + (ci + 1) * 64],
                           ident, start=True, stop=True, skip_group_check=True,
                           tile_position=(r0, r0))
                mm = TEN.matmul(ps_a[r0:r0 + 64, 256 + ci * 64:256 + (ci + 1) * 64],
                                b["kh_fm"][r0:r0 + 64, base + ci * 64:base + (ci + 1) * 64],
                                ident, start=True, stop=True, skip_group_check=True,
                                tile_position=(r0, r0))
        pe.inc(mm)
        _W(SCE, pe)
        _W(V, pe)
        sc.inc(SCE.copy(b["vtm"][:, 0:W], ps_a[:, 0:W]))
        sc.sw(SCE)
        sc.inc(SCE.copy(b["khtm"][:, 0:W], ps_a[:, 256:256 + W]))
        dv.inc(V.tensor_scalar_mul(b["bhtm"][:, 0:W], ps_a[:, 512:512 + W], -1.0))
        tp_dv, tp_sc = dv.n, sc.n
        _W(TEN, dv, sc)
        only_g = bool(_os4.environ.get("D_ONLYG"))
        for ci in range(nch):
            cs = slice(base + ci * 64, base + (ci + 1) * 64)
            ps_ci = slice(ci * 64, (ci + 1) * 64)
            if only_g:
                pairmm(lambda hh: ps_a[:, 512 + ci * 64:512 + (ci + 1) * 64],
                       lambda hh: b["bt_fm"][hh * 64:hh * 64 + 64, cs],
                       lambda hh: b["at_fm"][hh * 64:hh * 64 + 64, cs],
                       True, True)
                continue
            pairmm(lambda hh: ps_a[:, 0:2048][:, ps_ci],
                   lambda hh: b["at_fm"][hh * 64:hh * 64 + 64, cs],
                   lambda hh: b["bt_fm"][hh * 64:hh * 64 + 64, cs],
                   True, True)
            pairmm(lambda hh: ps_a[:, 512 + ci * 64:512 + (ci + 1) * 64],
                   lambda hh: b["bt_fm"][hh * 64:hh * 64 + 64, cs],
                   lambda hh: b["at_fm"][hh * 64:hh * 64 + 64, cs],
                   True, True)
            pairmm(lambda hh: ps_a[:, 1024 + ci * 64:1024 + (ci + 1) * 64],
                   lambda hh: b["kt_fm"][hh * 64:hh * 64 + 64, cs],
                   lambda hh: b["at_fm"][hh * 64:hh * 64 + 64, cs],
                   True, True)
            pairmm(lambda hh: ps_a[:, 1536 + ci * 64:1536 + (ci + 1) * 64],
                   lambda hh: b["bt_fm"][hh * 64:hh * 64 + 64, cs],
                   lambda hh: b["qt_fm"][hh * 64:hh * 64 + 64, cs],
                   True, True)
            pairmm(lambda hh: ps_b[:, ci * 64:(ci + 1) * 64],
                   lambda hh: b["kt_fm"][hh * 64:hh * 64 + 64, cs],
                   lambda hh: b["qt_fm"][hh * 64:hh * 64 + 64, cs],
                   True, True)
        _W(V, pe)
        dv.sw(V)
        dv.inc(V.tensor_mul(b["GTm"][:, 0:W], ps_a[:, 0:W], m_sl[:, 0:W]))
        dv.inc(V.tensor_mul(b["Gm"][:, 0:W], ps_a[:, 512:512 + W], m_su[:, 0:W]))
        dv.inc(V.tensor_mul(b["Hkm"][:, 0:W], ps_a[:, 1024:1024 + W], m_hk[:, 0:W]))
        dv.inc(V.tensor_mul(b["Abm"][:, 0:W], ps_a[:, 1536:1536 + W], m_tb[:, 0:W]))
        dv.inc(V.tensor_mul(b["Akm"][:, 0:W], ps_b[:, 0:W], m_tk[:, 0:W]))
        gh_dv = dv.n
        if _sect == "b":
            continue
        chain = [("Gm", "GTm", "G2m", "G2Tm"), ("G2m", "G2Tm", "G4m", "G4Tm"),
                 ("G4m", "G4Tm", "G8m", "G8Tm"), ("G8m", "G8Tm", "G16m", "G16Tm"),
                 ("G16m", "G16Tm", "G32m", None)]
        for (srcn, srcTn, dstn, dstTn) in chain:
            _W(TEN, dv, sc)
            for ci in range(nch):
                ps_ci = slice(ci * 64, (ci + 1) * 64)
                pairmm(lambda hh: ps_a[:, 0:512][:, ps_ci],
                       lambda hh: b[srcTn][hh * 64:hh * 64 + 64, ci * 64:(ci + 1) * 64],
                       lambda hh: b[srcn][hh * 64:hh * 64 + 64, ci * 64:(ci + 1) * 64],
                       True, True)
                if dstTn is not None:
                    pairmm(lambda hh: ps_a[:, 512:1024][:, ps_ci],
                           lambda hh: b[srcn][hh * 64:hh * 64 + 64, ci * 64:(ci + 1) * 64],
                           lambda hh: b[srcTn][hh * 64:hh * 64 + 64, ci * 64:(ci + 1) * 64],
                           True, True)
            _W(V, pe)
            _W(SCE, pe)
            dv.sw(V)
            dv.inc(V.tensor_copy(b[dstn][:, 0:W], ps_a[:, 0:W]))
            if dstTn is not None:
                sc.inc(SCE.copy(b[dstTn][:, 0:W], ps_a[:, 512:512 + W]))
        sq_dv, sq_sc = dv.n, sc.n
        if _sect == "c":
            continue
        levels = ["Gm", "G2m", "G4m", "G8m", "G16m", "G32m"]
        for ci in range(nch):
            cs = slice(base + ci * 64, base + (ci + 1) * 64)
            ccols = slice(ci * 64, (ci + 1) * 64)
            _W(TEN, dv, sc)
            pairmm(lambda hh: ps_b[hh * 64:hh * 64 + 64, 0:128] if False else ps_b[:, 0:128],
                   lambda hh: b["at_fm"][hh * 64:hh * 64 + 64, cs],
                   lambda hh: b["stateb"][hh * 64:hh * 64 + 64, st_sl],
                   True, False)
            pairmm(lambda hh: ps_b[:, 0:64],
                   lambda hh: b["Hkm"][hh * 64:hh * 64 + 64, ccols],
                   lambda hh: b["vtm"][hh * 64:hh * 64 + 64, ccols],
                   False, True)
            _W(V, pe)
            dv.sw(V)
            dv.inc(V.tensor_copy(b["rhsb"], ps_b[:, 0:128]))
            cur = "rhsb"
            for lv in range(6):
                dst = "saA" if lv % 2 == 0 else "saB"
                slot = 128 + (lv % 2) * 128
                _W(TEN, dv)
                pairmm(lambda hh: ps_b[:, slot:slot + 128],
                       lambda hh: b["idt"][hh * 64:hh * 64 + 64, hh * 64:hh * 64 + 64],
                       lambda hh: b[cur][hh * 64:hh * 64 + 64, 0:128],
                       True, False)
                pairmm(lambda hh: ps_b[:, slot:slot + 128],
                       lambda hh: b[levels[lv]][hh * 64:hh * 64 + 64, ccols],
                       lambda hh: b[cur][hh * 64:hh * 64 + 64, 0:128],
                       False, True)
                _W(V, pe)
                dv.sw(V)
                dv.inc(V.tensor_copy(b[dst], ps_b[:, slot:slot + 128]))
                cur = dst
            _W(TEN, dv)
            pairmm(lambda hh: ps_a[:, 1536 + ci * 64:1536 + (ci + 1) * 64],
                   lambda hh: b["stateb"][hh * 64:hh * 64 + 64, pp * 128 + 64:pp * 128 + 128],
                   lambda hh: b["qt_fm"][hh * 64:hh * 64 + 64, cs],
                   ci == 0, False)
            pairmm(lambda hh: ps_a[:, 1536 + ci * 64:1536 + (ci + 1) * 64],
                   lambda hh: b[cur][hh * 64:hh * 64 + 64, 64:128],
                   lambda hh: b["Abm"][hh * 64:hh * 64 + 64, ccols],
                   False, True)
            pairmm(lambda hh: ps_b[:, 512 + ci * 64:512 + (ci + 1) * 64],
                   lambda hh: b["stateb"][hh * 64:hh * 64 + 64, pp * 128:pp * 128 + 64],
                   lambda hh: b["qt_fm"][hh * 64:hh * 64 + 64, cs],
                   ci == 0, False)
            pairmm(lambda hh: ps_b[:, 512 + ci * 64:512 + (ci + 1) * 64],
                   lambda hh: b[cur][hh * 64:hh * 64 + 64, 0:64],
                   lambda hh: b["Abm"][hh * 64:hh * 64 + 64, ccols],
                   False, False)
            pairmm(lambda hh: ps_b[:, 512 + ci * 64:512 + (ci + 1) * 64],
                   lambda hh: b["vtm"][hh * 64:hh * 64 + 64, ccols],
                   lambda hh: b["Akm"][hh * 64:hh * 64 + 64, ccols],
                   False, True)
            pairmm(lambda hh: ps_b[:, 128:256],
                   lambda hh: b["bhtm"][hh * 64:hh * 64 + 64, ccols],
                   lambda hh: b[cur][hh * 64:hh * 64 + 64, 0:128],
                   True, False)
            pairmm(lambda hh: ps_b[:, 128:192],
                   lambda hh: b["khtm"][hh * 64:hh * 64 + 64, ccols],
                   lambda hh: b["vtm"][hh * 64:hh * 64 + 64, ccols],
                   False, True)
            _W(V, pe)
            dv.sw(V)
            dv.inc(V.scalar_tensor_tensor(b["state"][:, st_sl], b["state"][:, st_sl],
                                          b["dcs"][:, pp * 8 + ci:pp * 8 + ci + 1],
                                          ps_b[:, 128:256], ALU.mult, ALU.add))
            dv.sw(V)
            dv.inc(V.tensor_copy(b["stateb"][:, st_sl], b["state"][:, st_sl]))
        _W(V, pe)
        _W(V, sw0)
        dv.sw(V)
        dv.inc(V.tensor_copy(b["yl_sb"][:, 0:W], ps_b[:, 512:512 + W]))
        dv.inc(V.tensor_copy(b["zh_sb"][:, 0:W], ps_a[:, 1536:1536 + W]))
        _W(nc.gpsimd, dv)
        sw0.inc(nc.gpsimd.dma_start(
            out=b["scr_y"][:, pp * MS + ts:pp * MS + ts + W], in_=b["yl_sb"][:, 0:W]), 16)
        sw0.inc(nc.gpsimd.dma_start(
            out=b["scr_zh"][:, pp * MS + ts:pp * MS + ts + W], in_=b["zh_sb"][:, 0:W]), 16)


def _tap_D(p, c, dbg, n):
    nc, b = p.nc, p.b
    ld = b["s_ld"]
    _barrier(p, ("sync", "gpsimd"))
    MS = p.MAXSLICE
    ld.inc(nc.sync.dma_start(out=dbg[0:128, 0:1024], in_=b["state"]), 16)
    b["s_gp"].inc(nc.gpsimd.dma_start(out=b["dbg2"][0:128, 0:2 * p.MAXTOT],
                                      in_=b["scr_y"][:, 0:2 * p.MAXTOT]), 16)
    nc.sync.wait_ge(ld.sem, ld.n)
    nc.gpsimd.wait_ge(b["s_gp"].sem, b["s_gp"].n)


def _phase_EF(p, c, astart, pend, dbg, y_out):
    nc, b = p.nc, p.b
    pe, dv, sc = b["s_pe"], b["s_dve"], b["s_sce"]
    ld = b["s_ld"]
    sx = [b["s_x0"], b["s_x1"]]
    sr, sr1, sw0, sw1 = b["s_r0"], b["s_r1"], b["s_w0"], b["s_w1"]
    V, SCE, TEN = nc.vector, nc.scalar, nc.tensor
    start = STARTS[c]
    n = pend - start
    OWOFF = start - astart
    MS = p.MAXSLICE
    n_sup = ceil_div(n, SUPD)

    p.ph_sbuf("woT", [128, DCH * D], BF16)
    p.ph_sbuf("w1r", [128, DCH * 4096], BF16)
    p.ph_sbuf("w2s", [128, 2 * D], BF16)
    p.ph_sbuf("b2r", [1, D], BF16)
    p.ph_sbuf("lnwb", [128, 3 * DCH], F32)
    p.ph_sbuf("b1c", [128, 32], F32)
    p.ph_sbuf("relst", [128, 8 * 128], F32)
    p.ph_sbuf("relb", [128, 8 * 64], BF16)
    p.ph_sbuf("seg", [128, 1024], F32)
    p.ph_sbuf("yf", [128, DCH * SUPD], F32)
    p.ph_sbuf("yb", [128, DCH * SUPD], BF16)
    p.ph_sbuf("zr", [128, DCH * SUPD], BF16)
    p.ph_sbuf("rr", [128, DCH * SUPD], BF16)
    p.ph_sbuf("kr", [128, DCH * SUPD], BF16)
    p.ph_sbuf("vr", [128, DCH * SUPD], BF16)
    p.ph_sbuf("gr", [128, DCH * SUPD], BF16)
    p.ph_sbuf("rk2", [128, SUPD], BF16)
    p.ph_sbuf("gn16", [16, 6 * SUPD], F32)
    p.ph_sbuf("gn16b", [16, SUPD], BF16)
    p.ph_sbuf("mean16b", [16, SUPD], BF16)
    p.ph_sbuf("ygf", [128, DCH * SUPD], BF16)
    p.ph_sbuf("xc2", [128, 2 * D], F32)
    p.ph_sbuf("x3t", [128, 2 * D], F32)
    p.ph_sbuf("tbfE", [128, D], BF16)
    p.ph_sbuf("sq_junk", [128, D], BF16)
    p.ph_sbuf("h3T", [128, DCH * SUPD], BF16)
    p.ph_sbuf("hact", [128, 32 * SUPD], BF16)
    p.ph_sbuf("kb0", [128, SUPD], BF16)
    p.ph_sbuf("kb1", [128, SUPD], BF16)
    p.ph_sbuf("nb1", [128, SUPD], F32)
    p.ph_sbuf("nb2", [128, SUPD], F32)
    p.ph_sbuf("zb", [128, SUPD], F32)
    p.ph_sbuf("pwa", [128, SUPD], F32)
    p.ph_sbuf("pwb", [128, SUPD], F32)
    p.ph_sbuf("ob1", [128, D], F32)
    p.ph_sbuf("ob2", [128, D], F32)
    p.ph_sbuf("x4t", [128, D], F32)
    p.ph_psum("ps_e", [128, 1024], F32)
    p.ph_psum("ps_f", [128, 2048], F32)
    p.ph_psum("ps_tp", [128, 2048], BF16)
    cin = b["cc_in"]
    cout = b["cc_out"]

    _barrier(p)
    for dch in range(DCH):
        ld.inc(nc.sync.dma_start(out=b["woT"][:, dch * D:(dch + 1) * D],
                                 in_=b["WoT"][dch * 128:(dch + 1) * 128, :]), 16)
        ld.inc(nc.sync.dma_start(out=b["w1r"][:, dch * 4096:(dch + 1) * 4096],
                                 in_=b["w1_in"][dch * 128:(dch + 1) * 128, :]), 16)
    ld.inc(nc.sync.dma_start(out=b["b2r"], in_=b["b2_in"]), 16)
    ld.inc(nc.sync.dma_start(out=b["lnwb"], in_=b["lnwb_in"]), 16)
    ld.inc(nc.sync.dma_start(out=b["b1c"], in_=b["b1_in"]), 16)
    w_ld = ld.n

    _W(TEN, dv)

    gp = b["s_gp"]
    nc.gpsimd.wait_ge(gp.sem, gp.n)
    cc = nc.gpsimd.collective_compute(
        "AllGather", ALU.bypass, replica_groups=[list(range(N_CORES))],
        ins=[cin.opt()], outs=[cout.opt()])
    scc = b["s_cc"]
    scc.inc(cc, 1)
    nc.gpsimd.wait_ge(scc.sem, scc.n)
    nc.sync.wait_ge(scc.sem, scc.n)

    if c == 0:
        dv.inc(V.memset(b["relst"], 0.0))
        rel_dv = dv.n
    else:
        for k in range(c):
            par = k % 2
            if k > 0:
                _W(nc.sync, dv)
            sr.inc(nc.sync.dma_start(out=b["seg"], in_=cout[k * 128:(k + 1) * 128, :]), 16)
            _W(V, sr); _W(SCE, sr)
            if k == 0:
                dv.inc(V.tensor_copy(b["relst"], b["seg"].rearrange(
                    "q (pp t) -> q pp t", t=128)[:, :, 0:64].rearrange(
                    "q pp t -> q (pp t)") if False else b["seg"]))
                rel_dv = dv.n
                continue
            _W(V, sr)
            dv.sw(V)
            dv.inc(V.tensor_copy(b["zr"][:, 0:1024], b["seg"]))
            _W(TEN, dv, sc)
            for pp in range(DCH):
                for hh in range(2):
                    r0 = hh * 64
                    mm = TEN.matmul(b["ps_f"][r0:r0 + 64, pp * 64:(pp + 1) * 64],
                                    b["zr"][r0:r0 + 64, pp * 128 + 64:pp * 128 + 128],
                                    b["idt"][r0:r0 + 64, r0:r0 + 64],
                                    start=True, stop=True, skip_group_check=True,
                                    tile_position=(r0, r0))
            pe.inc(mm)
            _W(V, pe)
            dv.sw(V)
            dv.inc(V.tensor_copy(b["relb"], b["ps_f"][:, 0:512]))
            dv.sw(V)
            dv.inc(V.tensor_copy(b["yb"][:, 0:1024], b["relst"]))
            _W(TEN, dv)
            for pp in range(DCH):
                for hh in range(2):
                    r0 = hh * 64
                    TEN.matmul(b["ps_e"][r0:r0 + 64, pp * 128:pp * 128 + 64],
                               b["relb"][r0:r0 + 64, pp * 64:(pp + 1) * 64],
                               b["yb"][r0:r0 + 64, pp * 128:pp * 128 + 64],
                               start=True, stop=False, skip_group_check=True,
                               tile_position=(r0, r0))
                    mm = TEN.matmul(b["ps_e"][r0:r0 + 64, pp * 128:pp * 128 + 64],
                                    b["idt"][r0:r0 + 64, r0:r0 + 64],
                                    b["zr"][r0:r0 + 64, pp * 128:pp * 128 + 64],
                                    start=False, stop=True, skip_group_check=True,
                                    tile_position=(r0, r0))
            pe.inc(mm)
            _W(V, pe)
            dv.sw(V)
            for pp in range(DCH):
                dv.inc(V.tensor_copy(b["relst"][:, pp * 128:pp * 128 + 64],
                                     b["ps_e"][:, pp * 128:pp * 128 + 64]))
            rel_dv = dv.n
    _W(V, dv) if False else None
    dv.sw(V)
    for pp in range(DCH):
        dv.inc(V.tensor_copy(b["relb"][:, pp * 64:(pp + 1) * 64],
                             b["relst"][:, pp * 128:pp * 128 + 64]))
    relb_dv = dv.n
    _W(TEN, dv)
    TEN.wait_ge(ld.sem, w_ld)
    V.wait_ge(ld.sem, w_ld)
    SCE.wait_ge(ld.sem, w_ld)

    prev = dict(sup_dv=0)
    for sup in range(n_sup):
        ts = sup * SUPD
        W = min(SUPD, n - ts)
        ntile = W // 128
        _W(nc.sync, dv)
        _W(nc.sync, pe)
        for (buf, scr, ctr) in (("yf", "scr_y", sr), ("zr", "scr_zh", sr1),
                                ("rr", "scr_r", sr), ("kr", "scr_k", sr1),
                                ("vr", "scr_v1", sr), ("gr", "scr_g", sr1)):
            ctr.inc(nc.sync.dma_start(
                out=b[buf].rearrange("q (ch t) -> q ch t", t=SUPD)[:, :, 0:W],
                in_=b[scr].rearrange("q (ch t) -> q ch t", t=MS)[:, :, ts:ts + W]), 16)
        _W(TEN, sr, sr1)
        _W(V, sr, sr1)
        for pp in range(DCH):
            _W(TEN, dv)
            for hh in range(2):
                r0 = hh * 64
                mm = TEN.matmul(b["ps_e"][r0:r0 + 64, 0:W],
                                b["relb"][r0:r0 + 64, pp * 64:(pp + 1) * 64],
                                b["zr"][r0:r0 + 64, pp * SUPD:pp * SUPD + W],
                                start=True, stop=True, skip_group_check=True,
                                tile_position=(r0, r0))
            pe.inc(mm)
            _W(V, pe)
            dv.sw(V)
            dv.inc(V.scalar_tensor_tensor(b["yf"][:, pp * SUPD:pp * SUPD + W],
                                          b["ps_e"][:, 0:W], 1.0,
                                          b["yf"][:, pp * SUPD:pp * SUPD + W],
                                          ALU.mult, ALU.add))
            dv.sw(V)
            dv.inc(V.tensor_copy(b["yb"][:, pp * SUPD:pp * SUPD + W],
                                 b["yf"][:, pp * SUPD:pp * SUPD + W]))
        ycor_dv = dv.n
        _W(SCE, dv)
        _W(TEN, dv)
        for pp in range(DCH):
            sc.inc(SCE.activation(b["rk2"][:, 0:W], b["yb"][:, pp * SUPD:pp * SUPD + W],
                                  ACT.Square))
            _W(TEN, sc)
            TEN.matmul(b["ps_f"][0:16, 0:W], b["bo16"][:, pp * 16:(pp + 1) * 16],
                       b["yb"][:, pp * SUPD:pp * SUPD + W],
                       start=(pp == 0), stop=(pp == DCH - 1), skip_group_check=True)
            mm = TEN.matmul(b["ps_f"][0:16, 512:512 + W], b["bo16"][:, pp * 16:(pp + 1) * 16],
                            b["rk2"][:, 0:W],
                            start=(pp == 0), stop=(pp == DCH - 1), skip_group_check=True)
            pe.inc(mm)
            _W(SCE, pe) if pp < DCH - 1 else None
        gn_pe = pe.n
        _W(V, pe)
        dv.sw(V)
        dv.inc(V.tensor_scalar_mul(b["gn16"][0:16, 0:W], b["ps_f"][0:16, 0:W], 1.0 / 64.0))
        dv.inc(V.tensor_scalar_mul(b["gn16"][0:16, SUPD:SUPD + W],
                                   b["ps_f"][0:16, 512:512 + W], 1.0 / 64.0))
        dv.sw(V)
        dv.inc(V.tensor_mul(b["gn16"][0:16, 2 * SUPD:2 * SUPD + W],
                            b["gn16"][0:16, 0:W], b["gn16"][0:16, 0:W]))
        dv.sw(V)
        dv.inc(V.scalar_tensor_tensor(b["gn16"][0:16, 3 * SUPD:3 * SUPD + W],
                                      b["gn16"][0:16, SUPD:SUPD + W], EPS_GN,
                                      b["gn16"][0:16, 2 * SUPD:2 * SUPD + W],
                                      ALU.add, ALU.subtract))
        _W(SCE, dv)
        sc.inc(SCE.activation(b["gn16"][0:16, 4 * SUPD:4 * SUPD + W],
                              b["gn16"][0:16, 3 * SUPD:3 * SUPD + W], ACT.Ln))
        sc.sw(SCE)
        sc.inc(SCE.activation(b["gn16b"][0:16, 0:W],
                              b["gn16"][0:16, 4 * SUPD:4 * SUPD + W], ACT.Exp, scale=-0.5))
        _W(V, sc)
        dv.sw(V)
        dv.inc(V.tensor_copy(b["mean16b"][0:16, 0:W], b["gn16"][0:16, 0:W]))
        mean_bf_dv = dv.n
        _W(V, sr, sr1)
        for pp in range(DCH):
            _W(V, pe)
            dv.sw(V)
            dv.inc(V.scalar_tensor_tensor(b["rk2"][:, 0:W], b["rr"][:, pp * SUPD:pp * SUPD + W],
                                          b["lnwb"][:, 2 * DCH + pp:2 * DCH + pp + 1],
                                          b["kr"][:, pp * SUPD:pp * SUPD + W],
                                          ALU.mult, ALU.mult))
            _W(TEN, dv)
            mm = TEN.matmul(b["ps_f"][0:16, 1024:1024 + W], b["bo16"][:, pp * 16:(pp + 1) * 16],
                            b["rk2"][:, 0:W],
                            start=(pp == 0), stop=(pp == DCH - 1), skip_group_check=True)
            pe.inc(mm)
        rk_pe = pe.n
        _W(V, pe)
        dv.sw(V)
        dv.inc(V.tensor_copy(b["rk2"][0:16, 0:W], b["ps_f"][0:16, 1024:1024 + W]))
        rkb_dv = dv.n
        _W(TEN, dv, sc)
        for pp in range(DCH):
            _W(TEN, dv)
            mm = TEN.matmul(b["ps_e"][:, 0:W], b["sel16"][:, pp * 128:(pp + 1) * 128],
                            b["mean16b"][0:16, 0:W], start=True, stop=True,
                            skip_group_check=True)
            mm = TEN.matmul(b["ps_e"][:, 256:256 + W], b["sel16"][:, pp * 128:(pp + 1) * 128],
                            b["gn16b"][0:16, 0:W], start=True, stop=True,
                            skip_group_check=True)
            mm2 = TEN.matmul(b["ps_e"][:, 512:512 + W], b["sel16"][:, pp * 128:(pp + 1) * 128],
                             b["rk2"][0:16, 0:W], start=True, stop=True,
                             skip_group_check=True)
            pe.inc(mm2)
            _W(V, pe)
            dv.sw(V)
            dv.inc(V.tensor_sub(b["yf"][:, pp * SUPD:pp * SUPD + W],
                                b["yf"][:, pp * SUPD:pp * SUPD + W], b["ps_e"][:, 0:W]))
            dv.sw(V)
            dv.inc(V.tensor_mul(b["yf"][:, pp * SUPD:pp * SUPD + W],
                                b["yf"][:, pp * SUPD:pp * SUPD + W], b["ps_e"][:, 256:256 + W]))
            dv.sw(V)
            dv.inc(V.tensor_scalar(b["yf"][:, pp * SUPD:pp * SUPD + W],
                                   b["yf"][:, pp * SUPD:pp * SUPD + W],
                                   b["lnwb"][:, pp:pp + 1], b["lnwb"][:, DCH + pp:DCH + pp + 1],
                                   ALU.mult, ALU.add))
            dv.sw(V)
            dv.inc(V.scalar_tensor_tensor(b["zb"][:, 0:W], b["ps_e"][:, 512:512 + W], 1.0,
                                          b["vr"][:, pp * SUPD:pp * SUPD + W],
                                          ALU.mult, ALU.mult))
            dv.sw(V)
            dv.inc(V.tensor_add(b["yf"][:, pp * SUPD:pp * SUPD + W],
                                b["yf"][:, pp * SUPD:pp * SUPD + W], b["zb"][:, 0:W]))
            dv.sw(V)
            dv.inc(V.tensor_mul(b["ygf"][:, pp * SUPD:pp * SUPD + W],
                                b["yf"][:, pp * SUPD:pp * SUPD + W],
                                b["gr"][:, pp * SUPD:pp * SUPD + W]))
        yg_dv = dv.n
        for ti in range(ntile):
            loc = OWOFF + ts + ti * 128
            par = ti % 2
            sx[par].inc(nc.sync.dma_start(out=b["xc2"][:, par * D:(par + 1) * D],
                                          in_=b["scr_x2"][loc:loc + 128, :]), 16)
            _W(TEN, dv)
            _W(TEN, ld)
            for dch in range(DCH):
                mm = TEN.matmul(b["ps_f"][:, 1024:2048][:, 0:512],
                                b["ygf"][:, dch * SUPD + ti * 128:dch * SUPD + (ti + 1) * 128],
                                b["woT"][:, dch * D:dch * D + 512],
                                start=(dch == 0), stop=False, skip_group_check=True)
                mm = TEN.matmul(b["ps_f"][:, 1024:2048][:, 512:1024],
                                b["ygf"][:, dch * SUPD + ti * 128:dch * SUPD + (ti + 1) * 128],
                                b["woT"][:, dch * D + 512:(dch + 1) * D],
                                start=(dch == 0), stop=(dch == DCH - 1),
                                skip_group_check=True)
            pe.inc(mm)
            _W(V, pe)
            _W(V, sx[par])
            dv.sw(V)
            dv.inc(V.scalar_tensor_tensor(b["x3t"][:, par * D:(par + 1) * D],
                                          b["ps_f"][:, 1024:2048], 1.0,
                                          b["xc2"][:, par * D:(par + 1) * D],
                                          ALU.mult, ALU.add))
            x3_dv = dv.n
            _W(SCE, dv)
            dv.sw(V)
            _ln_tile(p, b["x3t"][:, par * D:(par + 1) * D], b["tbfE"])
            _W(TEN, sc, dv)
            _transpose_tile(p, b["tbfE"],
                            lambda dch, _ti=ti: b["h3T"][:, dch * SUPD + _ti * 128:
                                                         dch * SUPD + (_ti + 1) * 128])
        _ffn_sup(p, c, ts, W, ntile)
    return


def _ffn_sup(p, c, ts, W, ntile):
    nc, b = p.nc, p.b
    pe, dv, sc = b["s_pe"], b["s_dve"], b["s_sce"]
    V, SCE, TEN = nc.vector, nc.scalar, nc.tensor
    sw1 = b["s_w1"]
    sr1 = b["s_r1"]
    kn = _KNOTS
    _W(TEN, dv, sc)
    kb = ["kb0", "kb1"]
    dv_at = {}
    for hch in range(32):
        psl = b["ps_f"][:, (hch % 2) * 512:(hch % 2) * 512 + W]
        pacc = b["ps_f"][:, 1024 + (hch % 2) * 512:1024 + (hch % 2) * 512 + W]
        if hch >= 2:
            _W(TEN, dv)
        for dch in range(DCH):
            mm = TEN.matmul(psl,
                            b["w1r"][:, dch * 4096 + hch * 128:dch * 4096 + (hch + 1) * 128],
                            b["h3T"][:, dch * SUPD:dch * SUPD + W],
                            start=(dch == 0), stop=(dch == DCH - 1),
                            skip_group_check=True)
        pe.inc(mm)
        w1_pe = pe.n
        _W(V, pe)
        dv.sw(V)
        dv.inc(V.tensor_scalar(b["nb1"][:, 0:W], psl, b["b1c"][:, hch:hch + 1], 256.0,
                               ALU.add, ALU.mult))
        dv.sw(V)
        dv.inc(V.tensor_scalar(b["nb2"][:, 0:W], b["nb1"][:, 0:W], MAGIC, MAGIC,
                               ALU.add, ALU.subtract))
        dv.sw(V)
        dv.inc(V.tensor_scalar_mul(b["nb1"][:, 0:W], b["nb2"][:, 0:W], 1.0 / 64.0))
        dv.sw(V)
        dv.inc(V.scalar_tensor_tensor(b["zb"][:, 0:W], b["nb1"][:, 0:W], -1.0,
                                      b["nb1"][:, 0:W], ALU.mult, ALU.min))
        for k in range(17):
            K_k, d_k = kn[k]
            buf = b[kb[k % 2]]
            dv.sw(V)
            if k >= 2:
                nc.vector.wait_ge(pe.sem, dv_at[k - 2])
            dv.inc(V.tensor_scalar(buf[:, 0:W], b["zb"][:, 0:W], float(K_k), float(d_k),
                                   ALU.max, ALU.mult))
            nc.tensor.wait_ge(dv.sem, dv.n)
            mm = TEN.matmul(pacc, b["idt"], buf[:, 0:W], start=(k == 0), stop=(k == 16),
                            skip_group_check=True)
            pe.inc(mm)
            dv_at[k] = pe.n
        csum = -sum(K_k * d_k for K_k, d_k in kn)
        _W(V, pe)
        dv.sw(V)
        dv.inc(V.scalar_tensor_tensor(b["nb1"][:, 0:W], b["nb2"][:, 0:W], 0.0,
                                      pacc, ALU.max, ALU.add))
        dv.sw(V)
        dv.inc(V.tensor_scalar(b["nb1"][:, 0:W], b["nb1"][:, 0:W],
                               float(csum) + MAGIC, MAGIC, ALU.add, ALU.subtract))
        dv.sw(V)
        dv.inc(V.tensor_scalar_mul(b["hact"][:, hch * SUPD:hch * SUPD + W],
                                   b["nb1"][:, 0:W], 1.0 / 256.0))
    act_dv = dv.n
    _W(TEN, dv)
    pe_at = {}
    sw2 = [b["s_w2a"], b["s_w2b"]]
    for hch in range(32):
        wpar = hch % 2
        if hch >= 2:
            nc.sync.wait_ge(pe.sem, pe_at[hch - 2])
        sw2[wpar].inc(nc.sync.dma_start(out=b["w2s"][:, wpar * D:(wpar + 1) * D],
                                        in_=b["w2_in"][hch * 128:(hch + 1) * 128, :]), 16)
        nc.tensor.wait_ge(sw2[wpar].sem, sw2[wpar].n)
        for ti in range(ntile):
            for hq in range(2):
                mm = TEN.matmul(b["ps_f"][:, ti * 1024 + hq * 512:ti * 1024 + (hq + 1) * 512],
                                b["hact"][:, hch * SUPD + ti * 128:hch * SUPD + (ti + 1) * 128],
                                b["w2s"][:, wpar * D + hq * 512:wpar * D + (hq + 1) * 512],
                                start=(hch == 0), stop=(hch == 31), skip_group_check=True)
        pe.inc(mm)
        pe_at[hch] = pe.n
    for ti in range(ntile):
        mm = TEN.matmul(b["ps_f"][:, ti * 1024:ti * 1024 + 512], b["onesr"],
                        b["b2r"][:, 0:512], start=False, stop=True, skip_group_check=True)
        mm = TEN.matmul(b["ps_f"][:, ti * 1024 + 512:ti * 1024 + 1024], b["onesr"],
                        b["b2r"][:, 512:1024], start=False, stop=True, skip_group_check=True)
    pe.inc(mm)
    _W(V, pe)
    for ti in range(ntile):
        par = ti % 2
        dv.sw(V)
        dv.inc(V.tensor_scalar(b["ob1"], b["ps_f"][:, ti * 1024:(ti + 1) * 1024],
                               256.0, MAGIC, ALU.mult, ALU.add))
        dv.sw(V)
        dv.inc(V.tensor_scalar(b["ob2"], b["ob1"], MAGIC, 1.0 / 256.0,
                               ALU.subtract, ALU.mult))
        dv.sw(V)
        if sw1.n:
            nc.vector.wait_ge(sw1.sem, sw1.n)
        dv.inc(V.tensor_add(b["x4t"], b["ob2"], b["x3t"][:, par * D:(par + 1) * D]))
        x4_dv = dv.n
        nc.gpsimd.wait_ge(dv.sem, x4_dv)
        loc = ts + ti * 128
        sw1.inc(nc.gpsimd.dma_start(out=b["y_out_ap"][loc:loc + 128, :], in_=b["x4t"]), 16)



# version 05 (final)
# speedup vs baseline: 1.0025x; 1.0025x over previous
import math
import numpy as np
import ml_dtypes

import concourse.bass as bass
import concourse.mybir as mybir
from concourse.bass_utils import run_bass_kernel_spmd
from concourse.dve_ops import DveOp, OPS
from concourse import dve_spec as dvs
from concourse.dve_spec import Spec, Src0, Src1, C0, C1, C2, relu, lower, One, Zero, Latch, Bin, AluOp

F32 = mybir.dt.float32
BF16 = mybir.dt.bfloat16
ALU = mybir.AluOpType
ACT = mybir.ActivationFunctionType
AX = mybir.AxisListType

BF = ml_dtypes.bfloat16

N_CORES = 8
T_FULL = 16384
D = 1024
DCH = D // 128
MAGIC = 12582912.0

SLICES = [2944, 2560, 2304, 2048, 1920, 1792, 1536, 1280]
assert sum(SLICES) == T_FULL and all(s % 128 == 0 for s in SLICES)
STARTS = [sum(SLICES[:c]) for c in range(N_CORES)]
OV = 256

H = 16
HS = 64
CHUNK = 64
SUP = 512
SUPD = 256

EPS_LN = 1e-5
EPS_GN = 0.00064
L0FF_SCALE = 1.0 / 127.0

_PWL_X64 = np.linspace(-4.0, 4.0, 33)
_PWL_Y64 = 0.5 * _PWL_X64 * (1.0 + np.array([math.erf(v / math.sqrt(2.0)) for v in _PWL_X64]))


def _pwl_knots():
    xs = _PWL_X64[:17]
    ys = _PWL_Y64[:17]
    slopes = (_PWL_Y64[1:17] - _PWL_Y64[0:16]) / 0.25
    knots = []
    s_jump = ys[0] * 256.0
    knots.append((-16.0 - 1.0 / 16.0, 64.0 * s_jump))
    prev = s_jump
    for k in range(16):
        pos = -16.0 + k
        knots.append((pos, 64.0 * (slopes[k] - prev)))
        prev = slopes[k]
    return knots


_KNOTS = _pwl_knots()


def _const_node(v):
    raise RuntimeError("unused")


def _register_pwl_ops():
    ops = []
    toff2 = Latch(Bin(AluOp.ADD, C0, One))
    body_seed = relu(Src0 + C0) * C1 + relu(Src0 + toff2) * C2
    body_chain = (Src1 + relu(Src0 + C0) * C2) + relu(Src0 + toff2) * C1
    body_one = Src1 + relu(Src0 + C0) * C1
    specs = [("ATHPWL_SEED", Spec(body=body_seed),
              lambda in0, in1, s0, s1, imm2: np.maximum(in0 + s0, 0) * s1 + np.maximum(in0 + s0 + 1.0, 0) * imm2),
             ("ATHPWL_CHAIN", Spec(body=body_chain),
              lambda in0, in1, s0, s1, imm2: in1 + np.maximum(in0 + s0, 0) * imm2 + np.maximum(in0 + s0 + 1.0, 0) * s1)]
    from concourse.dve_ops import (_SUB_OPCODE_FOR_NAME, _CUSTOM_DVE_ROW_BASE,
                                   CUSTOM_DVE_REFS, CUSTOM_DVE_SPECS)
    have = {op.name for op in OPS}
    out = {}
    repl = len(OPS)
    for _ in specs:
        OPS.append(None)
    for i, (name, spec, ref) in enumerate(specs):
        if name in have:
            out[name] = next(op for op in OPS if op.name == name)
            continue
        shas = {}
        for ver in ("v3", "v4"):
            try:
                shas[ver] = lower(spec, ver=ver).sha(ver)
            except Exception:
                pass
        op = DveOp(name, spec, subdim=False, uops_sha=shas, reference=ref)
        idx = repl + i
        OPS[idx] = op
        _SUB_OPCODE_FOR_NAME[name] = _CUSTOM_DVE_ROW_BASE + idx
        CUSTOM_DVE_SPECS[name] = spec
        CUSTOM_DVE_REFS[name] = ref
        out[name] = op
    return out


_PWL_OPS = _register_pwl_ops()


def _bf(x):
    return np.ascontiguousarray(np.asarray(x)).astype(BF)


def _f32(x):
    return np.ascontiguousarray(np.asarray(x), dtype=np.float32)


def _perpart(vec):
    v = _f32(vec).reshape(DCH, 128)
    return np.ascontiguousarray(v.T)


class EngCtr:
    def __init__(self, sem):
        self.sem = sem
        self.n = 0

    def inc(self, instr, k=1):
        instr.then_inc(self.sem, k)
        self.n += k
        return self.n

    def sw(self, eng):
        eng.wait_ge(self.sem, self.n)


def ceil_div(a, b):
    return (a + b - 1) // b


import contextlib
import types


class Prog:

    def __init__(self, tap=None):
        self.tap = tap
        self.nc = bass.Bass("TRN2", target_bir_lowering=False, debug=False)
        self.stack = contextlib.ExitStack()
        self.b = {}
        self.MAXSLICE = max(SLICES)
        self.MAXTOT = self.MAXSLICE + OV

    def ein(self, name, shape, dt):
        self.b[name] = self.nc.dram_tensor(name, shape, dt, kind="ExternalInput").ap()

    def ph_open(self, tag):
        self._ph = contextlib.ExitStack()
        self._tag = tag

    def ph_sbuf(self, name, shape, dt):
        self.b[name] = self._ph.enter_context(
            self.nc.sbuf_tensor(f"{name}_{self._tag}", shape, dt))[:]
        return self.b[name]

    def ph_psum(self, name, shape, dt):
        self.b[name] = self._ph.enter_context(
            self.nc.psum_tensor(f"{name}_{self._tag}", shape, dt))[:]
        return self.b[name]

    def ph_close(self):
        self._ph.close()

    def sbuf(self, name, shape, dt):
        self.b[name] = self.stack.enter_context(self.nc.sbuf_tensor(name, shape, dt))[:]

    def psum(self, name, shape, dt):
        self.b[name] = self.stack.enter_context(self.nc.psum_tensor(name, shape, dt))[:]

    def sem(self, name):
        self.b[name] = EngCtr(self.stack.enter_context(self.nc.semaphore(name)))

    def dram(self, name, shape, dt, kind="Internal"):
        self.b[name] = self.nc.dram_tensor(name, shape, dt, kind=kind).ap()


def build_program(tap=None):
    p = Prog(tap)
    nc = p.nc
    b = p.b
    MAXTOT = p.MAXTOT

    p.ein("x", [T_FULL, D], F32)
    p.ein("invd_in", [128, MAXTOT], F32)
    p.ein("ident", [128, 128], BF16)
    p.ein("identf", [128, 128], F32)
    p.ein("trimask", [128, 128], BF16)
    p.ein("ln0g_in", [128, DCH], F32)
    p.ein("ln0b_in", [128, DCH], F32)
    p.ein("Whid", [D, 4096], BF16)
    p.ein("bhid_in", [1, 4096], BF16)
    p.ein("Wqk", [D, 128], BF16)
    p.ein("bqk_in", [128, 1], F32)
    p.ein("os_in", [128, 4], F32)
    p.ein("Wout", [2048, D], BF16)
    p.ein("bout_in", [1, D], BF16)
    p.ein("ones_in", [1, 128], BF16)
    p.ein("wgT", [D, 3072], BF16)
    p.ein("wuT", [D, 3072], BF16)
    p.ein("wdT", [3072, D], BF16)
    p.ein("maa_w1T", [D, 128], BF16)
    p.ein("gate_w1T", [D, 128], BF16)
    p.ein("decay_w1T", [D, 64], BF16)
    p.ein("WrT", [D, D], BF16)
    p.ein("WkT", [D, D], BF16)
    p.ein("WvT", [D, D], BF16)
    p.ein("lw1T_in", [4 * D, 16], BF16)
    p.ein("maa_w2_in", [32, 4 * D], BF16)
    p.ein("gate_w2_in", [128, D], BF16)
    p.ein("decay_w2_in", [64, D], BF16)
    p.ein("lw2_in", [16, 4 * D], BF16)
    p.ein("pvecs_in", [128, 40], F32)
    p.ein("npvecs_in", [128, 40], F32)
    p.ein("tmaa_in", [128, 4 * DCH], F32)
    p.ein("bo16_in", [128, 8 * 16], BF16)
    p.ein("sel16_in", [16, 8 * 128], BF16)
    p.ein("masks_in", [128, 5 * SUPD], F32)
    p.ein("eye64_in", [128, 64], F32)
    p.ein("WoT", [D, D], BF16)
    p.ein("w1_in", [D, 4096], BF16)
    p.ein("w2_in", [4096, D], BF16)
    p.ein("b2_in", [1, D], BF16)
    p.ein("lnwb_in", [128, 3 * DCH], F32)
    p.ein("b1_in", [128, 32], F32)

    y_out = nc.dram_tensor("y", [p.MAXSLICE, D], F32, kind="ExternalOutput").ap()
    b["y_out_ap"] = y_out
    dbg = None
    if tap is not None:
        dbg = nc.dram_tensor("dbg", [MAXTOT, 4096], F32, kind="ExternalOutput").ap()
        b["dbg2"] = nc.dram_tensor("dbg2", [128, 2 * MAXTOT], F32, kind="ExternalOutput").ap()

    p.dram("scr_vt", [T_FULL, 2048], BF16)
    p.dram("scr_gt", [MAXTOT, 2048], BF16)
    p.dram("scr_xh", [MAXTOT, D], F32)
    p.dram("scr_x1", [MAXTOT, D], F32)
    p.dram("scr_x2", [MAXTOT, D], F32)
    MS = p.MAXSLICE
    p.dram("scr_r", [128, DCH * MS], BF16)
    p.dram("scr_k", [128, DCH * MS], BF16)
    p.dram("scr_v1", [128, DCH * MS], BF16)
    p.dram("scr_g", [128, DCH * MS], BF16)
    p.dram("scr_zh", [128, DCH * MS], BF16)
    p.dram("scr_y", [128, DCH * MS], F32)
    b["cc_in"] = nc.dram_tensor("cc_in", [128, 1024], F32).ap()
    b["cc_out"] = nc.dram_tensor("cc_out", [8 * 128, 1024], F32).ap()

    for s in ("s_ld", "s_gp", "s_pe", "s_dve", "s_sce", "s_x0", "s_x1",
              "s_vt", "s_gt", "s_xh", "s_r0", "s_r1", "s_w0", "s_w1",
              "s_w2a", "s_w2b", "s_cc"):
        p.sem(s)

    with p.stack:
        engs = [nc.tensor, nc.vector, nc.scalar, nc.sync, nc.gpsimd]
        pids = [e.partition_id() for e in engs]
        for core in nc.Switch(engines=engs, index=pids, n=N_CORES):
            _emit_core(p, core, y_out, dbg)
    for blk in nc.m.functions[0].blocks:
        for inst in blk.instructions:
            if isinstance(inst, mybir.InstCollectiveCompute):
                inst.name = "cc_allgather"
    return nc


def _emit_core(p, c, y_out, dbg):
    import os as _os
    nc = p.nc
    b = p.b
    p.stop = int(_os.environ.get("A_STOP", "99"))
    for s in ("s_ld", "s_gp", "s_pe", "s_dve", "s_sce", "s_x0", "s_x1",
              "s_vt", "s_gt", "s_xh", "s_r0", "s_r1", "s_w0", "s_w1",
              "s_w2a", "s_w2b", "s_cc"):
        b[s].n = 0
    start = STARTS[c]
    sl = SLICES[c]
    pend = start + sl
    astart = max(0, start - OV)
    tot = pend - astart
    n_pref = pend // 128
    own0 = astart // 128

    p.ph_open(f"bd{c}")
    body_stack = p._ph
    p.ph_sbuf("onesr", [1, 128], BF16)
    p.ph_sbuf("idt", [128, 128], BF16)
    p.ph_sbuf("trim", [128, 128], BF16)
    p.ph_sbuf("idtf", [128, 128], F32)
    p.ph_sbuf("stats", [128, 24], F32)
    p.ph_sbuf("bo16", [128, 8 * 16], BF16)
    p.ph_sbuf("sel16", [16, 8 * 128], BF16)

    p.ph_open(f"ab{c}")
    ab_stack = p._ph
    p.ph_sbuf("kT", [128, T_FULL], BF16)
    p.ph_sbuf("qf", [128, p.MAXTOT], BF16)
    p.ph_sbuf("invd", [128, p.MAXTOT], F32)

    p.ph_open(f"A{c}")
    _phase_setup(p, c)
    _phase_A(p, c, n_pref, own0, astart)
    if p.tap == "A":
        _tap_A(p, c, dbg, astart, tot)
        p.ph_close()
        ab_stack.close()
        body_stack.close()
        return
    p.ph_close()
    p.ph_open(f"B{c}")
    _phase_B(p, c, astart, pend, dbg)
    p.ph_close()
    ab_stack.close()
    if p.tap == "B":
        body_stack.close()
        return
    p.ph_open(f"C{c}")
    _phase_C(p, c, astart, pend, dbg)
    p.ph_close()
    if p.tap == "C":
        body_stack.close()
        return
    p.ph_open(f"D{c}")
    _phase_D(p, c, astart, pend, dbg)
    if p.tap == "D":
        p.ph_close()
        body_stack.close()
        return
    p.ph_close()
    p.ph_open(f"E{c}")
    _phase_EF(p, c, astart, pend, dbg, y_out)
    nc = p.nc
    for en in ("sync", "gpsimd", "vector", "scalar", "tensor"):
        pass
    _barrier(p)
    p.ph_close()
    body_stack.close()


def _phase_setup(p, c):
    nc, b = p.nc, p.b
    p.ph_sbuf("xt", [128, 2 * D], F32)
    p.ph_sbuf("tbf", [128, D], BF16)
    p.ph_sbuf("nT", [128, DCH * 128], BF16)
    p.ph_sbuf("qk_s", [128, 128], F32)
    p.ph_sbuf("w_hid", [128, DCH * 4096], BF16)
    p.ph_sbuf("w_qk", [128, DCH * 128], BF16)
    p.ph_sbuf("b_hid", [1, 4096], BF16)
    p.ph_sbuf("ln0g", [128, DCH], F32)
    p.ph_sbuf("ln0b", [128, DCH], F32)
    p.ph_sbuf("bqk", [128, 1], F32)
    p.ph_sbuf("osv", [128, 4], F32)
    p.ph_sbuf("qtmp", [128, 128], F32)
    p.ph_sbuf("vt_sb", [128, 2048], BF16)
    p.ph_sbuf("gt_sb", [128, 2048], BF16)
    p.ph_sbuf("xh_sb", [128, D], F32)
    p.ph_psum("ps_big", [128, 2048], F32)
    p.ph_psum("ps_tp", [128, 2048], BF16)
    p.ph_psum("ps_s", [128, 512], F32)
    ld = b["s_ld"]
    sy = nc.sync
    ld.inc(sy.dma_start(out=b["invd"], in_=b["invd_in"]), 16)
    ld.inc(sy.dma_start(out=b["idt"], in_=b["ident"]), 16)
    ld.inc(sy.dma_start(out=b["idtf"], in_=b["identf"]), 16)
    ld.inc(sy.dma_start(out=b["trim"], in_=b["trimask"]), 16)
    ld.inc(sy.dma_start(out=b["ln0g"], in_=b["ln0g_in"]), 16)
    ld.inc(sy.dma_start(out=b["ln0b"], in_=b["ln0b_in"]), 16)
    ld.inc(sy.dma_start(out=b["bqk"], in_=b["bqk_in"]), 16)
    ld.inc(sy.dma_start(out=b["osv"], in_=b["os_in"]), 16)
    ld.inc(sy.dma_start(out=b["b_hid"], in_=b["bhid_in"]), 16)
    ld.inc(sy.dma_start(out=b["onesr"], in_=b["ones_in"]), 16)
    for dch in range(DCH):
        ld.inc(sy.dma_start(out=b["w_hid"][:, dch * 4096:(dch + 1) * 4096],
                            in_=b["Whid"][dch * 128:(dch + 1) * 128, :]), 16)
        ld.inc(sy.dma_start(out=b["w_qk"][:, dch * 128:(dch + 1) * 128],
                            in_=b["Wqk"][dch * 128:(dch + 1) * 128, :]), 16)


def _phase_A(p, c, n_pref, own0, astart):
    nc, b = p.nc, p.b
    ld, gp, pe, dv, sc = (b[s] for s in ("s_ld", "s_gp", "s_pe", "s_dve", "s_sce"))
    sx = [b["s_x0"], b["s_x1"]]
    svt, sgt, sxh = b["s_vt"], b["s_gt"], b["s_xh"]
    st = b["stats"]
    xt, tbf, nT = b["xt"], b["tbf"], b["nT"]
    ps_big, ps_tp, ps_s = b["ps_big"], b["ps_tp"], b["ps_s"]
    setup_ld = ld.n

    sx[0].inc(nc.sync.dma_start(out=xt[:, 0:D], in_=b["x"][0:128, :]), 16)

    prev = dict(nT_ev_dv=0, nT_ev_sc=0, silu_sc=0, vt_gp=0, gt_gp=0, xh_gp=0,
                qk_sc=0, xh_dv=0, tp_pe=0, apply_sc=0)

    import os as _os
    _lim = int(_os.environ.get("A_TILES", "0"))
    if _lim:
        n_pref = min(n_pref, _lim)
    for g in range(n_pref):
        slot = (g % 2) * D
        own = g >= own0
        loc = g * 128 - astart
        if g + 1 < n_pref:
            nslot = ((g + 1) % 2) * D
            if g >= 1:
                nc.sync.wait_ge(sc.sem, prev["apply_sc"])
                nc.sync.wait_ge(dv.sem, prev["xh_dv"])
            sx[(g + 1) % 2].inc(nc.sync.dma_start(out=xt[:, nslot:nslot + D],
                                     in_=b["x"][(g + 1) * 128:(g + 2) * 128, :]), 16)
        xsem = sx[g % 2]
        x_ready = 16 * (g // 2 + 1)

        if p.stop < 1:
            continue
        nc.vector.wait_ge(xsem.sem, x_ready)
        dv.inc(nc.vector.reduce_sum(st[:, 0:1], xt[:, slot:slot + D], AX.X))
        nc.scalar.wait_ge(xsem.sem, x_ready)
        if g > 0:
            nc.scalar.wait_ge(dv.sem, prev["xh_dv"])
        sc.inc(nc.scalar.activation(tbf[:], xt[:, slot:slot + D], ACT.Square,
                                    accum_out=st[:, 1:2]))
        sq_done = sc.n
        if p.stop < 2:
            continue
        nc.vector.wait_ge(sc.sem, sq_done)
        dv.sw(nc.vector)
        dv.inc(nc.vector.tensor_scalar_mul(st[:, 2:3], st[:, 0:1], 1.0 / D))
        dv.inc(nc.vector.tensor_scalar_mul(st[:, 3:4], st[:, 1:2], 1.0 / D))
        dv.sw(nc.vector)
        dv.inc(nc.vector.tensor_mul(st[:, 4:5], st[:, 2:3], st[:, 2:3]))
        dv.sw(nc.vector)
        dv.inc(nc.vector.scalar_tensor_tensor(st[:, 5:6], st[:, 3:4], EPS_LN,
                                              st[:, 4:5], ALU.add, ALU.subtract))
        dv.sw(nc.vector)
        dv.inc(nc.vector.tensor_scalar_add(st[:, 15:16], st[:, 5:6], -EPS_LN))
        v1_done = dv.n
        nc.scalar.wait_ge(dv.sem, v1_done)
        sc.inc(nc.scalar.activation(st[:, 6:7], st[:, 5:6], ACT.Sqrt))
        nc.vector.wait_ge(sc.sem, sc.n)
        dv.inc(nc.vector.reciprocal(st[:, 7:8], st[:, 6:7]))
        dv.sw(nc.vector)
        dv.inc(nc.vector.tensor_mul(st[:, 8:9], st[:, 15:16], st[:, 7:8]))
        dv.sw(nc.vector)
        dv.inc(nc.vector.tensor_mul(st[:, 9:10], st[:, 8:9], st[:, 7:8]))
        dv.sw(nc.vector)
        dv.inc(nc.vector.tensor_scalar_add(st[:, 9:10], st[:, 9:10], EPS_LN))
        v2_done = dv.n
        nc.scalar.wait_ge(dv.sem, v2_done)
        sc.inc(nc.scalar.activation(st[:, 10:11], st[:, 9:10], ACT.Sqrt))
        invr2_done = sc.n
        nc.vector.wait_ge(sc.sem, invr2_done)
        dv.inc(nc.vector.reciprocal(st[:, 11:12], st[:, 10:11]))
        dv.sw(nc.vector)
        dv.inc(nc.vector.tensor_mul(st[:, 12:13], st[:, 7:8], st[:, 11:12]))
        dv.sw(nc.vector)
        dv.inc(nc.vector.tensor_mul(st[:, 13:14], st[:, 2:3], st[:, 12:13]))
        dv.sw(nc.vector)
        dv.inc(nc.vector.tensor_scalar_mul(st[:, 14:15], st[:, 13:14], -1.0))
        micro_done = dv.n
        if p.stop < 3:
            continue
        nc.scalar.wait_ge(dv.sem, micro_done)
        sc.inc(nc.scalar.activation(tbf[:], xt[:, slot:slot + D], ACT.Identity,
                                    bias=st[:, 14:15], scale=st[:, 12:13]))
        apply_done = sc.n
        prev["apply_sc"] = apply_done
        if p.stop < 4:
            continue
        nc.tensor.wait_ge(sc.sem, apply_done)
        nc.tensor.wait_ge(ld.sem, setup_ld)
        if g > 0:
            nc.tensor.wait_ge(dv.sem, prev["nT_ev_dv"])
            nc.tensor.wait_ge(sc.sem, prev["nT_ev_sc"])
        for dch in range(DCH):
            off = (dch % 4) * 128 + (dch // 4) * 1024
            mm = nc.tensor.transpose(ps_tp[:, off:off + 128],
                                     tbf[:, dch * 128:(dch + 1) * 128], b["idt"])
            if dch == DCH - 1:
                pe.inc(mm)
        tp_done = pe.n
        if p.stop < 5:
            continue
        nc.vector.wait_ge(pe.sem, tp_done)
        nc.scalar.wait_ge(pe.sem, tp_done)
        for dch in range(DCH):
            off = (dch % 4) * 128 + (dch // 4) * 1024
            src = ps_tp[:, off:off + 128]
            dst = nT[:, dch * 128:(dch + 1) * 128]
            gsl = b["ln0g"][:, dch:dch + 1]
            bsl = b["ln0b"][:, dch:dch + 1]
            if dch < 4:
                dv.inc(nc.vector.tensor_scalar(dst, src, gsl, bsl, ALU.mult, ALU.add))
            else:
                sc.inc(nc.scalar.activation(dst, src, ACT.Identity, bias=bsl, scale=gsl))
        prev["nT_ev_dv"] = dv.n
        prev["nT_ev_sc"] = sc.n
        if p.stop < 6:
            continue
        nc.tensor.wait_ge(dv.sem, prev["nT_ev_dv"])
        nc.tensor.wait_ge(sc.sem, prev["nT_ev_sc"])
        nc.tensor.wait_ge(ld.sem, setup_ld)
        if g > 0:
            nc.tensor.wait_ge(sc.sem, prev["silu_sc"])
        for q4 in range(4):
            sl4 = slice(q4 * 512, (q4 + 1) * 512)
            for dch in range(DCH):
                nc.tensor.matmul(ps_big[:, sl4],
                                 nT[:, dch * 128:(dch + 1) * 128],
                                 b["w_hid"][:, dch * 4096 + q4 * 512:
                                            dch * 4096 + (q4 + 1) * 512],
                                 start=(dch == 0), stop=False,
                                 skip_group_check=True)
            mm = nc.tensor.matmul(ps_big[:, sl4],
                                  b["onesr"], b["b_hid"][:, q4 * 512:(q4 + 1) * 512],
                                  start=False, stop=True, skip_group_check=True)
            if q4 == 3:
                pe.inc(mm)
        vmm_done = pe.n
        if p.stop < 7:
            continue
        nc.scalar.wait_ge(pe.sem, vmm_done)
        if g > 0:
            nc.scalar.wait_ge(svt.sem, svt.n)
        sc.inc(nc.scalar.activation(b["vt_sb"][:, 0:1024], ps_big[:, 0:1024], ACT.Silu))
        sc.inc(nc.scalar.activation(b["vt_sb"][:, 1024:2048], ps_big[:, 1024:2048], ACT.Silu))
        prev["silu_sc"] = sc.n
        nc.gpsimd.wait_ge(sc.sem, prev["silu_sc"])
        svt.inc(nc.gpsimd.dma_start(out=b["scr_vt"][g * 128:(g + 1) * 128, :],
                                    in_=b["vt_sb"]), 16)
        if p.stop < 8:
            continue
        nc.tensor.wait_ge(sc.sem, prev["qk_sc"]) if g > 0 else None
        for dch in range(DCH):
            mm = nc.tensor.matmul(ps_s[:, 0:128],
                                  b["w_qk"][:, dch * 128:(dch + 1) * 128],
                                  nT[:, dch * 128:(dch + 1) * 128],
                                  start=(dch == 0), stop=(dch == DCH - 1))
            if dch == DCH - 1:
                pe.inc(mm)
        qkmm_done = pe.n
        nc.scalar.wait_ge(pe.sem, qkmm_done)
        sc.inc(nc.scalar.activation(b["qk_s"], ps_s[:, 0:128], ACT.Silu, bias=b["bqk"]))
        prev["qk_sc"] = sc.n
        nc.vector.wait_ge(sc.sem, prev["qk_sc"])
        dv.inc(nc.vector.tensor_scalar(b["kT"][:, g * 128:(g + 1) * 128], b["qk_s"],
                                       b["osv"][:, 2:3], b["osv"][:, 3:4],
                                       ALU.mult, ALU.add))
        if own:
            dv.inc(nc.vector.tensor_scalar(b["qtmp"], b["qk_s"],
                                           b["osv"][:, 0:1], b["osv"][:, 1:2],
                                           ALU.mult, ALU.add))
            dv.sw(nc.vector)
            dv.inc(nc.vector.tensor_mul(b["qf"][:, loc:loc + 128], b["qtmp"],
                                        b["invd"][:, loc:loc + 128]))
            nc.tensor.wait_ge(sc.sem, prev["silu_sc"])
            for q4 in range(4):
                sl4 = slice(q4 * 512, (q4 + 1) * 512)
                for dch in range(DCH):
                    nc.tensor.matmul(ps_big[:, sl4],
                                     nT[:, dch * 128:(dch + 1) * 128],
                                     b["w_hid"][:, dch * 4096 + 2048 + q4 * 512:
                                                dch * 4096 + 2048 + (q4 + 1) * 512],
                                     start=(dch == 0), stop=False,
                                     skip_group_check=True)
                mm = nc.tensor.matmul(ps_big[:, sl4],
                                      b["onesr"],
                                      b["b_hid"][:, 2048 + q4 * 512:2048 + (q4 + 1) * 512],
                                      start=False, stop=True, skip_group_check=True)
                if q4 == 3:
                    pe.inc(mm)
            gmm_done = pe.n
            nc.scalar.wait_ge(pe.sem, gmm_done)
            if sgt.n:
                nc.scalar.wait_ge(sgt.sem, sgt.n)
            sc.inc(nc.scalar.activation(b["gt_sb"][:, 0:1024], ps_big[:, 0:1024], ACT.Silu))
            sc.inc(nc.scalar.activation(b["gt_sb"][:, 1024:2048], ps_big[:, 1024:2048], ACT.Silu))
            gsilu_done = sc.n
            prev["silu_sc"] = sc.n
            nc.gpsimd.wait_ge(sc.sem, gsilu_done)
            sgt.inc(nc.gpsimd.dma_start(out=b["scr_gt"][loc:loc + 128, :],
                                        in_=b["gt_sb"]), 16)
            if sxh.n:
                nc.vector.wait_ge(sxh.sem, sxh.n)
            dv.inc(nc.vector.scalar_tensor_tensor(b["xh_sb"], tbf[:], st[:, 10:11],
                                                  xt[:, slot:slot + D],
                                                  ALU.mult, ALU.add))
            prev["xh_dv"] = dv.n
            nc.gpsimd.wait_ge(dv.sem, prev["xh_dv"])
            sxh.inc(nc.gpsimd.dma_start(out=b["scr_xh"][loc:loc + 128, :],
                                        in_=b["xh_sb"]), 16)
        else:
            prev["xh_dv"] = dv.n


def _tap_A(p, c, dbg, astart, tot):
    nc, b = p.nc, p.b
    ld, gp, pe, dv, sc = (b[s] for s in ("s_ld", "s_gp", "s_pe", "s_dve", "s_sce"))
    for nm in ("s_vt", "s_gt", "s_xh"):
        if b[nm].n:
            nc.sync.wait_ge(b[nm].sem, b[nm].n)
            nc.gpsimd.wait_ge(b[nm].sem, b[nm].n)
    nc.sync.wait_ge(dv.sem, dv.n)
    nc.sync.wait_ge(sc.sem, sc.n)
    nc.gpsimd.wait_ge(dv.sem, dv.n)
    nc.gpsimd.wait_ge(sc.sem, sc.n)
    pend = astart + tot
    import os
    if os.environ.get("TAP_MIN"):
        ld.inc(nc.sync.dma_start(out=dbg[0:128, 0:1024], in_=b["xh_sb"]), 16)
        nc.sync.wait_ge(ld.sem, ld.n)
        return
    ld.inc(nc.sync.dma_start(out=dbg[0:tot, 0:1024], in_=b["scr_xh"][0:tot, :]), 16)
    gp.inc(nc.gpsimd.dma_start(out=dbg[0:tot, 1024:3072], in_=b["scr_vt"][astart:pend, :]), 16)
    gp.inc(nc.gpsimd.dma_start(out=dbg[0:tot, 3072:4096], in_=b["scr_gt"][0:tot, 0:1024]), 16)
    dbg2 = b.get("dbg2")
    if dbg2 is not None:
        gp.inc(nc.gpsimd.dma_start(out=dbg2[0:128, 0:tot], in_=b["qf"][:, 0:tot]), 16)
        gp.inc(nc.gpsimd.dma_start(out=dbg2[0:128, p.MAXTOT:p.MAXTOT + tot],
                                   in_=b["kT"][:, astart:pend]), 16)
    nc.sync.wait_ge(ld.sem, ld.n)
    nc.sync.wait_ge(gp.sem, gp.n)


def _mk_bo16():
    m = np.zeros((128, 128), np.float32)
    for pp in range(8):
        m[0:64, pp * 16 + 2 * pp] = 1.0
        m[64:128, pp * 16 + 2 * pp + 1] = 1.0
    return m.astype(BF)


def _mk_sel16():
    m = np.zeros((16, 1024), np.float32)
    for pp in range(8):
        m[2 * pp, pp * 128:pp * 128 + 64] = 1.0
        m[2 * pp + 1, pp * 128 + 64:pp * 128 + 128] = 1.0
    return m.astype(BF)


def _mk_masks():
    r = np.arange(128)[:, None] % 64
    t = np.arange(64)[None, :]
    su = -np.where(r < t, 1.0, 0.0)
    sl = -np.where(t < r, 1.0, 0.0)
    hk = np.where(r < t, 1.0, 0.0)
    tb = -np.where(r <= t, 1.0, 0.0)
    tk = np.where(r <= t, 1.0, 0.0)
    out = np.concatenate([np.tile(m, (1, SUPD // 64)) for m in (su, sl, hk, tb, tk)], axis=1)
    return out.astype(np.float32)


def _host_inputs(inputs):
    MAXTOT = max(SLICES) + OV
    x = _f32(np.asarray(inputs["L0_attn_x"]).reshape(T_FULL, D))
    shared = {
        "x": x,
        "ident": np.eye(128, dtype=BF),
        "identf": np.eye(128, dtype=np.float32),
        "trimask": np.triu(np.ones((128, 128), np.float32)).astype(BF),
        "ln0g_in": _perpart(inputs["L0_attn_ln_g"]),
        "ln0b_in": _perpart(inputs["L0_attn_ln_b"]),
        "Whid": _bf(inputs["L0_attn_W_hidden"]),
        "bhid_in": _bf(np.asarray(inputs["L0_attn_b_hidden"]).reshape(1, 4096)),
        "Wqk": _bf(inputs["L0_attn_W_qk"]),
        "bqk_in": _f32(np.asarray(inputs["L0_attn_b_qk"]).reshape(128, 1)),
        "os_in": _f32(np.stack([np.asarray(inputs["L0_attn_os_gamma"])[0],
                                np.asarray(inputs["L0_attn_os_beta"])[0],
                                np.asarray(inputs["L0_attn_os_gamma"])[1],
                                np.asarray(inputs["L0_attn_os_beta"])[1]], axis=1)),
        "Wout": _bf(inputs["L0_attn_W_out"]),
        "bout_in": _bf(np.asarray(inputs["L0_attn_b_out"]).reshape(1, D)),
        "ones_in": np.ones((1, 128), BF),
        "wgT": _bf(np.asarray(inputs["L0_ffn_w_gate"]).T),
        "wuT": _bf(np.asarray(inputs["L0_ffn_w_up"]).T),
        "wdT": _bf(np.asarray(inputs["L0_ffn_w_down"]).T),
        "maa_w1T": _bf(np.asarray(inputs["L1_attn_maa_w1"]).T),
        "gate_w1T": _bf(np.asarray(inputs["L1_attn_gate_w1"]).T),
        "decay_w1T": _bf(np.asarray(inputs["L1_attn_decay_w1"]).T),
        "WrT": _bf(np.asarray(inputs["L1_attn_Wr"]).T),
        "WkT": _bf(np.asarray(inputs["L1_attn_Wk"]).T),
        "WvT": _bf(np.asarray(inputs["L1_attn_Wv"]).T),
        "lw1T_in": _bf(np.concatenate([np.asarray(inputs[f"L1_attn_{nm}_w1"]).T
                                       for nm in ("aaa", "kkk", "ma", "mk")], axis=0)),
        "maa_w2_in": _bf(np.concatenate(list(np.asarray(inputs["L1_attn_maa_w2"])), axis=1)),
        "gate_w2_in": _bf(inputs["L1_attn_gate_w2"]),
        "decay_w2_in": _bf(inputs["L1_attn_decay_w2"]),
        "lw2_in": _bf(np.concatenate([np.asarray(inputs[f"L1_attn_{nm}_w2"])
                                      for nm in ("aaa", "kkk", "ma", "mk")], axis=1)),
        "pvecs_in": np.concatenate([
            _perpart(inputs["L1_attn_time_maa_x"]),
            _perpart(inputs["L1_attn_time_decay"]),
            _perpart(inputs["L1_attn_time_aaaaa"]),
            _perpart(inputs["L1_attn_time_misc_a"]),
            _perpart(inputs["L1_attn_time_misc_k"])], axis=1),
        "tmaa_in": np.concatenate([_perpart(np.asarray(inputs["L1_attn_time_maa"])[g, 0, 0])
                                   for g in range(4)], axis=1),
        "npvecs_in": -np.concatenate([
            _perpart(inputs["L1_attn_time_maa_x"]),
            _perpart(inputs["L1_attn_time_decay"]),
            _perpart(inputs["L1_attn_time_aaaaa"]),
            _perpart(inputs["L1_attn_time_misc_a"]),
            _perpart(inputs["L1_attn_time_misc_k"])], axis=1),
        "bo16_in": _mk_bo16(),
        "sel16_in": _mk_sel16(),
        "masks_in": _mk_masks(),
        "eye64_in": np.concatenate([np.eye(64, dtype=np.float32)] * 2, axis=0),
        "WoT": _bf(np.asarray(inputs["L1_attn_Wo"]).T),
        "w1_in": _bf(inputs["L1_ffn_w1"]),
        "w2_in": _bf(inputs["L1_ffn_w2"]),
        "b2_in": _bf(np.asarray(inputs["L1_ffn_b2"]).reshape(1, D)),
        "lnwb_in": np.concatenate([_perpart(inputs["L1_attn_ln_w"]),
                                   _perpart(inputs["L1_attn_ln_b"]),
                                   _perpart(inputs["L1_attn_time_faaaa"])], axis=1),
        "b1_in": np.ascontiguousarray(
            _f32(inputs["L1_ffn_b1"]).reshape(32, 128).T),
    }
    inv = 1.0 / (np.arange(T_FULL, dtype=np.float32) + 1.0)
    in_maps = []
    for cc in range(N_CORES):
        start = STARTS[cc]
        pend = start + SLICES[cc]
        astart = max(0, start - OV)
        tot = pend - astart
        invd = np.zeros((128, MAXTOT), np.float32)
        invd[:, 0:tot] = np.broadcast_to(inv[astart:pend], (128, tot))
        m = dict(shared)
        m["invd_in"] = invd
        in_maps.append(m)
    return in_maps


def kernel(**inputs):
    in_maps = _host_inputs(inputs)
    nc = build_program()
    res = run_bass_kernel_spmd(nc, in_maps, core_ids=list(range(N_CORES)))
    outs = [res.results[cc]["y"][0:SLICES[cc], :] for cc in range(N_CORES)]
    return np.concatenate(outs, axis=0).reshape(1, T_FULL, D)


def _barrier(p, eng_names=("tensor", "vector", "scalar", "sync", "gpsimd")):
    nc, b = p.nc, p.b
    cts = [b[k] for k in ("s_pe", "s_dve", "s_sce", "s_ld", "s_gp", "s_x0", "s_x1",
                          "s_vt", "s_gt", "s_xh", "s_r0", "s_r1", "s_w0", "s_w1",
                          "s_w2a", "s_w2b", "s_cc")]
    for en in eng_names:
        eng = getattr(nc, en)
        for ct in cts:
            if ct.n:
                eng.wait_ge(ct.sem, ct.n)


def _phase_B(p, c, astart, pend, dbg):
    nc, b = p.nc, p.b
    pe, dv, sc = b["s_pe"], b["s_dve"], b["s_sce"]
    sv = [b["s_x0"], b["s_x1"]]
    sgt, sxh = b["s_r0"], b["s_r1"]
    sw1 = b["s_w1"]
    p.ph_sbuf("att", [128, 128 * 256], BF16)
    p.ph_sbuf("rl_sb", [128, 256], F32)
    p.ph_sbuf("vt2", [128, 2 * 2048], BF16)
    p.ph_sbuf("gt2", [128, 2 * 2048], BF16)
    p.ph_sbuf("og", [128, 2 * 2048], F32)
    p.ph_sbuf("ogT", [128, 2048], BF16)
    p.ph_sbuf("xh2", [128, 2 * D], F32)
    p.ph_sbuf("x1_sb", [128, D], F32)
    p.ph_sbuf("w_out", [128, 16 * 1024], BF16)
    p.ph_sbuf("b_out", [1, D], BF16)
    ps_o = [p.ph_psum("ps_o0", [128, 2048], F32), p.ph_psum("ps_o1", [128, 2048], F32)]
    _barrier(p)
    ld = b["s_ld"]
    for fc in range(16):
        ld.inc(nc.sync.dma_start(out=b["w_out"][:, fc * 1024:(fc + 1) * 1024],
                                 in_=b["Wout"][fc * 128:(fc + 1) * 128, :]), 16)
    ld.inc(nc.sync.dma_start(out=b["b_out"], in_=b["bout_in"]), 16)
    wout_ld = ld.n

    n_it = (pend - astart) // 128
    att, vt2, gt2, og, ogT = b["att"], b["vt2"], b["gt2"], b["og"], b["ogT"]
    prev = dict(vmm_pe=0, og_dv=0, x1_dv=0, ogT_done=(0, 0), wout_pe=0)
    for ig in range(ceil_div(n_it, 2)):
        t0 = astart // 128 + 2 * ig
        nt = min(2, n_it - 2 * ig)
        gis = [t0 + t for t in range(nt)]
        gimax = gis[-1]
        iloc0 = (t0 * 128) - astart
        nc.tensor.wait_ge(dv.sem, dv.n)
        nc.tensor.wait_ge(sc.sem, sc.n)
        nc.vector.wait_ge(pe.sem, pe.n)
        for j in range(gimax + 1):
            slot = (j % 2) * 256
            mm = nc.tensor.matmul(ps_o[1][:, slot:slot + nt * 128],
                                  b["kT"][:, j * 128:(j + 1) * 128],
                                  b["qf"][:, iloc0:iloc0 + nt * 128],
                                  start=True, stop=True, skip_group_check=True)
            pe.inc(mm)
            nc.vector.wait_ge(pe.sem, pe.n)
            nc.scalar.wait_ge(pe.sem, pe.n)
            nc.scalar.wait_ge(dv.sem, dv.n)
            sc.inc(nc.scalar.activation(b["rl_sb"][:, 0:nt * 128],
                                        ps_o[1][:, slot:slot + nt * 128], ACT.Relu))
            nc.vector.wait_ge(sc.sem, sc.n)
            dv.inc(nc.vector.tensor_mul(att[:, j * 256:j * 256 + nt * 128],
                                        b["rl_sb"][:, 0:nt * 128],
                                        ps_o[1][:, slot:slot + nt * 128]))
            for t in range(nt):
                if j == gis[t]:
                    dv.sw(nc.vector)
                    dv.inc(nc.vector.tensor_mul(
                        att[:, j * 256 + t * 128:j * 256 + (t + 1) * 128],
                        att[:, j * 256 + t * 128:j * 256 + (t + 1) * 128],
                        b["trim"]))
            if j >= 1:
                nc.tensor.wait_ge(dv.sem, dv.n - 1 if False else dv.n)
        att_done = dv.n
        nc.tensor.wait_ge(dv.sem, att_done)
        nc.sync.wait_ge(pe.sem, prev["vmm_pe"])
        vneed = {}
        sv[0].inc(nc.sync.dma_start(out=vt2[:, 0:2048], in_=b["scr_vt"][0:128, :]), 16)
        vneed[0] = sv[0].n
        for j in range(gimax + 1):
            par = j % 2
            if j + 1 <= gimax:
                if j >= 1:
                    nc.sync.wait_ge(pe.sem, prev["vj_pe"])
                npar = (j + 1) % 2
                sv[npar].inc(nc.sync.dma_start(
                    out=vt2[:, npar * 2048:(npar + 1) * 2048],
                    in_=b["scr_vt"][(j + 1) * 128:(j + 2) * 128, :]), 16)
                vneed[j + 1] = sv[npar].n
            nc.tensor.wait_ge(sv[par].sem, vneed[j])
            for t in range(nt):
                if j > gis[t]:
                    continue
                for q4 in range(4):
                    mm = nc.tensor.matmul(
                        ps_o[t][:, q4 * 512:(q4 + 1) * 512],
                        att[:, j * 256 + t * 128:j * 256 + (t + 1) * 128],
                        vt2[:, par * 2048 + q4 * 512:par * 2048 + (q4 + 1) * 512],
                        start=(j == 0), stop=(j == gis[t]),
                        skip_group_check=True)
            pe.inc(mm)
            prev["vj_pe"] = pe.n
        vmm_done = pe.n
        prev["vmm_pe"] = vmm_done
        nc.sync.wait_ge(dv.sem, prev["x1_dv"])
        for t in range(nt):
            loc = iloc0 + t * 128
            sgt.inc(nc.sync.dma_start(out=gt2[:, t * 2048:(t + 1) * 2048],
                                      in_=b["scr_gt"][loc:loc + 128, :]), 16)
            sxh.inc(nc.sync.dma_start(out=b["xh2"][:, t * D:(t + 1) * D],
                                      in_=b["scr_xh"][loc:loc + 128, :]), 16)
        for t in range(nt):
            loc = iloc0 + t * 128
            nc.vector.wait_ge(pe.sem, vmm_done)
            nc.vector.wait_ge(sgt.sem, sgt.n)
            for hh in range(2):
                dv.inc(nc.vector.scalar_tensor_tensor(
                    og[:, t * 2048 + hh * 1024:t * 2048 + (hh + 1) * 1024],
                    ps_o[t][:, hh * 1024:(hh + 1) * 1024], 1.0,
                    gt2[:, t * 2048 + hh * 1024:t * 2048 + (hh + 1) * 1024],
                    ALU.mult, ALU.mult))
        og_done = dv.n
        prev["og_dv"] = dv.n
        for t in range(nt):
            nc.tensor.wait_ge(dv.sem, og_done)
            if t > 0:
                nc.tensor.wait_ge(dv.sem, dv.n)
                nc.tensor.wait_ge(sc.sem, sc.n)
            for fc in range(16):
                grp = fc // 4
                off = grp * 512 + (fc % 4) * 128
                mm = nc.tensor.transpose(ps_o[1][:, off:off + 128],
                                         og[:, t * 2048 + fc * 128:t * 2048 + (fc + 1) * 128],
                                         b["idtf"])
                if fc == 15:
                    pe.inc(mm)
            tp_done = pe.n
            nc.vector.wait_ge(pe.sem, tp_done)
            nc.scalar.wait_ge(pe.sem, tp_done)
            for fc in range(16):
                grp = fc // 4
                off = grp * 512 + (fc % 4) * 128
                dst = ogT[:, fc * 128:(fc + 1) * 128]
                if grp < 2:
                    dv.inc(nc.vector.tensor_copy(dst, ps_o[1][:, off:off + 128]))
                else:
                    sc.inc(nc.scalar.copy(dst, ps_o[1][:, off:off + 128]))
            ev_dv, ev_sc = dv.n, sc.n
            nc.tensor.wait_ge(dv.sem, ev_dv)
            nc.tensor.wait_ge(sc.sem, ev_sc)
            nc.tensor.wait_ge(ld.sem, wout_ld)
            for fc in range(16):
                for hq in range(2):
                    nc.tensor.matmul(ps_o[0][:, hq * 512:(hq + 1) * 512],
                                     ogT[:, fc * 128:(fc + 1) * 128],
                                     b["w_out"][:, fc * 1024 + hq * 512:fc * 1024 + (hq + 1) * 512],
                                     start=(fc == 0), stop=False, skip_group_check=True)
            for hq in range(2):
                mm = nc.tensor.matmul(ps_o[0][:, hq * 512:(hq + 1) * 512],
                                      b["onesr"], b["b_out"][:, hq * 512:(hq + 1) * 512],
                                      start=False, stop=True, skip_group_check=True)
            pe.inc(mm)
            wout_done = pe.n
            nc.vector.wait_ge(pe.sem, wout_done)
            nc.vector.wait_ge(sxh.sem, sxh.n)
            if sw1.n:
                nc.vector.wait_ge(sw1.sem, sw1.n)
            dv.inc(nc.vector.scalar_tensor_tensor(
                b["x1_sb"], ps_o[0][:, 0:1024], 1.0, b["xh2"][:, t * D:(t + 1) * D],
                ALU.mult, ALU.add))
            prev["x1_dv"] = dv.n
            nc.gpsimd.wait_ge(dv.sem, dv.n)
            loc = iloc0 + t * 128
            sw1.inc(nc.gpsimd.dma_start(out=b["scr_x1"][loc:loc + 128, :],
                                        in_=b["x1_sb"]), 16)
    if p.tap == "B":
        tot = pend - astart
        _barrier(p, ("sync",))
        b["s_ld"].inc(nc.sync.dma_start(out=dbg[0:tot, 0:1024], in_=b["scr_x1"][0:tot, :]), 16)
        nc.sync.wait_ge(b["s_ld"].sem, b["s_ld"].n)


def _ln_tile(p, xsrc, tdst, extra_scale_col=None):
    nc, b = p.nc, p.b
    dv, sc = b["s_dve"], b["s_sce"]
    st = b["stats"]
    dv.inc(nc.vector.reduce_sum(st[:, 0:1], xsrc, AX.X))
    sc.inc(nc.scalar.activation(b["sq_junk"], xsrc, ACT.Square, accum_out=st[:, 1:2]))
    nc.vector.wait_ge(sc.sem, sc.n)
    dv.sw(nc.vector)
    dv.inc(nc.vector.tensor_scalar_mul(st[:, 2:3], st[:, 0:1], 1.0 / D))
    dv.inc(nc.vector.tensor_scalar_mul(st[:, 3:4], st[:, 1:2], 1.0 / D))
    dv.sw(nc.vector)
    dv.inc(nc.vector.tensor_mul(st[:, 4:5], st[:, 2:3], st[:, 2:3]))
    dv.sw(nc.vector)
    dv.inc(nc.vector.scalar_tensor_tensor(st[:, 5:6], st[:, 3:4], EPS_LN,
                                          st[:, 4:5], ALU.add, ALU.subtract))
    nc.scalar.wait_ge(dv.sem, dv.n)
    sc.inc(nc.scalar.activation(st[:, 6:7], st[:, 5:6], ACT.Sqrt))
    nc.vector.wait_ge(sc.sem, sc.n)
    dv.inc(nc.vector.reciprocal(st[:, 7:8], st[:, 6:7]))
    dv.sw(nc.vector)
    dv.inc(nc.vector.tensor_mul(st[:, 8:9], st[:, 2:3], st[:, 7:8]))
    dv.sw(nc.vector)
    dv.inc(nc.vector.tensor_scalar_mul(st[:, 9:10], st[:, 8:9], -1.0))
    nc.scalar.wait_ge(dv.sem, dv.n)
    sc.inc(nc.scalar.activation(tdst, xsrc, ACT.Identity,
                                bias=st[:, 9:10], scale=st[:, 7:8]))
    return sc.n


def _transpose_tile(p, tsrc, fm_dst_fn, pe_wait=None):
    nc, b = p.nc, p.b
    pe, dv, sc = b["s_pe"], b["s_dve"], b["s_sce"]
    ps_tp = b["ps_tp"]
    for dch in range(DCH):
        off = (dch % 4) * 128 + (dch // 4) * 1024
        mm = nc.tensor.transpose(ps_tp[:, off:off + 128],
                                 tsrc[:, dch * 128:(dch + 1) * 128], b["idt"])
        if dch == DCH - 1:
            pe.inc(mm)
    nc.vector.wait_ge(pe.sem, pe.n)
    nc.scalar.wait_ge(pe.sem, pe.n)
    for dch in range(DCH):
        off = (dch % 4) * 128 + (dch // 4) * 1024
        if dch < 4:
            dv.inc(nc.vector.tensor_copy(fm_dst_fn(dch), ps_tp[:, off:off + 128]))
        else:
            sc.inc(nc.scalar.copy(fm_dst_fn(dch), ps_tp[:, off:off + 128]))
    return dv.n, sc.n


def _phase_C(p, c, astart, pend, dbg):
    nc, b = p.nc, p.b
    pe, dv, sc = b["s_pe"], b["s_dve"], b["s_sce"]
    ld = b["s_ld"]
    sx = [b["s_x0"], b["s_x1"]]
    sw0 = b["s_w0"]
    tot = pend - astart
    p.ph_sbuf("xc", [128, 4 * D], F32)
    p.ph_sbuf("tbfC", [128, D], BF16)
    p.ph_sbuf("sq_junk", [128, D], BF16)
    p.ph_sbuf("h1T", [128, DCH * 512], BF16)
    p.ph_sbuf("hfm", [128, 24 * 512], BF16)
    p.ph_sbuf("sil", [128, 512], F32)
    p.ph_sbuf("x2_sb", [128, D], F32)
    p.ph_sbuf("w_g", [128, DCH * 3072], BF16)
    p.ph_sbuf("w_u", [128, DCH * 3072], BF16)
    p.ph_sbuf("w_d", [128, 24 * 1024], BF16)
    p.ph_psum("ps_c1", [128, 2048], F32)
    p.ph_psum("ps_tp", [128, 2048], BF16)
    p.ph_psum("ps_c2", [128, 1024], F32)
    _barrier(p)
    for dch in range(DCH):
        ld.inc(nc.sync.dma_start(out=b["w_g"][:, dch * 3072:(dch + 1) * 3072],
                                 in_=b["wgT"][dch * 128:(dch + 1) * 128, :]), 16)
        ld.inc(nc.sync.dma_start(out=b["w_u"][:, dch * 3072:(dch + 1) * 3072],
                                 in_=b["wuT"][dch * 128:(dch + 1) * 128, :]), 16)
    for hch in range(24):
        ld.inc(nc.sync.dma_start(out=b["w_d"][:, hch * 1024:(hch + 1) * 1024],
                                 in_=b["wdT"][hch * 128:(hch + 1) * 128, :]), 16)
    w_ld = ld.n
    SC = 1.0 / 127.0

    n_tiles = tot // 128
    prev = dict(x2_dv=0, hfm_dv=0, hfm_sc=0)
    for sup0 in range(0, n_tiles, 4):
        ntile = min(4, n_tiles - sup0)
        stoks = ntile * 128
        nc.sync.wait_ge(dv.sem, prev["x2_dv"])
        for ti in range(ntile):
            loc = (sup0 + ti) * 128
            sx[ti % 2].inc(nc.sync.dma_start(out=b["xc"][:, ti * D:(ti + 1) * D],
                                             in_=b["scr_x1"][loc:loc + 128, :]), 16)
        xld = [sx[ti % 2].n for ti in range(ntile)]
        for ti in range(ntile):
            nc.vector.wait_ge(sx[ti % 2].sem, xld[ti])
            nc.scalar.wait_ge(sx[ti % 2].sem, xld[ti])
            if ti == 0:
                nc.tensor.wait_ge(dv.sem, prev["hfm_dv"])
                nc.tensor.wait_ge(sc.sem, prev["hfm_sc"])
            _ln_tile(p, b["xc"][:, ti * D:(ti + 1) * D], b["tbfC"])
            nc.tensor.wait_ge(sc.sem, sc.n)
            nc.tensor.wait_ge(dv.sem, dv.n)
            _transpose_tile(p, b["tbfC"],
                            lambda dch, _ti=ti: b["h1T"][:, dch * 512 + _ti * 128:
                                                         dch * 512 + (_ti + 1) * 128])
        h1_dv, h1_sc = dv.n, sc.n
        nc.tensor.wait_ge(dv.sem, h1_dv)
        nc.tensor.wait_ge(sc.sem, h1_sc)
        nc.tensor.wait_ge(ld.sem, w_ld)
        ev_after = {}
        for hch in range(24):
            par = hch % 2
            gsl = slice(par * 1024, par * 1024 + 512)
            usl = slice(par * 1024 + 512, par * 1024 + 1024)
            if hch >= 2:
                nc.tensor.wait_ge(dv.sem, ev_after[hch - 2])
            for dch in range(DCH):
                nc.tensor.matmul(b["ps_c1"][:, gsl][:, 0:stoks],
                                 b["w_g"][:, dch * 3072 + hch * 128:dch * 3072 + (hch + 1) * 128],
                                 b["h1T"][:, dch * 512:dch * 512 + stoks],
                                 start=(dch == 0), stop=(dch == DCH - 1),
                                 skip_group_check=True)
            for dch in range(DCH):
                mm = nc.tensor.matmul(b["ps_c1"][:, usl][:, 0:stoks],
                                 b["w_u"][:, dch * 3072 + hch * 128:dch * 3072 + (hch + 1) * 128],
                                 b["h1T"][:, dch * 512:dch * 512 + stoks],
                                 start=(dch == 0), stop=(dch == DCH - 1),
                                 skip_group_check=True)
            pe.inc(mm)
            nc.scalar.wait_ge(pe.sem, pe.n)
            nc.scalar.wait_ge(dv.sem, dv.n)
            sc.inc(nc.scalar.activation(b["sil"][:, 0:stoks], b["ps_c1"][:, gsl][:, 0:stoks],
                                        ACT.Silu, scale=SC))
            nc.vector.wait_ge(sc.sem, sc.n)
            dv.inc(nc.vector.scalar_tensor_tensor(
                b["hfm"][:, hch * 512:hch * 512 + stoks],
                b["sil"][:, 0:stoks], SC,
                b["ps_c1"][:, usl][:, 0:stoks], ALU.mult, ALU.mult))
            ev_after[hch] = dv.n
        prev["hfm_dv"], prev["hfm_sc"] = dv.n, sc.n
        hfm_done = dv.n
        nc.tensor.wait_ge(dv.sem, hfm_done)
        for ti in range(ntile):
            nc.tensor.wait_ge(dv.sem, dv.n)
            for hch in range(24):
                for hq in range(2):
                    mm = nc.tensor.matmul(b["ps_c2"][:, hq * 512:(hq + 1) * 512],
                                          b["hfm"][:, hch * 512 + ti * 128:hch * 512 + (ti + 1) * 128],
                                          b["w_d"][:, hch * 1024 + hq * 512:hch * 1024 + (hq + 1) * 512],
                                          start=(hch == 0), stop=(hch == 23),
                                          skip_group_check=True)
            pe.inc(mm)
            nc.vector.wait_ge(pe.sem, pe.n)
            if b["s_w0"].n:
                nc.vector.wait_ge(sw0.sem, sw0.n)
            dv.inc(nc.vector.scalar_tensor_tensor(
                b["x2_sb"], b["ps_c2"][:], SC,
                b["xc"][:, ti * D:(ti + 1) * D], ALU.mult, ALU.add))
            prev["x2_dv"] = dv.n
            nc.gpsimd.wait_ge(dv.sem, dv.n)
            loc = (sup0 + ti) * 128
            sw0.inc(nc.gpsimd.dma_start(out=b["scr_x2"][loc:loc + 128, :],
                                        in_=b["x2_sb"]), 16)
    if p.tap == "C":
        _barrier(p, ("sync",))
        ld.inc(nc.sync.dma_start(out=dbg[0:tot, 0:1024], in_=b["scr_x2"][0:tot, :]), 16)
        nc.sync.wait_ge(ld.sem, ld.n)


BETA = 0.6065306597126334


def _phase_D(p, c, astart, pend, dbg):
    nc, b = p.nc, p.b
    pe, dv, sc = b["s_pe"], b["s_dve"], b["s_sce"]
    ld = b["s_ld"]
    sx = [b["s_x0"], b["s_x1"]]
    sw0 = b["s_w0"]
    start = STARTS[c]
    n = pend - start
    n_sup = ceil_div(n, SUPD)
    T2 = SUPD + 8

    p.ph_sbuf("maa1T", [128, DCH * 128], BF16)
    p.ph_sbuf("maa2", [32, 4 * D], BF16)
    p.ph_sbuf("gw1T", [128, DCH * 128], BF16)
    p.ph_sbuf("gw2", [128, D], BF16)
    p.ph_sbuf("dw1T", [128, DCH * 64], BF16)
    p.ph_sbuf("dw2", [64, D], BF16)
    p.ph_sbuf("lw1T", [128, 4 * DCH * 16], BF16)
    p.ph_sbuf("lw2", [16, 4 * D], BF16)
    p.ph_sbuf("wproj", [128, DCH * D], BF16)
    p.ph_sbuf("pvecs", [128, 40], F32)
    p.ph_sbuf("npvecs", [128, 40], F32)
    p.ph_sbuf("tmaa", [128, 4 * DCH], F32)
    p.ph_sbuf("masks", [128, 5 * SUPD], F32)
    p.ph_sbuf("eye64", [128, 64], F32)

    p.ph_sbuf("xc", [128, 2 * D], F32)
    p.ph_sbuf("tbfD", [128, D], BF16)
    p.ph_sbuf("sq_junk", [128, D], BF16)
    p.ph_sbuf("h2T", [128, DCH * T2], BF16)
    p.ph_sbuf("xxT", [128, DCH * SUPD], BF16)
    p.ph_sbuf("lorg", [32, 4 * SUPD], BF16)
    p.ph_sbuf("xmT", [128, DCH * SUPD], BF16)
    p.ph_sbuf("lsm", [16, SUPD], BF16)
    p.ph_sbuf("ldm", [64, SUPD], BF16)
    p.ph_sbuf("r_fm", [128, DCH * SUPD], BF16)
    p.ph_sbuf("k_fm", [128, DCH * SUPD], BF16)
    p.ph_sbuf("v_fm", [128, DCH * SUPD], BF16)
    p.ph_sbuf("kk_fm", [128, DCH * SUPD], BF16)
    p.ph_sbuf("sig", [128, DCH * SUPD], F32)
    p.ph_sbuf("spl", [128, DCH * SUPD], F32)
    p.ph_sbuf("ag_fm", [128, DCH * SUPD], BF16)
    p.ph_sbuf("ma_fm", [128, DCH * SUPD], BF16)
    p.ph_sbuf("mk_fm", [128, DCH * SUPD], BF16)
    p.ph_sbuf("cum", [128, DCH * T2], F32)
    p.ph_sbuf("cuml", [128, DCH], F32)
    p.ph_sbuf("dcs", [128, DCH * 8], F32)
    p.ph_sbuf("dcsr", [128, DCH * 8], F32)
    p.ph_sbuf("qt_fm", [128, DCH * SUPD], BF16)
    p.ph_sbuf("at_fm", [128, DCH * SUPD], BF16)
    p.ph_sbuf("bt_fm", [128, DCH * SUPD], BF16)
    p.ph_sbuf("bh_fm", [128, DCH * SUPD], BF16)
    p.ph_sbuf("kt_fm", [128, DCH * SUPD], BF16)
    p.ph_sbuf("kh_fm", [128, DCH * SUPD], BF16)
    p.ph_sbuf("tmp1", [128, SUPD], F32)
    p.ph_sbuf("tmp2", [128, SUPD], F32)
    p.ph_sbuf("tmp3", [128, SUPD], F32)
    p.ph_sbuf("tb1", [128, SUPD], BF16)
    p.ph_sbuf("tb2", [128, SUPD], BF16)
    p.ph_sbuf("gmid", [128, SUPD], BF16)
    p.ph_sbuf("g_fm", [128, DCH * SUPD], BF16)
    p.ph_sbuf("ksqb", [128, SUPD], BF16)
    p.ph_sbuf("kns", [16, 2 * SUPD], F32)
    p.ph_sbuf("knsb", [16, SUPD], BF16)
    p.ph_sbuf("vtm", [128, SUPD], BF16)
    p.ph_sbuf("bhtm", [128, SUPD], BF16)
    p.ph_sbuf("khtm", [128, SUPD], BF16)
    for nm in ("Gm", "GTm", "G2m", "G2Tm", "G4m", "G4Tm", "G8m", "G8Tm",
               "G16m", "G16Tm", "G32m", "Hkm"):
        p.ph_sbuf(nm, [128, SUPD], BF16)
    p.ph_sbuf("Abm", [128, SUPD], BF16)
    p.ph_sbuf("Akm", [128, SUPD], BF16)
    p.ph_sbuf("rhsb", [128, 128], BF16)
    p.ph_sbuf("saA", [128, 128], BF16)
    p.ph_sbuf("saB", [128, 128], BF16)
    p.ph_sbuf("zh_sb", [128, SUPD], BF16)
    p.ph_sbuf("yl_sb", [128, SUPD], F32)
    p.ph_sbuf("state", [128, 8 * 128], F32)
    p.ph_sbuf("stateb", [128, 8 * 128], BF16)

    p.ph_psum("ps_da", [128, 2048], F32)
    p.ph_psum("ps_db", [128, 1024], F32)
    p.ph_psum("ps_tp", [128, 2048], BF16)

    _barrier(p)
    sy = nc.sync
    for dch in range(DCH):
        ld.inc(sy.dma_start(out=b["maa1T"][:, dch * 128:(dch + 1) * 128],
                            in_=b["maa_w1T"][dch * 128:(dch + 1) * 128, :]), 16)
        ld.inc(sy.dma_start(out=b["gw1T"][:, dch * 128:(dch + 1) * 128],
                            in_=b["gate_w1T"][dch * 128:(dch + 1) * 128, :]), 16)
        ld.inc(sy.dma_start(out=b["dw1T"][:, dch * 64:(dch + 1) * 64],
                            in_=b["decay_w1T"][dch * 128:(dch + 1) * 128, :]), 16)
        for li in range(4):
            ld.inc(sy.dma_start(
                out=b["lw1T"][:, li * DCH * 16 + dch * 16:li * DCH * 16 + (dch + 1) * 16],
                in_=b["lw1T_in"][li * D + dch * 128:li * D + (dch + 1) * 128, :]), 16)
    ld.inc(sy.dma_start(out=b["maa2"], in_=b["maa_w2_in"]), 16)
    ld.inc(sy.dma_start(out=b["gw2"], in_=b["gate_w2_in"]), 16)
    ld.inc(sy.dma_start(out=b["dw2"], in_=b["decay_w2_in"]), 16)
    ld.inc(sy.dma_start(out=b["lw2"], in_=b["lw2_in"]), 16)
    ld.inc(sy.dma_start(out=b["pvecs"], in_=b["pvecs_in"]), 16)
    ld.inc(sy.dma_start(out=b["npvecs"], in_=b["npvecs_in"]), 16)
    ld.inc(sy.dma_start(out=b["tmaa"], in_=b["tmaa_in"]), 16)
    ld.inc(sy.dma_start(out=b["bo16"], in_=b["bo16_in"]), 16)
    ld.inc(sy.dma_start(out=b["sel16"], in_=b["sel16_in"]), 16)
    ld.inc(sy.dma_start(out=b["masks"], in_=b["masks_in"]), 16)
    ld.inc(sy.dma_start(out=b["eye64"], in_=b["eye64_in"]), 16)
    w_ld = ld.n

    dv = b["s_dve"]
    nc.vector.wait_ge(ld.sem, w_ld)
    for pp in range(8):
        dv.inc(nc.vector.memset(b["state"][:, pp * 128:pp * 128 + 64], 0.0))
        dv.inc(nc.vector.tensor_copy(b["state"][:, pp * 128 + 64:pp * 128 + 128],
                                     b["eye64"]))
        dv.inc(nc.vector.memset(b["stateb"][:, pp * 128:pp * 128 + 64], 0.0))
        dv.inc(nc.vector.tensor_copy(b["stateb"][:, pp * 128 + 64:pp * 128 + 128],
                                     b["eye64"]))
    dv.inc(nc.vector.memset(b["cuml"], 0.0))
    if c == 0:
        for dch in range(DCH):
            dv.inc(nc.vector.memset(b["h2T"][:, dch * T2:dch * T2 + 1], 0.0))
        bnd_dv, bnd_sc = dv.n, sc.n
    else:
        sx[0].inc(nc.sync.dma_start(out=b["xc"][:, 0:D],
                                    in_=b["scr_x2"][OV - 128:OV, :]), 16)
        nc.vector.wait_ge(sx[0].sem, sx[0].n)
        nc.scalar.wait_ge(sx[0].sem, sx[0].n)
        _ln_tile(p, b["xc"][:, 0:D], b["tbfD"])
        nc.tensor.wait_ge(sc.sem, sc.n)
        _transpose_tile(p, b["tbfD"],
                        lambda dch: b["xxT"][:, dch * SUPD:dch * SUPD + 128])
        nc.vector.wait_ge(sc.sem, sc.n)
        dv.sw(nc.vector)
        for dch in range(DCH):
            dv.inc(nc.vector.tensor_copy(b["h2T"][:, dch * T2:dch * T2 + 1],
                                         b["xxT"][:, dch * SUPD + 127:dch * SUPD + 128]))
        bnd_dv, bnd_sc = dv.n, sc.n

    nc.tensor.wait_ge(dv.sem, dv.n)
    import os as _os9
    if _os9.environ.get("D_NSUP"):
        n_sup = min(n_sup, int(_os9.environ["D_NSUP"]))
    for sup in range(n_sup):
        ts = sup * SUPD
        S = min(SUPD, n - ts)
        ntile = S // 128
        nch = S // CHUNK
        _D_dense(p, c, sup, ts, S, ntile, nch, T2)
        _D_chunks(p, c, sup, ts, S, nch, T2)
    gp = b["s_gp"]
    _W(nc.gpsimd, dv)
    gp.inc(nc.gpsimd.dma_start(out=b["cc_in"][:, :], in_=b["state"]), 16)
    if p.tap == "D":
        _tap_D(p, c, dbg, n)


def _W(eng, *cts):
    for ct in cts:
        if ct.n:
            eng.wait_ge(ct.sem, ct.n)


def _D_dense(p, c, sup, ts, S, ntile, nch, T2):
    nc, b = p.nc, p.b
    pe, dv, sc = b["s_pe"], b["s_dve"], b["s_sce"]
    ld = b["s_ld"]
    sx = [b["s_x0"], b["s_x1"]]
    sw0 = b["s_w0"]
    OWOFF = STARTS[c] - max(0, STARTS[c] - OV)
    MS = p.MAXSLICE
    ps_a, ps_b = b["ps_da"], b["ps_db"]
    V = nc.vector
    SCE = nc.scalar
    TEN = nc.tensor

    if sup > 0:
        _W(V, sc, pe)
        dv.sw(V)
        h2v = b["h2T"].rearrange("q (ch t) -> q ch t", t=T2)
        dv.inc(V.tensor_copy(h2v[:, :, 0:1], h2v[:, :, SUPD:SUPD + 1]))

    for ti in range(ntile):
        loc = OWOFF + ts + ti * 128
        par = ti % 2
        _W(nc.sync, dv, sc)
        sx[par].inc(nc.sync.dma_start(out=b["xc"][:, ti * D:(ti + 1) * D],
                                      in_=b["scr_x2"][loc:loc + 128, :]), 16)
        _W(V, sx[par]); _W(SCE, sx[par])
        _W(SCE, pe)
        _ln_tile(p, b["xc"][:, ti * D:(ti + 1) * D], b["tbfD"])
        _W(TEN, sc, dv)
        _transpose_tile(p, b["tbfD"],
                        lambda dch, _ti=ti: b["h2T"][:, dch * T2 + 1 + _ti * 128:
                                                     dch * T2 + 1 + (_ti + 1) * 128])
    h2_dv, h2_sc = dv.n, sc.n

    _W(V, sc)
    dv.sw(V)
    for dch in range(DCH):
        cur = b["h2T"][:, dch * T2 + 1:dch * T2 + 1 + S]
        prv = b["h2T"][:, dch * T2:dch * T2 + S]
        dv.inc(V.tensor_sub(b["xxT"][:, dch * SUPD:dch * SUPD + S], prv, cur))
        dv.sw(V)
        dv.inc(V.scalar_tensor_tensor(b["xmT"][:, dch * SUPD:dch * SUPD + S],
                                      b["xxT"][:, dch * SUPD:dch * SUPD + S],
                                      b["pvecs"][:, dch:dch + 1], cur,
                                      ALU.mult, ALU.add))
    mix_dv = dv.n

    _W(TEN, dv)
    for dch in range(DCH):
        mm = TEN.matmul(ps_a[:, 0:S], b["maa1T"][:, dch * 128:(dch + 1) * 128],
                        b["xmT"][:, dch * SUPD:dch * SUPD + S],
                        start=(dch == 0), stop=(dch == DCH - 1), skip_group_check=True)
    pe.inc(mm)
    _W(SCE, pe)
    for g in range(4):
        sc.inc(SCE.activation(b["lorg"][:, g * SUPD:g * SUPD + S],
                              ps_a[g * 32:(g + 1) * 32, 0:S], ACT.Tanh))
    lorg_sc = sc.n

    sr = b["s_r0"]

    def load_wproj(src_name):
        _W(nc.sync, pe)
        for dch in range(DCH):
            sr.inc(nc.sync.dma_start(out=b["wproj"][:, dch * D:(dch + 1) * D],
                                     in_=b[src_name][dch * 128:(dch + 1) * 128, :]), 16)
        _W(TEN, sr)

    def proj(dst_fm, wT, fch, extra=None):
        for dch in range(DCH):
            mm = TEN.matmul(ps_a[:, 512:512 + S],
                            wT[:, dch * D + fch * 128:dch * D + (fch + 1) * 128],
                            b["xmT"][:, dch * SUPD:dch * SUPD + S],
                            start=(dch == 0), stop=(dch == DCH - 1),
                            skip_group_check=True)
        pe.inc(mm)

    def lora_small(li, k_dim, midbuf, mid_act, w1T_all, w2, outbuf, out_act, bias_col):
        for dch in range(DCH):
            mm = TEN.matmul(ps_a[0:k_dim, 1024:1024 + S],
                            w1T_all(dch),
                            b["xmT"][:, dch * SUPD:dch * SUPD + S],
                            start=(dch == 0), stop=(dch == DCH - 1),
                            skip_group_check=True)
        pe.inc(mm)
        _W(SCE, pe, dv)
        if mid_act is None:
            sc.inc(SCE.copy(midbuf[0:k_dim, 0:S], ps_a[0:k_dim, 1024:1024 + S]))
        else:
            sc.inc(SCE.activation(midbuf[0:k_dim, 0:S], ps_a[0:k_dim, 1024:1024 + S],
                                  mid_act))
        _W(TEN, sc)
        for fch in range(DCH):
            mm = TEN.matmul(ps_a[:, 1536:1536 + S], w2(fch), midbuf[0:k_dim, 0:S],
                            start=True, stop=True, skip_group_check=True)
            pe.inc(mm)
            _W(SCE, pe)
            if out_act is not None:
                sc.inc(SCE.activation(outbuf(fch), ps_a[:, 1536:1536 + S], out_act,
                                      bias=bias_col(fch)))
            else:
                sc.inc(SCE.activation(outbuf(fch), ps_a[:, 1536:1536 + S], ACT.Identity,
                                      bias=bias_col(fch)))
            _W(TEN, sc)
        return sc.n

    for g in range(4):
        _W(TEN, dv, sc)
        for dch in range(DCH):
            mm = TEN.matmul(ps_a[:, 512 + (dch % 2) * SUPD:512 + (dch % 2) * SUPD + S]
                            if False else ps_a[:, 512:512 + S],
                            b["maa2"][:, g * D + dch * 128:g * D + (dch + 1) * 128],
                            b["lorg"][:, g * SUPD:g * SUPD + S],
                            start=True, stop=True, skip_group_check=True)
            pe.inc(mm)
            _W(V, pe)
            dv.sw(V)
            dv.inc(V.scalar_tensor_tensor(b["tmp1"][:, 0:S], ps_a[:, 512:512 + S],
                                          b["tmaa"][:, g * DCH + dch:g * DCH + dch + 1],
                                          b["xxT"][:, dch * SUPD:dch * SUPD + S],
                                          ALU.add, ALU.mult))
            dv.sw(V)
            dv.inc(V.tensor_add(b["xmT"][:, dch * SUPD:dch * SUPD + S],
                                b["tmp1"][:, 0:S],
                                b["h2T"][:, dch * T2 + 1:dch * T2 + 1 + S]))
            _W(TEN, dv)
        if g == 0:
            load_wproj("WrT")
            for fch in range(DCH):
                proj(None, b["wproj"], fch)
                _W(SCE, pe)
                sc.inc(SCE.copy(b["r_fm"][:, fch * SUPD:fch * SUPD + S], ps_a[:, 512:512 + S]))
                _W(TEN, sc)
            for dch in range(DCH):
                mm = TEN.matmul(ps_a[:, 1024:1024 + S],
                                b["gw1T"][:, dch * 128:(dch + 1) * 128],
                                b["xmT"][:, dch * SUPD:dch * SUPD + S],
                                start=(dch == 0), stop=(dch == DCH - 1),
                                skip_group_check=True)
            pe.inc(mm)
            _W(SCE, pe)
            sc.inc(SCE.activation(b["gmid"][:, 0:S], ps_a[:, 1024:1024 + S], ACT.Tanh))
            _W(TEN, sc)
            for fch in range(DCH):
                mm = TEN.matmul(ps_a[:, 1536:1536 + S],
                                b["gw2"][:, fch * 128:(fch + 1) * 128],
                                b["gmid"][:, 0:S],
                                start=True, stop=True, skip_group_check=True)
                pe.inc(mm)
                _W(SCE, pe)
                sc.inc(SCE.copy(b["g_fm"][:, fch * SUPD:fch * SUPD + S], ps_a[:, 1536:1536 + S]))
                _W(TEN, sc)
            _W(nc.gpsimd, sc)
            _W(nc.gpsimd, sw0)
            rv = b["r_fm"].rearrange("q (ch t) -> q ch t", t=SUPD)[:, :, 0:S]
            gv = b["g_fm"].rearrange("q (ch t) -> q ch t", t=SUPD)[:, :, 0:S]
            sw0.inc(nc.gpsimd.dma_start(
                out=b["scr_r"].rearrange("q (ch t) -> q ch t", t=MS)[:, :, ts:ts + S], in_=rv), 16)
            sw0.inc(nc.gpsimd.dma_start(
                out=b["scr_g"].rearrange("q (ch t) -> q ch t", t=MS)[:, :, ts:ts + S], in_=gv), 16)
        elif g == 1:
            for dch in range(DCH):
                mm = TEN.matmul(ps_a[0:64, 1024:1024 + S],
                                b["dw1T"][:, dch * 64:(dch + 1) * 64],
                                b["xmT"][:, dch * SUPD:dch * SUPD + S],
                                start=(dch == 0), stop=(dch == DCH - 1),
                                skip_group_check=True)
            pe.inc(mm)
            _W(SCE, pe)
            sc.inc(SCE.activation(b["ldm"][:, 0:S], ps_a[0:64, 1024:1024 + S], ACT.Tanh))
            _W(TEN, sc)
            for fch in range(DCH):
                mm = TEN.matmul(ps_a[:, 1536:1536 + S],
                                b["dw2"][:, fch * 128:(fch + 1) * 128],
                                b["ldm"][0:64, 0:S],
                                start=True, stop=True, skip_group_check=True)
                pe.inc(mm)
                _W(SCE, pe)
                sc.inc(SCE.activation(b["sig"][:, fch * SUPD:fch * SUPD + S],
                                      ps_a[:, 1536:1536 + S], ACT.Sigmoid,
                                      bias=b["pvecs"][:, 8 + fch:9 + fch]))
                sc.sw(SCE)
                sc.inc(SCE.activation(b["spl"][:, fch * SUPD:fch * SUPD + S],
                                      ps_a[:, 1536:1536 + S], ACT.Identity,
                                      bias=b["pvecs"][:, 8 + fch:9 + fch]))
                _W(TEN, sc)
            lora_small(0, 16, b["lsm"], None,
                       lambda dch: b["lw1T"][:, 0 * DCH * 16 + dch * 16:0 * DCH * 16 + (dch + 1) * 16],
                       lambda fch: b["lw2"][:, 0 * D + fch * 128:0 * D + (fch + 1) * 128],
                       lambda fch: b["ag_fm"][:, fch * SUPD:fch * SUPD + S], ACT.Sigmoid,
                       lambda fch: b["pvecs"][:, 16 + fch:17 + fch])
            lora_small(2, 16, b["lsm"], None,
                       lambda dch: b["lw1T"][:, 2 * DCH * 16 + dch * 16:2 * DCH * 16 + (dch + 1) * 16],
                       lambda fch: b["lw2"][:, 2 * D + fch * 128:2 * D + (fch + 1) * 128],
                       lambda fch: b["ma_fm"][:, fch * SUPD:fch * SUPD + S], ACT.Sigmoid,
                       lambda fch: b["pvecs"][:, 24 + fch:25 + fch])
        elif g == 2:
            load_wproj("WkT")
            for fch in range(DCH):
                proj(None, b["wproj"], fch)
                _W(SCE, pe)
                sc.inc(SCE.copy(b["k_fm"][:, fch * SUPD:fch * SUPD + S], ps_a[:, 512:512 + S]))
                _W(TEN, sc)
            for dch in range(DCH):
                mm = TEN.matmul(ps_a[0:16, 1024:1024 + S],
                                b["lw1T"][:, 1 * DCH * 16 + dch * 16:1 * DCH * 16 + (dch + 1) * 16],
                                b["xmT"][:, dch * SUPD:dch * SUPD + S],
                                start=(dch == 0), stop=(dch == DCH - 1),
                                skip_group_check=True)
            pe.inc(mm)
            _W(SCE, pe)
            sc.inc(SCE.activation(b["lsm"][0:16, 0:S], ps_a[0:16, 1024:1024 + S], ACT.Tanh))
            _W(TEN, sc)
            for fch in range(DCH):
                mm = TEN.matmul(ps_a[:, 1536:1536 + S],
                                b["lw2"][:, 1 * D + fch * 128:1 * D + (fch + 1) * 128],
                                b["lsm"][0:16, 0:S],
                                start=True, stop=True, skip_group_check=True)
                pe.inc(mm)
                _W(V, pe, sc)
                dv.sw(V)
                dv.inc(V.scalar_tensor_tensor(b["kk_fm"][:, fch * SUPD:fch * SUPD + S],
                                              ps_a[:, 1536:1536 + S], 1.0,
                                              b["k_fm"][:, fch * SUPD:fch * SUPD + S],
                                              ALU.mult, ALU.add))
                _W(TEN, dv)
            lora_small(3, 16, b["lsm"], None,
                       lambda dch: b["lw1T"][:, 3 * DCH * 16 + dch * 16:3 * DCH * 16 + (dch + 1) * 16],
                       lambda fch: b["lw2"][:, 3 * D + fch * 128:3 * D + (fch + 1) * 128],
                       lambda fch: b["mk_fm"][:, fch * SUPD:fch * SUPD + S], ACT.Sigmoid,
                       lambda fch: b["pvecs"][:, 32 + fch:33 + fch])
        else:
            load_wproj("WvT")
            for fch in range(DCH):
                proj(None, b["wproj"], fch)
                _W(SCE, pe)
                sc.inc(SCE.copy(b["v_fm"][:, fch * SUPD:fch * SUPD + S], ps_a[:, 512:512 + S]))
                _W(TEN, sc)
            _W(nc.gpsimd, sc, sw0)
            vv = b["v_fm"].rearrange("q (ch t) -> q ch t", t=SUPD)[:, :, 0:S]
            sw0.inc(nc.gpsimd.dma_start(
                out=b["scr_v1"].rearrange("q (ch t) -> q ch t", t=MS)[:, :, ts:ts + S], in_=vv), 16)

    _W(SCE, dv)
    for pp in range(DCH):
        _W(SCE, pe) if pp else None
        sc.inc(SCE.activation(b["ksqb"][:, 0:S], b["kk_fm"][:, pp * SUPD:pp * SUPD + S],
                              ACT.Square))
        _W(TEN, sc)
        mm = TEN.matmul(ps_b[0:16, 0:S], b["bo16"][:, pp * 16:(pp + 1) * 16],
                        b["ksqb"][:, 0:S],
                        start=(pp == 0), stop=(pp == DCH - 1), skip_group_check=True)
        pe.inc(mm)
    _W(V, pe)
    dv.inc(V.tensor_scalar_max(b["kns"][0:16, 0:S], ps_b[0:16, 0:S], 1e-24))
    _W(SCE, dv)
    sc.inc(SCE.activation(b["kns"][0:16, SUPD:SUPD + S], b["kns"][0:16, 0:S], ACT.Ln))
    sc.sw(SCE)
    sc.inc(SCE.activation(b["knsb"][0:16, 0:S], b["kns"][0:16, SUPD:SUPD + S], ACT.Exp,
                          scale=-0.5))
    rn_sc = sc.n
    for pp in range(DCH):
        _W(TEN, sc)
        if pp:
            _W(TEN, dv)
        mm = TEN.matmul(ps_b[:, 512:512 + S], b["sel16"][:, pp * 128:(pp + 1) * 128],
                        b["knsb"][0:16, 0:S],
                        start=True, stop=True, skip_group_check=True)
        pe.inc(mm)
        _W(V, pe)
        dv.sw(V)
        dv.inc(V.tensor_mul(b["kk_fm"][:, pp * SUPD:pp * SUPD + S],
                            b["kk_fm"][:, pp * SUPD:pp * SUPD + S], ps_b[:, 512:512 + S]))
    kk_dv = dv.n

    _W(V, sc)
    dv.sw(V)
    cumv = b["cum"].rearrange("q (ch t) -> q ch t", t=T2)
    dv.inc(V.tensor_copy(cumv[:, :, 0:1],
                         b["cuml"].rearrange("q ch -> q ch ()")))
    dv.sw(V)
    for dch in range(DCH):
        dv.inc(V.tensor_tensor_scan(b["cum"][:, dch * T2 + 1:dch * T2 + 1 + S],
                                    b["sig"][:, dch * SUPD:dch * SUPD + S],
                                    b["sig"][:, dch * SUPD:dch * SUPD + S],
                                    b["cum"][:, dch * T2:dch * T2 + 1],
                                    ALU.add, ALU.bypass))
        dv.sw(V)
    dv.inc(V.tensor_copy(b["cuml"].rearrange("q ch -> q ch ()"),
                         cumv[:, :, S:S + 1]))
    dv.sw(V)
    for dch in range(DCH):
        ends = b["cum"][:, dch * T2 + 64: dch * T2 + 64 + 64 * (nch - 1) + 1: 64]
        starts = b["cum"][:, dch * T2: dch * T2 + 64 * (nch - 1) + 1: 64]
        dv.inc(V.tensor_sub(b["dcsr"][:, dch * 8:dch * 8 + nch], ends, starts))
    _W(SCE, dv)
    dview = b["dcsr"].rearrange("q (ch k) -> q ch k", k=8)[:, :, 0:nch]
    dview_o = b["dcs"].rearrange("q (ch k) -> q ch k", k=8)[:, :, 0:nch]
    sc.inc(SCE.activation(dview_o, dview, ACT.Exp, scale=-BETA))
    dcs_sc = sc.n

    for dch in range(DCH):
        sl = slice(dch * SUPD, dch * SUPD + S)
        cur = b["cum"][:, dch * T2 + 1:dch * T2 + 1 + S].rearrange(
            "q (ci k) -> q ci k", k=64)
        prv = b["cum"][:, dch * T2:dch * T2 + S].rearrange("q (ci k) -> q ci k", k=64)
        starts = b["cum"][:, dch * T2: dch * T2 + 64 * (nch - 1) + 1: 64].rearrange(
            "q ci -> q ci ()").broadcast_to((128, nch, 64))
        ends = b["cum"][:, dch * T2 + 64: dch * T2 + 64 + 64 * (nch - 1) + 1: 64].rearrange(
            "q ci -> q ci ()").broadcast_to((128, nch, 64))
        t1v = b["tmp1"][:, 0:S].rearrange("q (ci k) -> q ci k", k=64)
        _W(V, sc, pe)
        dv.sw(V)
        dv.inc(V.tensor_mul(b["tb1"][:, 0:S], b["ma_fm"][:, sl], b["ag_fm"][:, sl]))
        dv.sw(V)
        dv.inc(V.scalar_tensor_tensor(b["tb2"][:, 0:S], b["tb1"][:, 0:S], -1.0,
                                      b["ag_fm"][:, sl], ALU.mult, ALU.add))
        dv.sw(V)
        dv.inc(V.tensor_add(b["tb1"][:, 0:S], b["tb2"][:, 0:S], b["ma_fm"][:, sl]))
        _W(V, sc)
        dv.sw(V)
        dv.inc(V.scalar_tensor_tensor(b["tmp2"][:, 0:S], b["spl"][:, sl], -1.0,
                                      b["spl"][:, sl], ALU.mult, ALU.min))
        _W(SCE, dv)
        sc.inc(SCE.activation(b["tmp2"][:, 0:S], b["tmp2"][:, 0:S], ACT.Exp))
        sc.sw(SCE)
        sc.inc(SCE.activation(b["tmp2"][:, 0:S], b["tmp2"][:, 0:S], ACT.Ln, bias=1.0))
        _W(V, sc)
        dv.sw(V)
        dv.inc(V.tensor_scalar_min(b["tmp3"][:, 0:S], b["spl"][:, sl], 0.0))
        dv.sw(V)
        dv.inc(V.tensor_sub(b["tmp2"][:, 0:S], b["tmp3"][:, 0:S], b["tmp2"][:, 0:S]))
        dv.sw(V)
        dv.inc(V.scalar_tensor_tensor(b["tmp2"][:, 0:S], b["tmp2"][:, 0:S], -0.5,
                                      b["mk_fm"][:, sl], ALU.add, ALU.mult))
        _W(SCE, dv)
        sc.inc(SCE.activation(b["tmp3"][:, 0:S], b["tmp2"][:, 0:S], ACT.Exp))
        _W(V, sc)
        dv.sw(V)
        dv.inc(V.tensor_mul(b["k_fm"][:, sl], b["k_fm"][:, sl], b["tb1"][:, 0:S]))
        dv.sw(V)
        dv.inc(V.tensor_mul(b["k_fm"][:, sl], b["k_fm"][:, sl], b["tmp3"][:, 0:S]))
        kflast = dv.n
        dv.sw(V)
        dv.inc(V.tensor_sub(t1v, cur, starts))
        _W(SCE, dv)
        sc.inc(SCE.activation(b["tmp2"][:, 0:S], b["tmp1"][:, 0:S], ACT.Exp, scale=-BETA))
        _W(V, sc)
        dv.sw(V)
        dv.inc(V.tensor_mul(b["qt_fm"][:, sl], b["r_fm"][:, sl], b["tmp2"][:, 0:S]))
        _W(SCE, dv)
        sc.inc(SCE.activation(b["tmp2"][:, 0:S], b["tmp1"][:, 0:S], ACT.Exp, scale=BETA))
        _W(V, sc)
        dv.sw(V)
        dv.inc(V.tensor_mul(b["kt_fm"][:, sl], b["k_fm"][:, sl], b["tmp2"][:, 0:S]))
        dv.inc(V.tensor_mul(b["bt_fm"][:, sl], b["kk_fm"][:, sl], b["tmp2"][:, 0:S]))
        dv.sw(V)
        dv.inc(V.tensor_mul(b["bt_fm"][:, sl], b["bt_fm"][:, sl], b["ag_fm"][:, sl]))
        dv.inc(V.tensor_sub(t1v, ends, cur))
        _W(SCE, dv)
        sc.inc(SCE.activation(b["tmp2"][:, 0:S], b["tmp1"][:, 0:S], ACT.Exp, scale=-BETA))
        _W(V, sc)
        dv.sw(V)
        dv.inc(V.tensor_mul(b["kh_fm"][:, sl], b["k_fm"][:, sl], b["tmp2"][:, 0:S]))
        dv.inc(V.tensor_mul(b["bh_fm"][:, sl], b["kk_fm"][:, sl], b["tmp2"][:, 0:S]))
        dv.sw(V)
        dv.inc(V.tensor_mul(b["bh_fm"][:, sl], b["bh_fm"][:, sl], b["ag_fm"][:, sl]))
        dv.inc(V.tensor_sub(t1v, prv, starts))
        _W(SCE, dv)
        sc.inc(SCE.activation(b["tmp2"][:, 0:S], b["tmp1"][:, 0:S], ACT.Exp, scale=-BETA))
        _W(V, sc)
        dv.sw(V)
        dv.inc(V.tensor_mul(b["at_fm"][:, sl], b["kk_fm"][:, sl], b["tmp2"][:, 0:S]))
    _W(nc.gpsimd, dv, sw0)
    kv = b["k_fm"].rearrange("q (ch t) -> q ch t", t=SUPD)[:, :, 0:S]
    sw0.inc(nc.gpsimd.dma_start(
        out=b["scr_k"].rearrange("q (ch t) -> q ch t", t=MS)[:, :, ts:ts + S], in_=kv), 16)


def _D_chunks(p, c, sup, ts, S, nch, T2):
    nc, b = p.nc, p.b
    pe, dv, sc = b["s_pe"], b["s_dve"], b["s_sce"]
    sw0 = b["s_w0"]
    MS = p.MAXSLICE
    ps_a, ps_b, ps_tp = b["ps_da"], b["ps_db"], b["ps_tp"]
    V, SCE, TEN = nc.vector, nc.scalar, nc.tensor
    msk = b["masks"]
    m_su, m_sl, m_hk, m_tb, m_tk = (msk[:, i * SUPD:(i + 1) * SUPD] for i in range(5))
    W = S

    def pairmm(out_ap_fn, lhs_fn, rhs_fn, start, stop, n=None):
        for hh in range(2):
            r0 = hh * 64
            mm = TEN.matmul(out_ap_fn(hh)[r0:r0 + 64, :], lhs_fn(hh), rhs_fn(hh),
                            start=start, stop=stop,
                            skip_group_check=True, tile_position=(r0, r0))
        pe.inc(mm)

    import os as _os4
    _np_ = int(_os4.environ.get("D_NPAIR", "8"))
    _sect = _os4.environ.get("D_SECT", "z")
    for pp in range(min(DCH, _np_)):
        base = pp * SUPD
        st_sl = slice(pp * 128, (pp + 1) * 128)
        _W(TEN, dv, sc)
        for ci in range(nch):
            for hh in range(2):
                r0 = hh * 64
                ident = b["idt"][r0:r0 + 64, r0:r0 + 64]
                TEN.matmul(ps_a[r0:r0 + 64, ci * 64:(ci + 1) * 64],
                           b["v_fm"][r0:r0 + 64, base + ci * 64:base + (ci + 1) * 64],
                           ident, start=True, stop=True, skip_group_check=True,
                           tile_position=(r0, r0))
                TEN.matmul(ps_a[r0:r0 + 64, 512 + ci * 64:512 + (ci + 1) * 64],
                           b["bh_fm"][r0:r0 + 64, base + ci * 64:base + (ci + 1) * 64],
                           ident, start=True, stop=True, skip_group_check=True,
                           tile_position=(r0, r0))
                mm = TEN.matmul(ps_a[r0:r0 + 64, 256 + ci * 64:256 + (ci + 1) * 64],
                                b["kh_fm"][r0:r0 + 64, base + ci * 64:base + (ci + 1) * 64],
                                ident, start=True, stop=True, skip_group_check=True,
                                tile_position=(r0, r0))
        pe.inc(mm)
        _W(SCE, pe)
        _W(V, pe)
        sc.inc(SCE.copy(b["vtm"][:, 0:W], ps_a[:, 0:W]))
        sc.sw(SCE)
        sc.inc(SCE.copy(b["khtm"][:, 0:W], ps_a[:, 256:256 + W]))
        dv.inc(V.tensor_scalar_mul(b["bhtm"][:, 0:W], ps_a[:, 512:512 + W], -1.0))
        tp_dv, tp_sc = dv.n, sc.n
        _W(TEN, dv, sc)
        only_g = bool(_os4.environ.get("D_ONLYG"))
        for ci in range(nch):
            cs = slice(base + ci * 64, base + (ci + 1) * 64)
            ps_ci = slice(ci * 64, (ci + 1) * 64)
            if only_g:
                pairmm(lambda hh: ps_a[:, 512 + ci * 64:512 + (ci + 1) * 64],
                       lambda hh: b["bt_fm"][hh * 64:hh * 64 + 64, cs],
                       lambda hh: b["at_fm"][hh * 64:hh * 64 + 64, cs],
                       True, True)
                continue
            pairmm(lambda hh: ps_a[:, 0:2048][:, ps_ci],
                   lambda hh: b["at_fm"][hh * 64:hh * 64 + 64, cs],
                   lambda hh: b["bt_fm"][hh * 64:hh * 64 + 64, cs],
                   True, True)
            pairmm(lambda hh: ps_a[:, 512 + ci * 64:512 + (ci + 1) * 64],
                   lambda hh: b["bt_fm"][hh * 64:hh * 64 + 64, cs],
                   lambda hh: b["at_fm"][hh * 64:hh * 64 + 64, cs],
                   True, True)
            pairmm(lambda hh: ps_a[:, 1024 + ci * 64:1024 + (ci + 1) * 64],
                   lambda hh: b["kt_fm"][hh * 64:hh * 64 + 64, cs],
                   lambda hh: b["at_fm"][hh * 64:hh * 64 + 64, cs],
                   True, True)
            pairmm(lambda hh: ps_a[:, 1536 + ci * 64:1536 + (ci + 1) * 64],
                   lambda hh: b["bt_fm"][hh * 64:hh * 64 + 64, cs],
                   lambda hh: b["qt_fm"][hh * 64:hh * 64 + 64, cs],
                   True, True)
            pairmm(lambda hh: ps_b[:, ci * 64:(ci + 1) * 64],
                   lambda hh: b["kt_fm"][hh * 64:hh * 64 + 64, cs],
                   lambda hh: b["qt_fm"][hh * 64:hh * 64 + 64, cs],
                   True, True)
        _W(V, pe)
        dv.sw(V)
        dv.inc(V.tensor_mul(b["GTm"][:, 0:W], ps_a[:, 0:W], m_sl[:, 0:W]))
        dv.inc(V.tensor_mul(b["Gm"][:, 0:W], ps_a[:, 512:512 + W], m_su[:, 0:W]))
        dv.inc(V.tensor_mul(b["Hkm"][:, 0:W], ps_a[:, 1024:1024 + W], m_hk[:, 0:W]))
        dv.inc(V.tensor_mul(b["Abm"][:, 0:W], ps_a[:, 1536:1536 + W], m_tb[:, 0:W]))
        dv.inc(V.tensor_mul(b["Akm"][:, 0:W], ps_b[:, 0:W], m_tk[:, 0:W]))
        gh_dv = dv.n
        if _sect == "b":
            continue
        chain = [("Gm", "GTm", "G2m", "G2Tm"), ("G2m", "G2Tm", "G4m", "G4Tm"),
                 ("G4m", "G4Tm", "G8m", "G8Tm"), ("G8m", "G8Tm", "G16m", "G16Tm"),
                 ("G16m", "G16Tm", "G32m", None)]
        for (srcn, srcTn, dstn, dstTn) in chain:
            _W(TEN, dv, sc)
            for ci in range(nch):
                ps_ci = slice(ci * 64, (ci + 1) * 64)
                pairmm(lambda hh: ps_a[:, 0:512][:, ps_ci],
                       lambda hh: b[srcTn][hh * 64:hh * 64 + 64, ci * 64:(ci + 1) * 64],
                       lambda hh: b[srcn][hh * 64:hh * 64 + 64, ci * 64:(ci + 1) * 64],
                       True, True)
                if dstTn is not None:
                    pairmm(lambda hh: ps_a[:, 512:1024][:, ps_ci],
                           lambda hh: b[srcn][hh * 64:hh * 64 + 64, ci * 64:(ci + 1) * 64],
                           lambda hh: b[srcTn][hh * 64:hh * 64 + 64, ci * 64:(ci + 1) * 64],
                           True, True)
            _W(V, pe)
            _W(SCE, pe)
            dv.sw(V)
            dv.inc(V.tensor_copy(b[dstn][:, 0:W], ps_a[:, 0:W]))
            if dstTn is not None:
                sc.inc(SCE.copy(b[dstTn][:, 0:W], ps_a[:, 512:512 + W]))
        sq_dv, sq_sc = dv.n, sc.n
        if _sect == "c":
            continue
        levels = ["Gm", "G2m", "G4m", "G8m", "G16m", "G32m"]
        for ci in range(nch):
            cs = slice(base + ci * 64, base + (ci + 1) * 64)
            ccols = slice(ci * 64, (ci + 1) * 64)
            _W(TEN, dv, sc)
            pairmm(lambda hh: ps_b[hh * 64:hh * 64 + 64, 0:128] if False else ps_b[:, 0:128],
                   lambda hh: b["at_fm"][hh * 64:hh * 64 + 64, cs],
                   lambda hh: b["stateb"][hh * 64:hh * 64 + 64, st_sl],
                   True, False)
            pairmm(lambda hh: ps_b[:, 0:64],
                   lambda hh: b["Hkm"][hh * 64:hh * 64 + 64, ccols],
                   lambda hh: b["vtm"][hh * 64:hh * 64 + 64, ccols],
                   False, True)
            _W(V, pe)
            dv.sw(V)
            dv.inc(V.tensor_copy(b["rhsb"], ps_b[:, 0:128]))
            cur = "rhsb"
            for lv in range(6):
                dst = "saA" if lv % 2 == 0 else "saB"
                slot = 128 + (lv % 2) * 128
                _W(TEN, dv)
                pairmm(lambda hh: ps_b[:, slot:slot + 128],
                       lambda hh: b["idt"][hh * 64:hh * 64 + 64, hh * 64:hh * 64 + 64],
                       lambda hh: b[cur][hh * 64:hh * 64 + 64, 0:128],
                       True, False)
                pairmm(lambda hh: ps_b[:, slot:slot + 128],
                       lambda hh: b[levels[lv]][hh * 64:hh * 64 + 64, ccols],
                       lambda hh: b[cur][hh * 64:hh * 64 + 64, 0:128],
                       False, True)
                _W(V, pe)
                dv.sw(V)
                dv.inc(V.tensor_copy(b[dst], ps_b[:, slot:slot + 128]))
                cur = dst
            _W(TEN, dv)
            pairmm(lambda hh: ps_a[:, 1536 + ci * 64:1536 + (ci + 1) * 64],
                   lambda hh: b["stateb"][hh * 64:hh * 64 + 64, pp * 128 + 64:pp * 128 + 128],
                   lambda hh: b["qt_fm"][hh * 64:hh * 64 + 64, cs],
                   ci == 0, False)
            pairmm(lambda hh: ps_a[:, 1536 + ci * 64:1536 + (ci + 1) * 64],
                   lambda hh: b[cur][hh * 64:hh * 64 + 64, 64:128],
                   lambda hh: b["Abm"][hh * 64:hh * 64 + 64, ccols],
                   False, True)
            pairmm(lambda hh: ps_b[:, 512 + ci * 64:512 + (ci + 1) * 64],
                   lambda hh: b["stateb"][hh * 64:hh * 64 + 64, pp * 128:pp * 128 + 64],
                   lambda hh: b["qt_fm"][hh * 64:hh * 64 + 64, cs],
                   ci == 0, False)
            pairmm(lambda hh: ps_b[:, 512 + ci * 64:512 + (ci + 1) * 64],
                   lambda hh: b[cur][hh * 64:hh * 64 + 64, 0:64],
                   lambda hh: b["Abm"][hh * 64:hh * 64 + 64, ccols],
                   False, False)
            pairmm(lambda hh: ps_b[:, 512 + ci * 64:512 + (ci + 1) * 64],
                   lambda hh: b["vtm"][hh * 64:hh * 64 + 64, ccols],
                   lambda hh: b["Akm"][hh * 64:hh * 64 + 64, ccols],
                   False, True)
            pairmm(lambda hh: ps_b[:, 128:256],
                   lambda hh: b["bhtm"][hh * 64:hh * 64 + 64, ccols],
                   lambda hh: b[cur][hh * 64:hh * 64 + 64, 0:128],
                   True, False)
            pairmm(lambda hh: ps_b[:, 128:192],
                   lambda hh: b["khtm"][hh * 64:hh * 64 + 64, ccols],
                   lambda hh: b["vtm"][hh * 64:hh * 64 + 64, ccols],
                   False, True)
            _W(V, pe)
            dv.sw(V)
            dv.inc(V.scalar_tensor_tensor(b["state"][:, st_sl], b["state"][:, st_sl],
                                          b["dcs"][:, pp * 8 + ci:pp * 8 + ci + 1],
                                          ps_b[:, 128:256], ALU.mult, ALU.add))
            dv.sw(V)
            dv.inc(V.tensor_copy(b["stateb"][:, st_sl], b["state"][:, st_sl]))
        _W(V, pe)
        _W(V, sw0)
        dv.sw(V)
        dv.inc(V.tensor_copy(b["yl_sb"][:, 0:W], ps_b[:, 512:512 + W]))
        dv.inc(V.tensor_copy(b["zh_sb"][:, 0:W], ps_a[:, 1536:1536 + W]))
        _W(nc.gpsimd, dv)
        sw0.inc(nc.gpsimd.dma_start(
            out=b["scr_y"][:, pp * MS + ts:pp * MS + ts + W], in_=b["yl_sb"][:, 0:W]), 16)
        sw0.inc(nc.gpsimd.dma_start(
            out=b["scr_zh"][:, pp * MS + ts:pp * MS + ts + W], in_=b["zh_sb"][:, 0:W]), 16)


def _tap_D(p, c, dbg, n):
    nc, b = p.nc, p.b
    ld = b["s_ld"]
    _barrier(p, ("sync", "gpsimd"))
    MS = p.MAXSLICE
    ld.inc(nc.sync.dma_start(out=dbg[0:128, 0:1024], in_=b["state"]), 16)
    b["s_gp"].inc(nc.gpsimd.dma_start(out=b["dbg2"][0:128, 0:2 * p.MAXTOT],
                                      in_=b["scr_y"][:, 0:2 * p.MAXTOT]), 16)
    nc.sync.wait_ge(ld.sem, ld.n)
    nc.gpsimd.wait_ge(b["s_gp"].sem, b["s_gp"].n)


def _phase_EF(p, c, astart, pend, dbg, y_out):
    nc, b = p.nc, p.b
    pe, dv, sc = b["s_pe"], b["s_dve"], b["s_sce"]
    ld = b["s_ld"]
    sx = [b["s_x0"], b["s_x1"]]
    sr, sr1, sw0, sw1 = b["s_r0"], b["s_r1"], b["s_w0"], b["s_w1"]
    V, SCE, TEN = nc.vector, nc.scalar, nc.tensor
    start = STARTS[c]
    n = pend - start
    OWOFF = start - astart
    MS = p.MAXSLICE
    n_sup = ceil_div(n, SUPD)

    p.ph_sbuf("woT", [128, DCH * D], BF16)
    p.ph_sbuf("w1r", [128, DCH * 4096], BF16)
    p.ph_sbuf("w2s", [128, 2 * D], BF16)
    p.ph_sbuf("b2r", [1, D], BF16)
    p.ph_sbuf("lnwb", [128, 3 * DCH], F32)
    p.ph_sbuf("b1c", [128, 32], F32)
    p.ph_sbuf("relst", [128, 8 * 128], F32)
    p.ph_sbuf("relb", [128, 8 * 64], BF16)
    p.ph_sbuf("seg", [128, 1024], F32)
    p.ph_sbuf("yf", [128, DCH * SUPD], F32)
    p.ph_sbuf("yb", [128, DCH * SUPD], BF16)
    p.ph_sbuf("zr", [128, DCH * SUPD], BF16)
    p.ph_sbuf("rr", [128, DCH * SUPD], BF16)
    p.ph_sbuf("kr", [128, DCH * SUPD], BF16)
    p.ph_sbuf("vr", [128, DCH * SUPD], BF16)
    p.ph_sbuf("gr", [128, DCH * SUPD], BF16)
    p.ph_sbuf("rk2", [128, SUPD], BF16)
    p.ph_sbuf("gn16", [16, 6 * SUPD], F32)
    p.ph_sbuf("gn16b", [16, SUPD], BF16)
    p.ph_sbuf("mean16b", [16, SUPD], BF16)
    p.ph_sbuf("ygf", [128, DCH * SUPD], BF16)
    p.ph_sbuf("xc2", [128, 2 * D], F32)
    p.ph_sbuf("x3t", [128, 2 * D], F32)
    p.ph_sbuf("tbfE", [128, D], BF16)
    p.ph_sbuf("sq_junk", [128, D], BF16)
    p.ph_sbuf("h3T", [128, DCH * SUPD], BF16)
    p.ph_sbuf("hact", [128, 32 * SUPD], BF16)
    p.ph_sbuf("kb0", [128, SUPD], BF16)
    p.ph_sbuf("kb1", [128, SUPD], BF16)
    p.ph_sbuf("nb1", [128, SUPD], F32)
    p.ph_sbuf("nb2", [128, SUPD], F32)
    p.ph_sbuf("zb", [128, SUPD], F32)
    p.ph_sbuf("pwa", [128, SUPD], F32)
    p.ph_sbuf("pwb", [128, SUPD], F32)
    p.ph_sbuf("ob1", [128, D], F32)
    p.ph_sbuf("ob2", [128, D], F32)
    p.ph_sbuf("x4t", [128, D], F32)
    p.ph_psum("ps_e", [128, 1024], F32)
    p.ph_psum("ps_f", [128, 2048], F32)
    p.ph_psum("ps_tp", [128, 2048], BF16)
    cin = b["cc_in"]
    cout = b["cc_out"]

    _barrier(p)
    for dch in range(DCH):
        ld.inc(nc.sync.dma_start(out=b["woT"][:, dch * D:(dch + 1) * D],
                                 in_=b["WoT"][dch * 128:(dch + 1) * 128, :]), 16)
        ld.inc(nc.sync.dma_start(out=b["w1r"][:, dch * 4096:(dch + 1) * 4096],
                                 in_=b["w1_in"][dch * 128:(dch + 1) * 128, :]), 16)
    ld.inc(nc.sync.dma_start(out=b["b2r"], in_=b["b2_in"]), 16)
    ld.inc(nc.sync.dma_start(out=b["lnwb"], in_=b["lnwb_in"]), 16)
    ld.inc(nc.sync.dma_start(out=b["b1c"], in_=b["b1_in"]), 16)
    w_ld = ld.n

    _W(TEN, dv)

    gp = b["s_gp"]
    nc.gpsimd.wait_ge(gp.sem, gp.n)
    cc = nc.gpsimd.collective_compute(
        "AllGather", ALU.bypass, replica_groups=[list(range(N_CORES))],
        ins=[cin.opt()], outs=[cout.opt()])
    scc = b["s_cc"]
    scc.inc(cc, 1)
    nc.gpsimd.wait_ge(scc.sem, scc.n)
    nc.sync.wait_ge(scc.sem, scc.n)

    if c == 0:
        dv.inc(V.memset(b["relst"], 0.0))
        rel_dv = dv.n
    else:
        for k in range(c):
            par = k % 2
            if k > 0:
                _W(nc.sync, dv)
            sr.inc(nc.sync.dma_start(out=b["seg"], in_=cout[k * 128:(k + 1) * 128, :]), 16)
            _W(V, sr); _W(SCE, sr)
            if k == 0:
                dv.inc(V.tensor_copy(b["relst"], b["seg"].rearrange(
                    "q (pp t) -> q pp t", t=128)[:, :, 0:64].rearrange(
                    "q pp t -> q (pp t)") if False else b["seg"]))
                rel_dv = dv.n
                continue
            _W(V, sr)
            dv.sw(V)
            dv.inc(V.tensor_copy(b["zr"][:, 0:1024], b["seg"]))
            _W(TEN, dv, sc)
            for pp in range(DCH):
                for hh in range(2):
                    r0 = hh * 64
                    mm = TEN.matmul(b["ps_f"][r0:r0 + 64, pp * 64:(pp + 1) * 64],
                                    b["zr"][r0:r0 + 64, pp * 128 + 64:pp * 128 + 128],
                                    b["idt"][r0:r0 + 64, r0:r0 + 64],
                                    start=True, stop=True, skip_group_check=True,
                                    tile_position=(r0, r0))
            pe.inc(mm)
            _W(V, pe)
            dv.sw(V)
            dv.inc(V.tensor_copy(b["relb"], b["ps_f"][:, 0:512]))
            dv.sw(V)
            dv.inc(V.tensor_copy(b["yb"][:, 0:1024], b["relst"]))
            _W(TEN, dv)
            for pp in range(DCH):
                for hh in range(2):
                    r0 = hh * 64
                    TEN.matmul(b["ps_e"][r0:r0 + 64, pp * 128:pp * 128 + 64],
                               b["relb"][r0:r0 + 64, pp * 64:(pp + 1) * 64],
                               b["yb"][r0:r0 + 64, pp * 128:pp * 128 + 64],
                               start=True, stop=False, skip_group_check=True,
                               tile_position=(r0, r0))
                    mm = TEN.matmul(b["ps_e"][r0:r0 + 64, pp * 128:pp * 128 + 64],
                                    b["idt"][r0:r0 + 64, r0:r0 + 64],
                                    b["zr"][r0:r0 + 64, pp * 128:pp * 128 + 64],
                                    start=False, stop=True, skip_group_check=True,
                                    tile_position=(r0, r0))
            pe.inc(mm)
            _W(V, pe)
            dv.sw(V)
            for pp in range(DCH):
                dv.inc(V.tensor_copy(b["relst"][:, pp * 128:pp * 128 + 64],
                                     b["ps_e"][:, pp * 128:pp * 128 + 64]))
            rel_dv = dv.n
    _W(V, dv) if False else None
    dv.sw(V)
    for pp in range(DCH):
        dv.inc(V.tensor_copy(b["relb"][:, pp * 64:(pp + 1) * 64],
                             b["relst"][:, pp * 128:pp * 128 + 64]))
    relb_dv = dv.n
    _W(TEN, dv)
    TEN.wait_ge(ld.sem, w_ld)
    V.wait_ge(ld.sem, w_ld)
    SCE.wait_ge(ld.sem, w_ld)

    prev = dict(sup_dv=0)
    for sup in range(n_sup):
        ts = sup * SUPD
        W = min(SUPD, n - ts)
        ntile = W // 128
        _W(nc.sync, dv)
        _W(nc.sync, pe)
        for (buf, scr, ctr) in (("yf", "scr_y", sr), ("zr", "scr_zh", sr1),
                                ("rr", "scr_r", sr), ("kr", "scr_k", sr1),
                                ("vr", "scr_v1", sr), ("gr", "scr_g", sr1)):
            ctr.inc(nc.sync.dma_start(
                out=b[buf].rearrange("q (ch t) -> q ch t", t=SUPD)[:, :, 0:W],
                in_=b[scr].rearrange("q (ch t) -> q ch t", t=MS)[:, :, ts:ts + W]), 16)
        _W(TEN, sr, sr1)
        _W(V, sr, sr1)
        for pp in range(DCH):
            _W(TEN, dv)
            for hh in range(2):
                r0 = hh * 64
                mm = TEN.matmul(b["ps_e"][r0:r0 + 64, 0:W],
                                b["relb"][r0:r0 + 64, pp * 64:(pp + 1) * 64],
                                b["zr"][r0:r0 + 64, pp * SUPD:pp * SUPD + W],
                                start=True, stop=True, skip_group_check=True,
                                tile_position=(r0, r0))
            pe.inc(mm)
            _W(V, pe)
            dv.sw(V)
            dv.inc(V.scalar_tensor_tensor(b["yf"][:, pp * SUPD:pp * SUPD + W],
                                          b["ps_e"][:, 0:W], 1.0,
                                          b["yf"][:, pp * SUPD:pp * SUPD + W],
                                          ALU.mult, ALU.add))
            dv.sw(V)
            dv.inc(V.tensor_copy(b["yb"][:, pp * SUPD:pp * SUPD + W],
                                 b["yf"][:, pp * SUPD:pp * SUPD + W]))
        ycor_dv = dv.n
        _W(SCE, dv)
        _W(TEN, dv)
        for pp in range(DCH):
            sc.inc(SCE.activation(b["rk2"][:, 0:W], b["yb"][:, pp * SUPD:pp * SUPD + W],
                                  ACT.Square))
            _W(TEN, sc)
            TEN.matmul(b["ps_f"][0:16, 0:W], b["bo16"][:, pp * 16:(pp + 1) * 16],
                       b["yb"][:, pp * SUPD:pp * SUPD + W],
                       start=(pp == 0), stop=(pp == DCH - 1), skip_group_check=True)
            mm = TEN.matmul(b["ps_f"][0:16, 512:512 + W], b["bo16"][:, pp * 16:(pp + 1) * 16],
                            b["rk2"][:, 0:W],
                            start=(pp == 0), stop=(pp == DCH - 1), skip_group_check=True)
            pe.inc(mm)
            _W(SCE, pe) if pp < DCH - 1 else None
        gn_pe = pe.n
        _W(V, pe)
        dv.sw(V)
        dv.inc(V.tensor_scalar_mul(b["gn16"][0:16, 0:W], b["ps_f"][0:16, 0:W], 1.0 / 64.0))
        dv.inc(V.tensor_scalar_mul(b["gn16"][0:16, SUPD:SUPD + W],
                                   b["ps_f"][0:16, 512:512 + W], 1.0 / 64.0))
        dv.sw(V)
        dv.inc(V.tensor_mul(b["gn16"][0:16, 2 * SUPD:2 * SUPD + W],
                            b["gn16"][0:16, 0:W], b["gn16"][0:16, 0:W]))
        dv.sw(V)
        dv.inc(V.scalar_tensor_tensor(b["gn16"][0:16, 3 * SUPD:3 * SUPD + W],
                                      b["gn16"][0:16, SUPD:SUPD + W], EPS_GN,
                                      b["gn16"][0:16, 2 * SUPD:2 * SUPD + W],
                                      ALU.add, ALU.subtract))
        _W(SCE, dv)
        sc.inc(SCE.activation(b["gn16"][0:16, 4 * SUPD:4 * SUPD + W],
                              b["gn16"][0:16, 3 * SUPD:3 * SUPD + W], ACT.Ln))
        sc.sw(SCE)
        sc.inc(SCE.activation(b["gn16b"][0:16, 0:W],
                              b["gn16"][0:16, 4 * SUPD:4 * SUPD + W], ACT.Exp, scale=-0.5))
        _W(V, sc)
        dv.sw(V)
        dv.inc(V.tensor_copy(b["mean16b"][0:16, 0:W], b["gn16"][0:16, 0:W]))
        mean_bf_dv = dv.n
        _W(V, sr, sr1)
        for pp in range(DCH):
            _W(V, pe)
            dv.sw(V)
            dv.inc(V.scalar_tensor_tensor(b["rk2"][:, 0:W], b["rr"][:, pp * SUPD:pp * SUPD + W],
                                          b["lnwb"][:, 2 * DCH + pp:2 * DCH + pp + 1],
                                          b["kr"][:, pp * SUPD:pp * SUPD + W],
                                          ALU.mult, ALU.mult))
            _W(TEN, dv)
            mm = TEN.matmul(b["ps_f"][0:16, 1024:1024 + W], b["bo16"][:, pp * 16:(pp + 1) * 16],
                            b["rk2"][:, 0:W],
                            start=(pp == 0), stop=(pp == DCH - 1), skip_group_check=True)
            pe.inc(mm)
        rk_pe = pe.n
        _W(V, pe)
        dv.sw(V)
        dv.inc(V.tensor_copy(b["rk2"][0:16, 0:W], b["ps_f"][0:16, 1024:1024 + W]))
        rkb_dv = dv.n
        _W(TEN, dv, sc)
        for pp in range(DCH):
            _W(TEN, dv)
            mm = TEN.matmul(b["ps_e"][:, 0:W], b["sel16"][:, pp * 128:(pp + 1) * 128],
                            b["mean16b"][0:16, 0:W], start=True, stop=True,
                            skip_group_check=True)
            mm = TEN.matmul(b["ps_e"][:, 256:256 + W], b["sel16"][:, pp * 128:(pp + 1) * 128],
                            b["gn16b"][0:16, 0:W], start=True, stop=True,
                            skip_group_check=True)
            mm2 = TEN.matmul(b["ps_e"][:, 512:512 + W], b["sel16"][:, pp * 128:(pp + 1) * 128],
                             b["rk2"][0:16, 0:W], start=True, stop=True,
                             skip_group_check=True)
            pe.inc(mm2)
            _W(V, pe)
            dv.sw(V)
            dv.inc(V.tensor_sub(b["yf"][:, pp * SUPD:pp * SUPD + W],
                                b["yf"][:, pp * SUPD:pp * SUPD + W], b["ps_e"][:, 0:W]))
            dv.sw(V)
            dv.inc(V.tensor_mul(b["yf"][:, pp * SUPD:pp * SUPD + W],
                                b["yf"][:, pp * SUPD:pp * SUPD + W], b["ps_e"][:, 256:256 + W]))
            dv.sw(V)
            dv.inc(V.tensor_scalar(b["yf"][:, pp * SUPD:pp * SUPD + W],
                                   b["yf"][:, pp * SUPD:pp * SUPD + W],
                                   b["lnwb"][:, pp:pp + 1], b["lnwb"][:, DCH + pp:DCH + pp + 1],
                                   ALU.mult, ALU.add))
            dv.sw(V)
            dv.inc(V.scalar_tensor_tensor(b["zb"][:, 0:W], b["ps_e"][:, 512:512 + W], 1.0,
                                          b["vr"][:, pp * SUPD:pp * SUPD + W],
                                          ALU.mult, ALU.mult))
            dv.sw(V)
            dv.inc(V.tensor_add(b["yf"][:, pp * SUPD:pp * SUPD + W],
                                b["yf"][:, pp * SUPD:pp * SUPD + W], b["zb"][:, 0:W]))
            dv.sw(V)
            dv.inc(V.tensor_mul(b["ygf"][:, pp * SUPD:pp * SUPD + W],
                                b["yf"][:, pp * SUPD:pp * SUPD + W],
                                b["gr"][:, pp * SUPD:pp * SUPD + W]))
        yg_dv = dv.n
        for ti in range(ntile):
            loc = OWOFF + ts + ti * 128
            par = ti % 2
            sx[par].inc(nc.sync.dma_start(out=b["xc2"][:, par * D:(par + 1) * D],
                                          in_=b["scr_x2"][loc:loc + 128, :]), 16)
            _W(TEN, dv)
            _W(TEN, ld)
            for dch in range(DCH):
                mm = TEN.matmul(b["ps_f"][:, 1024:2048][:, 0:512],
                                b["ygf"][:, dch * SUPD + ti * 128:dch * SUPD + (ti + 1) * 128],
                                b["woT"][:, dch * D:dch * D + 512],
                                start=(dch == 0), stop=False, skip_group_check=True)
                mm = TEN.matmul(b["ps_f"][:, 1024:2048][:, 512:1024],
                                b["ygf"][:, dch * SUPD + ti * 128:dch * SUPD + (ti + 1) * 128],
                                b["woT"][:, dch * D + 512:(dch + 1) * D],
                                start=(dch == 0), stop=(dch == DCH - 1),
                                skip_group_check=True)
            pe.inc(mm)
            _W(V, pe)
            _W(V, sx[par])
            dv.sw(V)
            dv.inc(V.scalar_tensor_tensor(b["x3t"][:, par * D:(par + 1) * D],
                                          b["ps_f"][:, 1024:2048], 1.0,
                                          b["xc2"][:, par * D:(par + 1) * D],
                                          ALU.mult, ALU.add))
            x3_dv = dv.n
            _W(SCE, dv)
            dv.sw(V)
            _ln_tile(p, b["x3t"][:, par * D:(par + 1) * D], b["tbfE"])
            _W(TEN, sc, dv)
            _transpose_tile(p, b["tbfE"],
                            lambda dch, _ti=ti: b["h3T"][:, dch * SUPD + _ti * 128:
                                                         dch * SUPD + (_ti + 1) * 128])
        _ffn_sup(p, c, ts, W, ntile)
    return


def _ffn_sup(p, c, ts, W, ntile):
    nc, b = p.nc, p.b
    pe, dv, sc = b["s_pe"], b["s_dve"], b["s_sce"]
    V, SCE, TEN = nc.vector, nc.scalar, nc.tensor
    sw1 = b["s_w1"]
    sr1 = b["s_r1"]
    kn = _KNOTS
    _W(TEN, dv, sc)
    kb = ["kb0", "kb1"]
    dv_at = {}
    for hch in range(32):
        psl = b["ps_f"][:, (hch % 2) * 512:(hch % 2) * 512 + W]
        pacc = b["ps_f"][:, 1024 + (hch % 2) * 512:1024 + (hch % 2) * 512 + W]
        if hch >= 2:
            _W(TEN, dv)
        for dch in range(DCH):
            mm = TEN.matmul(psl,
                            b["w1r"][:, dch * 4096 + hch * 128:dch * 4096 + (hch + 1) * 128],
                            b["h3T"][:, dch * SUPD:dch * SUPD + W],
                            start=(dch == 0), stop=(dch == DCH - 1),
                            skip_group_check=True)
        pe.inc(mm)
        w1_pe = pe.n
        _W(V, pe)
        dv.sw(V)
        dv.inc(V.tensor_scalar(b["nb1"][:, 0:W], psl, b["b1c"][:, hch:hch + 1], 256.0,
                               ALU.add, ALU.mult))
        dv.sw(V)
        dv.inc(V.tensor_scalar(b["nb2"][:, 0:W], b["nb1"][:, 0:W], MAGIC, MAGIC,
                               ALU.add, ALU.subtract))
        dv.sw(V)
        dv.inc(V.tensor_scalar_mul(b["nb1"][:, 0:W], b["nb2"][:, 0:W], 1.0 / 64.0))
        dv.sw(V)
        dv.inc(V.scalar_tensor_tensor(b["zb"][:, 0:W], b["nb1"][:, 0:W], -1.0,
                                      b["nb1"][:, 0:W], ALU.mult, ALU.min))
        for k in range(17):
            K_k, d_k = kn[k]
            buf = b[kb[k % 2]]
            dv.sw(V)
            if k >= 2:
                nc.vector.wait_ge(pe.sem, dv_at[k - 2])
            dv.inc(V.tensor_scalar(buf[:, 0:W], b["zb"][:, 0:W], float(K_k), float(d_k),
                                   ALU.max, ALU.mult))
            nc.tensor.wait_ge(dv.sem, dv.n)
            mm = TEN.matmul(pacc, b["idt"], buf[:, 0:W], start=(k == 0), stop=(k == 16),
                            skip_group_check=True)
            pe.inc(mm)
            dv_at[k] = pe.n
        csum = -sum(K_k * d_k for K_k, d_k in kn)
        _W(V, pe)
        dv.sw(V)
        dv.inc(V.scalar_tensor_tensor(b["nb1"][:, 0:W], b["nb2"][:, 0:W], 0.0,
                                      pacc, ALU.max, ALU.add))
        dv.sw(V)
        dv.inc(V.tensor_scalar(b["nb1"][:, 0:W], b["nb1"][:, 0:W],
                               float(csum) + MAGIC, MAGIC, ALU.add, ALU.subtract))
        dv.sw(V)
        dv.inc(V.tensor_scalar_mul(b["hact"][:, hch * SUPD:hch * SUPD + W],
                                   b["nb1"][:, 0:W], 1.0 / 256.0))
    act_dv = dv.n
    _W(TEN, dv)
    pe_at = {}
    sw2 = [b["s_w2a"], b["s_w2b"]]
    for hch in range(32):
        wpar = hch % 2
        if hch >= 2:
            nc.sync.wait_ge(pe.sem, pe_at[hch - 2])
        sw2[wpar].inc(nc.sync.dma_start(out=b["w2s"][:, wpar * D:(wpar + 1) * D],
                                        in_=b["w2_in"][hch * 128:(hch + 1) * 128, :]), 16)
        nc.tensor.wait_ge(sw2[wpar].sem, sw2[wpar].n)
        for ti in range(ntile):
            for hq in range(2):
                mm = TEN.matmul(b["ps_f"][:, ti * 1024 + hq * 512:ti * 1024 + (hq + 1) * 512],
                                b["hact"][:, hch * SUPD + ti * 128:hch * SUPD + (ti + 1) * 128],
                                b["w2s"][:, wpar * D + hq * 512:wpar * D + (hq + 1) * 512],
                                start=(hch == 0), stop=(hch == 31), skip_group_check=True)
        pe.inc(mm)
        pe_at[hch] = pe.n
    for ti in range(ntile):
        mm = TEN.matmul(b["ps_f"][:, ti * 1024:ti * 1024 + 512], b["onesr"],
                        b["b2r"][:, 0:512], start=False, stop=True, skip_group_check=True)
        mm = TEN.matmul(b["ps_f"][:, ti * 1024 + 512:ti * 1024 + 1024], b["onesr"],
                        b["b2r"][:, 512:1024], start=False, stop=True, skip_group_check=True)
    pe.inc(mm)
    _W(V, pe)
    for ti in range(ntile):
        par = ti % 2
        dv.sw(V)
        dv.inc(V.tensor_scalar(b["ob1"], b["ps_f"][:, ti * 1024:(ti + 1) * 1024],
                               256.0, MAGIC, ALU.mult, ALU.add))
        dv.sw(V)
        dv.inc(V.tensor_scalar(b["ob2"], b["ob1"], MAGIC, 1.0 / 256.0,
                               ALU.subtract, ALU.mult))
        dv.sw(V)
        if sw1.n:
            nc.vector.wait_ge(sw1.sem, sw1.n)
        dv.inc(V.tensor_add(b["x4t"], b["ob2"], b["x3t"][:, par * D:(par + 1) * D]))
        x4_dv = dv.n
        nc.gpsimd.wait_ge(dv.sem, x4_dv)
        loc = ts + ti * 128
        sw1.inc(nc.gpsimd.dma_start(out=b["y_out_ap"][loc:loc + 128, :], in_=b["x4t"]), 16)

